# version 19
# baseline (speedup 1.0000x reference)
import numpy as np

import concourse.bass as bass
import concourse.mybir as mybir
import concourse.library_config as libcfg
from concourse.bass_isa_rust import ReduceOp
from concourse.bass_utils import run_bass_kernel_spmd

F32 = mybir.dt.float32
BF16 = mybir.dt.bfloat16
F16 = mybir.dt.float16
ALU = mybir.AluOpType
AF = mybir.ActivationFunctionType
AX = mybir.AxisListType

MAGIC = float(np.float32(3 * 2 ** 22))
MAGIC16 = 1536.0


class Cfg:
    def __init__(self, small=False):
        if small:
            self.B, self.T, self.D, self.DI, self.DH, self.Tc = 1, 512, 512, 512, 1024, 256
        else:
            self.B, self.T, self.D, self.DI, self.DH, self.Tc = 2, 2048, 1024, 2048, 4096, 1024
        self.NS = 64
        self.KD = self.D // 128
        self.KI = self.DI // 128
        self.KH = self.DH // 128
        self.NTC = self.T // self.Tc
        self.NFC = self.T // 512
        self.NTT = self.T // 128
        self.TT_C = self.Tc // 128
        self.NDH = 2
        self.DPH = self.KI // self.NDH
        self.GD = min(4, self.DPH)
        self.NDG = self.DPH // self.GD
        self.NG = 4
        self.NNG = self.NS // self.NG
        self.HBS = min(512, self.Tc)
        self.HB = self.Tc // self.HBS
        assert self.GD * self.Tc * 4 <= 8 * 2048


ENGINES = ("pe", "ve", "se", "ge", "sy")


class Sem:
    __slots__ = ("h", "n", "name", "dma", "last_waited", "boundary")

    def __init__(self, h, name, dma=False):
        self.h, self.n, self.name, self.dma = h, 0, name, dma
        self.last_waited = 0
        self.boundary = 0

    def bump(self, k):
        self.n += k
        return self.n


class Pool:
    def __init__(self, t, elems):
        self.t, self.elems, self.off, self.marks = t, elems, 0, []

    def alloc(self, n):
        assert self.off + n <= self.elems, f"pool overflow: {self.off}+{n}>{self.elems}"
        o = self.off
        self.off += n
        return self.t[:, o:o + n]

    def push(self):
        self.marks.append(self.off)

    def pop(self):
        self.off = self.marks.pop()


class Builder:
    def __init__(self, cfg):
        self.c = cfg
        self.nc = bass.Bass("TRN2", target_bir_lowering=False)
        self.ops = {e: [] for e in ENGINES}
        self.lastwait = {e: {} for e in ENGINES}
        self._guards = []
        self.dma_sems = []

    def emit(self, e, fn, wait=(), drain=False, inc=None):
        if drain:
            sem = getattr(self, "engine_sem", {}).get(e)
            if sem is not None:
                wait = list(wait) + [(sem, sem.n)]
                drain = False
        wl = []
        for sem, cnt in wait:
            if cnt is None or sem.dma:
                cnt = sem.n
            if sem.dma:
                sem.last_waited = max(sem.last_waited, cnt)
            if cnt > self.lastwait[e].get(id(sem), -1):
                wl.append((sem, cnt))
                self.lastwait[e][id(sem)] = cnt
        if e == "sy" and wl:
            for s in self.dma_sems:
                s.boundary = s.n
        if inc is not None:
            inc[0].bump(inc[1])

        def op(eng, wl=wl, drain=drain, fn=fn, inc=inc):
            for sem, cnt in wl:
                eng.wait_ge(sem.h, cnt)
            if drain:
                eng.drain()
            r = fn(eng)
            if inc is not None:
                r.then_inc(inc[0].h, inc[1])

        self.ops[e].append(op)

    def barrier(self):
        b = self.bar
        tgt = b.n + len(ENGINES)
        b.n = tgt
        dwaits = [(s.h, s.n) for s in self.dma_sems if s.n > 0]
        for s in self.dma_sems:
            s.last_waited = max(s.last_waited, s.n)
            s.boundary = s.n
        for e in ENGINES:
            def op(eng, tgt=tgt, b=b, dwaits=dwaits):
                for h, n in dwaits:
                    eng.wait_ge(h, n)
                eng.drain()
                eng.sem_inc(b.h, 1)
                eng.wait_ge(b.h, tgt)
            self.ops[e].append(op)
            self.lastwait[e] = {}

    def dma(self, dst, src, wait=(), sem=None, slow=False):
        if sem is None:
            try:
                is_store = "DRAM" in str(dst.space)
            except Exception:
                is_store = False
            sem = self.qst if is_store else self.qio
        wait = list(wait)
        if sem.last_waited > sem.boundary:
            wait.append((sem, sem.last_waited))
            self.lastwait["sy"].pop(id(sem), None)
        self.emit("sy", lambda e, d=dst, s=src, sl=slow: e.dma_start(
            out=d, in_=s, allow_slow_non_contiguous=sl),
                  wait=wait, inc=(sem, 16))
        return sem.n

    def ev(self, fn, **kw):
        self.emit("ve", fn, **kw)

    def es(self, fn, **kw):
        self.emit("se", fn, **kw)

    def ep(self, fn, **kw):
        self.emit("pe", fn, **kw)

    def eg(self, fn, **kw):
        self.emit("ge", fn, **kw)

    def sb(self, name, shape, dtype):
        g = self.nc.sbuf_tensor(name, list(shape), dtype)
        t = g.__enter__()
        self._guards.append(g)
        return t

    def build(self, dbg=False):
        nc, c = self.nc, self.c
        B, T, D, DI, DH, NS = c.B, c.T, c.D, c.DI, c.DH, c.NS
        KD, KI, KH, Tc = c.KD, c.KI, c.KH, c.Tc

        din = lambda n, s: nc.dram_tensor(n, list(s), F32, kind="ExternalInput").ap()
        self.x_h = din("L0_attn_x", (B, T, D))
        self.ipw_h = din("L0_attn_in_proj_w", (D, 2 * DI))
        self.mcw_h = din("L0_attn_conv_w", (DI, 1, 4))
        self.mcb_h = din("L0_attn_conv_b", (DI,))
        self.xpw_h = din("L0_attn_x_proj_w", (DI, 2 * NS + 1))
        self.dtw_h = din("L0_attn_dt_w", (1, DI))
        self.dtb_h = din("L0_attn_dt_b", (DI,))
        self.alog_h = din("L0_attn_A_log", (DI, NS))
        self.dvec_h = din("L0_attn_D", (DI,))
        self.opw_h = din("L0_attn_out_proj_w", (DI, D))
        self.w1_h = din("L0_ffn_w1", (DH, D))
        self.fcw_h = din("L0_ffn_conv_w", (DH, 1, 3))
        self.fcb_h = din("L0_ffn_conv_b", (DH,))
        self.w2_h = din("L0_ffn_w2", (D, DH))
        self.idf_h = din("ident_f32", (128, 128))
        self.idb_h = nc.dram_tensor("ident_bf16", [128, 128], BF16,
                                    kind="ExternalInput").ap()
        self.out_h = nc.dram_tensor("out", [B, T, D], F32, kind="ExternalOutput").ap()

        scr = lambda n, s, d: nc.dram_tensor(n, list(s), d, kind="Internal").ap()
        self.ipw_d = scr("ipw_bf", (2 * KI, 128, KD * 128), BF16)
        self.opw_d = scr("opw_bf", (KD, 128, KI * 128), BF16)
        self.w1t_d = scr("w1qT", (KH, 128, KD * 128), BF16)
        self.w2t_d = scr("w2qT", (KH, 128, KD * 128), BF16)
        self.z_d = scr("z_dram", (B, KI, 128, T), BF16)
        self.xs_d = scr("xs_dram", (B, KI, 128, T), BF16)
        self.bmat_d = scr("bmat_dram", (B, NS, T), BF16)
        self.cmat_d = scr("cmat_dram", (B, NS, T), BF16)
        self.y_d = scr("y_dram", (B, KI, 128, T), BF16)
        self.x1_d = scr("x1_dram", (B, T, D), F32)
        self.hs_d = scr("hs_dram", (B, KH, 128, T), BF16)
        self.h2q_d = scr("h2q_dram", (B, KH, 128, T), BF16)
        self.itok_d = scr("itok_dram", (128, c.NTT), F32)
        self.gtok_d = scr("gtok_dram", (128, c.NTT), F32)
        self.row_d = scr("row_dram", (2, T), F32)
        self.rowb_d = scr("rowb_dram", (4, T), BF16)
        self.scl_d = scr("scl_dram", (2, 2), F32)

        self.idf = self.sb("idf", (128, 128), F32)
        self.idb = self.sb("idb", (128, 128), BF16)
        self.mcw = self.sb("mcw", (128, KI, 4), F32)
        self.mcb = self.sb("mcb", (128, KI), F32)
        self.dtw = self.sb("dtw", (128, KI), F32)
        self.dtb = self.sb("dtb", (128, KI), F32)
        self.dvec = self.sb("dvec", (128, KI), F32)
        self.aneg = self.sb("aneg", (128, KI, NS), F32)
        self.fcw = self.sb("fcw", (128, KH, 3), F32)
        self.fcb = self.sb("fcb", (128, KH), F32)
        self.xpw = self.sb("xpw", (128, KI, 144), BF16)
        self.sw1 = self.sb("sw1", (128, 1), F32)
        self.iw1 = self.sb("iw1", (128, 1), F32)
        self.sw2 = self.sb("sw2", (128, 1), F32)
        self.iw2 = self.sb("iw2", (128, 1), F32)
        self.rowbuf = self.sb("rowbuf", (128, T), F32)
        self.gm_row = self.rowbuf[0:1, 0:T]
        self.dtr_row = self.rowbuf[32:33, 0:T]
        self.is_row = self.rowbuf[64:65, 0:T]
        self.rowb16 = self.sb("rowb16", (128, 2 * T), BF16)
        self.sc2_rowb = self.rowb16[0:1, 0:T]
        self.iv2_rowb = self.rowb16[0:1, T:2 * T]
        self.is_rowb = self.rowb16[64:65, 0:T]
        self.smv = self.sb("smv", (128, 16), F32)
        self.itok = self.sb("itok", (128, c.NTT), F32)

        self.p32 = Pool(self.sb("p32", (128, 16640), F32), 16640)
        self.p16 = Pool(self.sb("p16", (128, 49408), BF16), 49408)
        self.pf16 = Pool(self.sb("pf16", (128, 8192), F16), 8192)
        pg = self.nc.psum_tensor("psum", [128, 4096], F32)
        self.psum = pg.__enter__()
        self._guards.append(pg)
        self.psum_b = self.psum[:, :].bitcast(BF16)

        self.sems = {}
        for name in ("bar", "qio", "qst", "qout", "eV", "eS", "eT", "eG"):
            g = nc.semaphore(name)
            h = g.__enter__()
            self._guards.append(g)
            self.sems[name] = Sem(h, name, dma=name.startswith("q"))
        self.bar = self.sems["bar"]
        self.qio = self.sems["qio"]
        self.qst = self.sems["qst"]
        self.qout = self.sems["qout"]
        self.dma_sems = [self.sems[n] for n in ("qio", "qst", "qout")]
        self.eV = self.sems["eV"]
        self.eS = self.sems["eS"]
        self.eT = self.sems["eT"]
        self.eG = self.sems["eG"]
        self.engine_sem = {"ve": self.eV, "se": self.eS, "pe": self.eT,
                           "ge": self.eG}

        self.phase0()
        self.barrier()
        for b in range(B):
            self.phase1(b)
            self.barrier()
            self.scan(b)
            self.barrier()
            self.phase34(b)
            self.barrier()
        if dbg:
            for name, t in (("d_xs", self.xs_d), ("d_z", self.z_d),
                            ("d_bm", self.bmat_d), ("d_cm", self.cmat_d),
                            ("d_y", self.y_d), ("d_x1", self.x1_d),
                            ("d_hs", self.hs_d), ("d_h2q", self.h2q_d)):
                o = nc.dram_tensor(name, list(t.shape), t.dtype,
                                   kind="ExternalOutput").ap()
                self.dma(o, t, sem=self.qout)
        self.emit("sy", lambda e: e.nop(), wait=[(self.qout, None)])

        with nc.Block() as block:
            block.sync(lambda e: [op(e) for op in self.ops["sy"]])
            block.vector(lambda e: [op(e) for op in self.ops["ve"]])
            block.scalar(lambda e: [op(e) for op in self.ops["se"]])
            block.tensor(lambda e: [op(e) for op in self.ops["pe"]])
            block.gpsimd(lambda e: [op(e) for op in self.ops["ge"]])

        for g in reversed(self._guards):
            g.__exit__(None, None, None)
        return nc

    def phase0(self):
        c, KD, KI, KH = self.c, self.c.KD, self.c.KI, self.c.KH
        D, DH, NS = self.c.D, self.c.DH, self.c.NS
        qio, qst = self.qio, self.qst
        eV, eS, eT, eG = self.eV, self.eS, self.eT, self.eG
        dma, ev, es, ep, eg = self.dma, self.ev, self.es, self.ep, self.eg

        dma(self.idf[:], self.idf_h)
        dma(self.idb[:], self.idb_h)
        for kc in range(KI):
            dma(self.mcw[:, kc, :], self.mcw_h[kc * 128:(kc + 1) * 128, 0, :])
            dma(self.aneg[:, kc, :], self.alog_h[kc * 128:(kc + 1) * 128, :])
        dma(self.mcb[:], self.mcb_h.rearrange("(c p) -> p c", p=128), slow=True)
        dma(self.dtw[:], self.dtw_h[0, :].rearrange("(c p) -> p c", p=128), slow=True)
        dma(self.dtb[:], self.dtb_h.rearrange("(c p) -> p c", p=128), slow=True)
        dma(self.dvec[:], self.dvec_h.rearrange("(c p) -> p c", p=128), slow=True)
        for kc in range(KH):
            dma(self.fcw[:, kc, :], self.fcw_h[kc * 128:(kc + 1) * 128, 0, :])
        dma(self.fcb[:], self.fcb_h.rearrange("(c p) -> p c", p=128), slow=True)
        ld0 = qio.n

        es(lambda e: e.activation(self.aneg[:], self.aneg[:], AF.Exp),
           wait=[(qio, ld0)], inc=(eS, 1))
        ev(lambda e: e.tensor_scalar(self.aneg[:], self.aneg[:], -1.0, None, ALU.mult),
           wait=[(eS, None)], inc=(eV, 1))

        self.p32.push()
        stg = self.p32.alloc(2 * NS + 1)
        for kc in range(KI):
            t = dma(stg, self.xpw_h[kc * 128:(kc + 1) * 128, :], wait=[(eV, None)])
            ev(lambda e, kc=kc: e.tensor_copy(self.xpw[:, kc, 0:2 * NS + 1], stg),
               wait=[(qio, t)], drain=True, inc=(eV, 1))
        self.p32.pop()

        DI2 = 2 * self.c.DI
        self.p32.push()
        self.p16.push()
        wstg = [self.p32.alloc(max(DI2, DH)) for _ in range(2)]
        wbig = self.p16.alloc(KD * DI2).rearrange("p (k m) -> p k m", k=KD)
        mstg = [self.p16.alloc(KI * 128) for _ in range(2)]
        cast_ev = [eV.n, eV.n]
        for kc in range(KD):
            pr = kc % 2
            t = dma(wstg[pr][:, :DI2], self.ipw_h[kc * 128:(kc + 1) * 128, :],
                    wait=[(eV, cast_ev[pr])])
            ev(lambda e, pr=pr, kc=kc: e.tensor_copy(wbig[:, kc, :],
                                                     wstg[pr][:, :DI2]),
               wait=[(qio, t)], inc=(eV, 1))
            cast_ev[pr] = eV.n
        for m in range(2 * KI):
            pr = m % 2
            ev(lambda e, pr=pr, m=m: e.tensor_copy(
                mstg[pr][:, :KD * 128],
                wbig[:, :, m * 128:(m + 1) * 128]),
               wait=[(eV, None), (qst, None)], inc=(eV, 1))
            dma(self.ipw_d[m], mstg[pr][:, :KD * 128], wait=[(eV, None)])
        cast_ev = [eV.n, eV.n]
        for kc in range(KI):
            pr = kc % 2
            t = dma(wstg[pr][:, :D], self.opw_h[kc * 128:(kc + 1) * 128, :],
                    wait=[(eV, cast_ev[pr])])
            ev(lambda e, pr=pr, kc=kc: e.tensor_copy(
                wbig.rearrange("p k m -> p (k m)")[:, kc * D:(kc + 1) * D],
                wstg[pr][:, :D]),
               wait=[(qio, t), (qst, None)], inc=(eV, 1))
            cast_ev[pr] = eV.n
        obig = wbig.rearrange("p k m -> p (k m)")[:, :KI * D].rearrange(
            "p (k m) -> p k m", k=KI)
        for m in range(KD):
            pr = m % 2
            ev(lambda e, pr=pr, m=m: e.tensor_copy(
                mstg[pr], obig[:, :, m * 128:(m + 1) * 128]),
               wait=[(eV, None), (qst, None)], inc=(eV, 1))
            dma(self.opw_d[m], mstg[pr], wait=[(eV, None)])

        acc = self.p32.alloc(4)
        ev(lambda e: e.memset(acc[:, 0:2], 0.0), inc=(eV, 1))
        for src_, col, rows, cols in ((self.w1_h, 0, KH, D), (self.w2_h, 1, KD, DH)):
            for r in range(rows):
                pr = r % 2
                t = dma(wstg[pr][:, :cols], src_[r * 128:(r + 1) * 128, :],
                        wait=[(eV, None)])
                ev(lambda e, pr=pr, col=col, cols=cols: e.reduce_sum(
                    acc[:, 2 + col:3 + col], wstg[pr][:, :cols], axis=AX.X,
                    apply_absolute_value=True),
                   wait=[(qio, t)], drain=True, inc=(eV, 1))
                ev(lambda e, col=col: e.tensor_add(
                    acc[:, col:col + 1], acc[:, col:col + 1], acc[:, 2 + col:3 + col]),
                   drain=True, inc=(eV, 1))
        ep(lambda e: e.transpose(self.psum[0:2, 0:128], acc[:, 0:2], self.idf[:]),
           wait=[(eV, None), (eS, None)], inc=(eT, 1))
        ev(lambda e: e.reduce_sum(acc[0:2, 2:3], self.psum[0:2, 0:128], axis=AX.X),
           wait=[(eT, None)], drain=True, inc=(eV, 1))
        scale = 1.0 / (DH * D)
        ev(lambda e: e.tensor_scalar(acc[0:2, 2:3], acc[0:2, 2:3], scale, 1e-5,
                                     ALU.mult, ALU.max), drain=True, inc=(eV, 1))
        ev(lambda e: e.tensor_scalar(acc[0:2, 3:4], acc[0:2, 2:3], 1.0 / 127.0, None,
                                     ALU.mult), drain=True, inc=(eV, 1))
        ev(lambda e: e.reciprocal(acc[0:2, 0:1], acc[0:2, 2:3]), drain=True,
           inc=(eV, 1))
        dma(self.scl_d[0:1, :], acc[0:2, 0:1], wait=[(eV, None)], slow=True)
        dma(self.scl_d[1:2, :], acc[0:2, 3:4], wait=[(eV, None)], slow=True)
        dma(self.sw1[:], self.scl_d[0:1, 0:1].broadcast_to([128, 1]), slow=True,
            wait=[(qst, None)])
        dma(self.sw2[:], self.scl_d[0:1, 1:2].broadcast_to([128, 1]), slow=True)
        dma(self.iw1[:], self.scl_d[1:2, 0:1].broadcast_to([128, 1]), slow=True)
        dma(self.iw2[:], self.scl_d[1:2, 1:2].broadcast_to([128, 1]), slow=True)

        self.pf16.push()
        f16w = self.pf16.alloc(max(D, DH))
        qb = [self.p16.alloc(max(D, DH)) for _ in range(2)]
        rslab = [self.p16.alloc(KD * 128) for _ in range(2)]
        w2big = wbig.rearrange("p k m -> p (k m)")[:, :KH * KD * 128].rearrange(
            "p (h m q) -> p h m q", h=KH, m=KD)
        for src_, rows, cols, scol, w2 in ((self.w1_h, KH, D, self.sw1, False),
                                           (self.w2_h, KD, DH, self.sw2, True)):
            for r in range(rows):
                pr = r % 2
                t = dma(wstg[pr][:, :cols], src_[r * 128:(r + 1) * 128, :],
                        wait=[(eV, None)])
                ev(lambda e, pr=pr, cols=cols, scol=scol: e.tensor_scalar(
                    wstg[pr][:, :cols], wstg[pr][:, :cols], scol[:, 0:1], None,
                    ALU.mult), wait=[(qio, t), (eV, None), (eT, None)], drain=True,
                   inc=(eV, 1))
                ev(lambda e, pr=pr, cols=cols: e.tensor_scalar(
                    wstg[pr][:, :cols], wstg[pr][:, :cols], 1.49, -1.49,
                    ALU.min, ALU.max), drain=True, inc=(eV, 1))
                ev(lambda e, pr=pr, cols=cols: e.tensor_scalar(
                    f16w[:, :cols], wstg[pr][:, :cols], MAGIC16, None,
                    ALU.add), drain=True, inc=(eV, 1))
                ev(lambda e, pr=pr, cols=cols: e.tensor_scalar(
                    qb[pr][:, :cols], f16w[:, :cols], MAGIC16, None,
                    ALU.subtract), drain=True, inc=(eV, 1))
                nblk = cols // 128
                for g0 in range(0, nblk, 4):
                    gn = min(4, nblk - g0)
                    for j in range(gn):
                        kc = g0 + j
                        self.ep(lambda e, pr=pr, kc=kc, j=j: e.transpose(
                            self.psum_b[:, j * 128:(j + 1) * 128],
                            qb[pr][:, kc * 128:(kc + 1) * 128], self.idb[:]),
                            wait=[(eV, None), (eS, None)],
                            inc=(eT, 1) if j == gn - 1 else None)
                    if w2:
                        es(lambda e, r=r, g0=g0, gn=gn: e.copy(
                            w2big[:, g0:g0 + gn, r, :],
                            self.psum_b[:, :gn * 128].rearrange(
                                "p (a q) -> p a q", a=gn)),
                           wait=[(eT, None)], inc=(eS, 1))
                    else:
                        es(lambda e, pr=pr, g0=g0, gn=gn: e.copy(
                            rslab[pr][:, g0 * 128:(g0 + gn) * 128],
                            self.psum_b[:, :gn * 128]),
                           wait=[(eT, None), (qst, None)], inc=(eS, 1))
                if not w2:
                    dma(self.w1t_d[r], rslab[pr], wait=[(eS, None)])
        for kh in range(KH):
            dma(self.w2t_d[kh],
                w2big.rearrange("p h m q -> p (h m q)")
                [:, kh * KD * 128:(kh + 1) * KD * 128],
                wait=[(eS, None)])
        self.p32.pop()
        self.p16.pop()
        self.pf16.pop()
        self.w_prep_tgt = self.qio.n

    def phase1(self, b):
        c = self.c
        KD, KI, T, D, NS, NFC, NTT = c.KD, c.KI, c.T, c.D, c.NS, c.NFC, c.NTT
        qio, qst = self.qio, self.qst
        eV, eS, eT, eG = self.eV, self.eS, self.eT, self.eG
        dma, ev, es, ep, eg = self.dma, self.ev, self.es, self.ep, self.eg
        ps = self.psum

        self.p32.push()
        self.p16.push()
        xT = self.p16.alloc(KD * T).rearrange("p (k t) -> p k t", k=KD)
        xtok = [self.p32.alloc(D) for _ in range(2)]
        self.warm()
        tr_ev = [0, 0]
        ev_ev = [0, 0]
        for tt in range(NTT):
            pr = tt % 2
            t = dma(xtok[pr], self.x_h[b, tt * 128:(tt + 1) * 128, :],
                    wait=[(eT, tr_ev[pr])])
            for g in range(KD // 4):
                for j in range(4):
                    kc = g * 4 + j
                    ep(lambda e, pr=pr, kc=kc, j=j: e.transpose(
                        ps[:, pr * 1024 + (kc % 4) * 128 + (kc // 4) * 512:
                           pr * 1024 + (kc % 4) * 128 + (kc // 4) * 512 + 128],
                        xtok[pr][:, kc * 128:(kc + 1) * 128], self.idf[:]),
                        wait=[(qio, t), (eS, ev_ev[pr])],
                        inc=(eT, 1) if (g == KD // 4 - 1 and j == 3) else None)
            tr_ev[pr] = eT.n
            for g in range(KD // 4):
                es(lambda e, pr=pr, g=g, tt=tt: e.copy(
                    xT[:, g * 4:(g + 1) * 4, tt * 128:(tt + 1) * 128],
                    ps[:, pr * 1024 + g * 512:pr * 1024 + (g + 1) * 512]
                    .rearrange("p (k t) -> p k t", k=4)),
                   wait=[(eT, None)], inc=(eS, 1))
            ev_ev[pr] = eS.n
        xT_done = eS.n

        xs_pad = [self.p16.alloc(T + 3) for _ in range(2)]
        xss = [self.p16.alloc(T) for _ in range(2)]
        xc = [self.p32.alloc(T) for _ in range(2)]
        bcacc = self.p32.alloc(T)
        wt = [self.p16.alloc(KD * 128) for _ in range(2)]
        zst = [self.p16.alloc(512) for _ in range(2)]
        wt_use = [0, 0]
        ip_ev = [0, 0]
        xss_use = [0, 0]
        silu_done = [0, 0]
        conv_done = [0, 0]
        xs_spill = [0, 0]
        zspill = [0, 0]
        dt_ev = [0, 0]
        bc_ev = [0, 0]
        ev(lambda e: e.memset(self.dtr_row[:], 0.0), inc=(eV, 1))

        def xp_block(dch):
            pr = dch % 2
            for fc in range(NFC):
                bpr = (dch * NFC + fc) % 2
                ep(lambda e, pr=pr, dch=dch, fc=fc, bpr=bpr: e.matmul(
                    ps[:, 2048 + bpr * 512:2048 + (bpr + 1) * 512],
                    self.xpw[:, dch, 0:128], xss[pr][:, fc * 512:(fc + 1) * 512],
                    start=True, stop=True),
                    wait=[(eS, silu_done[pr]), (eV, bc_ev[bpr])], inc=(eT, 1))
                if dch == 0:
                    ev(lambda e, fc=fc, bpr=bpr: e.tensor_copy(
                        bcacc[:, fc * 512:(fc + 1) * 512],
                        ps[:, 2048 + bpr * 512:2048 + (bpr + 1) * 512]),
                       wait=[(eT, None)], drain=True, inc=(eV, 1))
                else:
                    ev(lambda e, fc=fc, bpr=bpr: e.scalar_tensor_tensor(
                        bcacc[:, fc * 512:(fc + 1) * 512],
                        ps[:, 2048 + bpr * 512:2048 + (bpr + 1) * 512], 1.0,
                        bcacc[:, fc * 512:(fc + 1) * 512], ALU.mult, ALU.add),
                       wait=[(eT, None)], drain=True, inc=(eV, 1))
                bc_ev[bpr] = eV.n
                dpr = (dch * NFC + fc) % 2
                ep(lambda e, pr=pr, dch=dch, fc=fc, dpr=dpr: e.matmul(
                    ps[0:1, 1024 + dpr * 512:1024 + dpr * 512 + 512],
                    self.xpw[:, dch, 128:129], xss[pr][:, fc * 512:(fc + 1) * 512],
                    start=True, stop=True),
                    wait=[(eV, dt_ev[dpr])], inc=(eT, 1))
                ev(lambda e, fc=fc, dpr=dpr: e.scalar_tensor_tensor(
                    self.dtr_row[0:1, fc * 512:(fc + 1) * 512],
                    ps[0:1, 1024 + dpr * 512:1024 + dpr * 512 + 512], 1.0,
                    self.dtr_row[0:1, fc * 512:(fc + 1) * 512], ALU.mult, ALU.add),
                   wait=[(eT, None)], drain=True, inc=(eV, 1))
                dt_ev[dpr] = eV.n
            xss_use[pr] = eT.n

        for dch in range(KI):
            pr = dch % 2
            tW = dma(wt[pr], self.ipw_d[dch],
                     wait=[(eT, wt_use[pr]), (qst, None)])
            ev(lambda e, pr=pr: e.memset(xs_pad[pr][:, 0:3], 0.0), inc=(eV, 1))
            pad_ev = eV.n
            for fc in range(NFC):
                fpr = fc % 2
                for kc in range(KD):
                    ep(lambda e, pr=pr, fpr=fpr, kc=kc, fc=fc: e.matmul(
                        ps[:, fpr * 512:(fpr + 1) * 512],
                        wt[pr][:, kc * 128:(kc + 1) * 128],
                        xT[:, kc, fc * 512:(fc + 1) * 512],
                        start=(kc == 0), stop=(kc == KD - 1)),
                        wait=[(qio, tW), (eS, ip_ev[fpr]), (eS, xT_done)],
                        inc=(eT, 1) if kc == KD - 1 else None)
                es(lambda e, pr=pr, fpr=fpr, fc=fc: e.copy(
                    xs_pad[pr][:, 3 + fc * 512:3 + (fc + 1) * 512],
                    ps[:, fpr * 512:(fpr + 1) * 512]),
                   wait=[(eT, None), (eV, conv_done[pr])], inc=(eS, 1))
                ip_ev[fpr] = eS.n
            wt_use[pr] = eT.n
            ev(lambda e, pr=pr, dch=dch: e.tensor_scalar(
                xc[pr], xs_pad[pr][:, 0:T], self.mcw[:, dch, 0:1], None, ALU.mult),
               wait=[(eS, None), (eV, pad_ev)], inc=(eV, 1))
            for k in range(1, 4):
                ev(lambda e, pr=pr, k=k, dch=dch: e.scalar_tensor_tensor(
                    xc[pr], xs_pad[pr][:, k:k + T], self.mcw[:, dch, k:k + 1],
                    xc[pr], ALU.mult, ALU.add), drain=True, inc=(eV, 1))
            es(lambda e, pr=pr, dch=dch: e.activation(
                xss[pr], xc[pr], AF.Silu, bias=self.mcb[:, dch:dch + 1], scale=1.0),
               wait=[(eV, None), (eT, xss_use[pr]), (qst, None)],
               inc=(eS, 1))
            silu_done[pr] = eS.n
            xs_spill[pr] = dma(self.xs_d[b, dch], xss[pr], wait=[(eS, None)])
            if dch > 0:
                xp_block(dch - 1)
        xp_block(KI - 1)
        bst = self.p16.alloc(512)
        cst = self.p16.alloc(512)
        for fc in range(NFC):
            es(lambda e, fc=fc: e.copy(bst[0:64, :],
                                       bcacc[0:64, fc * 512:(fc + 1) * 512]),
               wait=[(eV, None), (qst, None)], inc=(eS, 1))
            es(lambda e, fc=fc: e.copy(cst[64:128, :],
                                       bcacc[64:128, fc * 512:(fc + 1) * 512]),
               wait=[(qst, None)], inc=(eS, 1))
            t1 = dma(self.bmat_d[b, :, fc * 512:(fc + 1) * 512], bst[0:64, :],
                     wait=[(eS, None)])
            t2 = dma(self.cmat_d[b, :, fc * 512:(fc + 1) * 512], cst[64:128, :])
        zp_ev = [0, 0]
        for mz in range(KI):
            pr = mz % 2
            tW = dma(wt[pr], self.ipw_d[KI + mz], wait=[(eT, wt_use[pr])])
            for fc in range(NFC):
                fpr = fc % 2
                for kc in range(KD):
                    ep(lambda e, pr=pr, fpr=fpr, kc=kc, fc=fc: e.matmul(
                        ps[:, fpr * 512:(fpr + 1) * 512],
                        wt[pr][:, kc * 128:(kc + 1) * 128],
                        xT[:, kc, fc * 512:(fc + 1) * 512],
                        start=(kc == 0), stop=(kc == KD - 1)),
                        wait=[(qio, tW), (eS, zp_ev[fpr]), (qio, None)],
                        inc=(eT, 1) if kc == KD - 1 else None)
                es(lambda e, fpr=fpr: e.copy(zst[fpr], ps[:, fpr * 512:(fpr + 1) * 512]),
                   wait=[(eT, None), (qst, None)], inc=(eS, 1))
                zspill[fpr] = dma(self.z_d[b, mz, :, fc * 512:(fc + 1) * 512],
                                  zst[fpr], wait=[(eS, None)])
                zp_ev[fpr] = eS.n
            wt_use[pr] = eT.n
        self.bc_tgt = self.qio.n
        self.p32.pop()
        self.p16.pop()

    def scan(self, b):
        c = self.c
        KI, T, NS, Tc = c.KI, c.T, c.NS, c.Tc
        NDH, DPH, GD, NDG, NG, NNG, HB = c.NDH, c.DPH, c.GD, c.NDG, c.NG, c.NNG, c.HB
        qio, qst = self.qio, self.qst
        eV, eS, eT, eG = self.eV, self.eS, self.eT, self.eG
        dma, ev, es, ep, eg = self.dma, self.ev, self.es, self.ep, self.eg
        ps = self.psum

        self.p32.push()
        self.p16.push()
        self.pf16.push()
        dtrep = self.p32.alloc(T)
        dtt = self.p32.alloc(GD * Tc).rearrange("p (d t) -> p d t", d=GD)
        spt = self.p32.alloc(Tc)
        yfull = self.p32.alloc(Tc)
        gz = self.p32.alloc(Tc)
        ST = self.p32.alloc(KI * NS)
        dtx = self.p16.alloc(GD * Tc).rearrange("p (d t) -> p d t", d=GD)
        xssl = self.p16.alloc(GD * Tc).rearrange("p (d t) -> p d t", d=GD)
        brep = self.p16.alloc(2 * NG * Tc).rearrange("p (r n t) -> p r n t", r=2, n=NG)
        crep = self.p16.alloc(2 * NG * Tc).rearrange("p (r n t) -> p r n t", r=2, n=NG)
        ubuf = self.p16.alloc(GD * Tc).rearrange("p (d t) -> p d t", d=GD)
        sbig = self.p16.alloc(GD * Tc).rearrange("p (d t) -> p d t", d=GD)
        scb = self.p16.alloc(2 * GD * Tc).rearrange("p (r d t) -> p r d t", r=2, d=GD)
        zsl = [self.p16.alloc(Tc) for _ in range(2)]
        ysl = [self.p16.alloc(Tc) for _ in range(2)]

        dma(self.row_d[0:1, :], self.dtr_row, wait=[(eV, None)])
        dma(dtrep, self.row_d[0:1, :].broadcast_to([128, T]), wait=[(qst, None)])
        dt_bcast = None

        bc_use = [0, 0]
        bc_useG = [0, 0]
        dab_use = [0, 0]
        scb_use = [0, 0]
        it = 0
        y_ev = 0
        ysl_dma = [0, 0]
        for tc in range(c.NTC):
            tcs = tc * Tc
            for dh in range(NDH):
                dch0 = dh * DPH
                for dg in range(NDG):
                    xs_t = []
                    for i in range(GD):
                        tx = dma(xssl[:, i, :],
                                 self.xs_d[b, dch0 + dg * GD + i, :, tcs:tcs + Tc],
                                 wait=[(eV, None), (qst, None)])
                        xs_t.append(tx)
                    for i in range(GD):
                        dch = dch0 + dg * GD + i
                        es(lambda e, i=i, dch=dch, tcs=tcs: e.activation(
                            spt, dtrep[:, tcs:tcs + Tc], AF.Abs,
                            scale=self.dtw[:, dch:dch + 1],
                            bias=self.dtb[:, dch:dch + 1]),
                           wait=[(qio, None), (eV, None)], drain=True, inc=(eS, 1))
                        es(lambda e: e.activation(spt, spt, AF.Exp, scale=-1.0),
                           drain=True, inc=(eS, 1))
                        es(lambda e: e.activation(spt, spt, AF.Ln, bias=1.0),
                           drain=True, inc=(eS, 1))
                        es(lambda e, i=i, dch=dch, tcs=tcs: e.activation(
                            dtt[:, i, :], dtrep[:, tcs:tcs + Tc], AF.Relu,
                            scale=self.dtw[:, dch:dch + 1],
                            bias=self.dtb[:, dch:dch + 1]),
                           drain=True, inc=(eS, 1))
                        ev(lambda e, i=i: e.tensor_add(dtt[:, i, :], dtt[:, i, :],
                                                       spt),
                           wait=[(eS, None)], drain=True, inc=(eV, 1))
                        eg(lambda e, i=i: e.tensor_mul(dtx[:, i, :], dtt[:, i, :],
                                                       xssl[:, i, :]),
                           wait=[(qio, xs_t[i]), (eV, None)], drain=True, inc=(eG, 1))
                    dt_built = eV.n
                    jds = list(range(GD))
                    for ng in range(NNG):
                        gpr = ng % 2
                        n0 = ng * NG
                        tB = dma(brep[:, gpr],
                                 self.bmat_d[b, n0:n0 + NG, tcs:tcs + Tc]
                                 .unsqueeze(0).broadcast_to([128, NG, Tc]),
                                 wait=[(eV, bc_use[gpr]), (eG, bc_useG[gpr]),
                                       (qst, None)])
                        tC = dma(crep[:, gpr],
                                 self.cmat_d[b, n0:n0 + NG, tcs:tcs + Tc]
                                 .unsqueeze(0).broadcast_to([128, NG, Tc]))
                        for ni in range(NG):
                            n = n0 + ni
                            pr = it % 2
                            for j in jds:
                                dch = dch0 + dg * GD + j
                                es(lambda e, pr=pr, j=j, dch=dch, n=n: e.activation(
                                    self.dab(pr, j), dtt[:, j, :], AF.Exp,
                                    scale=self.aneg[:, dch, n:n + 1]),
                                   wait=[(eV, dab_use[pr]), (eV, dt_built)],
                                   inc=(eS, 1) if j == jds[-1] else None)
                            ev(lambda e, gpr=gpr, ni=ni: e.tensor_mul(
                                ubuf[:, :, :],
                                dtx[:, :, :],
                                brep[:, gpr, ni, :].unsqueeze(1)
                                .broadcast_to([128, GD, Tc])),
                               wait=[(qio, tB), (eS, None), (eG, None)],
                               inc=(eV, 1))
                            for j in jds:
                                dch = dch0 + dg * GD + j
                                jj = j
                                if tc == 0:
                                    init = 0.0
                                else:
                                    init = ST[:, dch * NS + n:dch * NS + n + 1]
                                ev(lambda e, jj=jj, pr=pr, init=init: e.tensor_tensor_scan(
                                    sbig[:, jj, :], self.dab(pr, jj), ubuf[:, jj, :],
                                    init, ALU.mult, ALU.add),
                                   wait=[(eG, None)], drain=(jj == 0),
                                   inc=(eV, 1) if jj == GD - 1 else None)
                            dab_use[pr] = eV.n
                            ev(lambda e, pr=pr, gpr=gpr, ni=ni: e.tensor_mul(
                                scb[:, pr, :, :],
                                sbig[:, :, :],
                                crep[:, gpr, ni, :].unsqueeze(1)
                                .broadcast_to([128, GD, Tc])),
                               wait=[(qio, tC), (eT, scb_use[pr])], drain=True,
                               inc=(eV, 1))
                            if tc < c.NTC - 1:
                                eg(lambda e, dch0=dch0, dg=dg, n=n: e.tensor_copy(
                                    ST.rearrange("p (d n) -> p d n", n=NS)
                                    [:, dch0 + dg * GD:dch0 + (dg + 1) * GD, n:n + 1],
                                    sbig[:, :, Tc - 1:Tc]),
                                   wait=[(eV, None)], inc=(eG, 1))
                            sC_done = eV.n
                            first = (ng == 0 and ni == 0)
                            last = (ng == NNG - 1 and ni == NG - 1)
                            HBS = c.HBS
                            ybank = max(Tc, 512)
                            for j in range(GD):
                                for h in range(HB):
                                    ep(lambda e, j=j, h=h, pr=pr, first=first,
                                       last=last, HBS=HBS, ybank=ybank: e.matmul(
                                           ps[:, j * ybank + h * HBS:
                                              j * ybank + (h + 1) * HBS],
                                           self.idb[:],
                                           scb[:, pr, j, h * HBS:(h + 1) * HBS],
                                           start=first, stop=last),
                                       wait=[(eV, sC_done), (eV, y_ev)],
                                       inc=(eT, 1) if (j == GD - 1 and h == HB - 1)
                                       else None)
                            scb_use[pr] = eT.n
                            it += 1
                        bc_use[gpr] = eV.n
                        bc_useG[gpr] = eG.n
                    for j in range(GD):
                        dch = dch0 + dg * GD + j
                        zpr = j % 2
                        tz = dma(zsl[zpr], self.z_d[b, dch, :, tcs:tcs + Tc],
                                 wait=[(eS, None), (qst, None)])
                        ev(lambda e, j=j, dch=dch: e.scalar_tensor_tensor(
                            yfull, xssl[:, j, :],
                            self.dvec[:, dch:dch + 1],
                            ps[:, j * max(Tc, 512):j * max(Tc, 512) + Tc],
                            ALU.mult, ALU.add),
                           wait=[(eT, None)], drain=True, inc=(eV, 1))
                        es(lambda e, zpr=zpr: e.activation(gz, zsl[zpr], AF.Silu),
                           wait=[(qio, tz), (eV, None)], drain=True, inc=(eS, 1))
                        ev(lambda e, zpr=zpr: e.tensor_mul(ysl[zpr], yfull, gz),
                           wait=[(eS, None), (qst, None)], drain=True,
                           inc=(eV, 1))
                        ysl_dma[zpr] = dma(self.y_d[b, dch, :, tcs:tcs + Tc],
                                           ysl[zpr], wait=[(eV, None)])
                    y_ev = eV.n
        self.y_tgt = self.qio.n
        self.p32.pop()
        self.p16.pop()
        self.pf16.pop()

    def warm(self, n=24, wait=()):
        ps = self.psum
        for i in range(n):
            self.ep(lambda e, i=i, n=n: e.matmul(
                ps[:, 3968:4096], self.idb[:], self.idb[:],
                start=(i == 0), stop=(i == n - 1)),
                wait=([(self.eT, None)] + list(wait)) if i == 0 else (),
                inc=(self.eT, 1))

    def dab(self, pr, j):
        Tc, GD = self.c.Tc, self.c.GD
        off = (pr * GD + j) * Tc
        return self.pf16.t[:, off:off + Tc]

    def phase34(self, b):
        c = self.c
        KD, KI, KH, T, D, DH, Tc = c.KD, c.KI, c.KH, c.T, c.D, c.DH, c.Tc
        NFC, NTT, TT_C = c.NFC, c.NTT, c.TT_C
        qio, qst, qout = self.qio, self.qst, self.qout
        eV, eS, eT, eG = self.eV, self.eS, self.eT, self.eG
        dma, ev, es, ep, eg = self.dma, self.ev, self.es, self.ep, self.eg
        ps = self.psum
        smv = self.smv

        self.p32.push()
        self.p16.push()
        self.pf16.push()
        f16q = [self.pf16.alloc(D) for _ in range(2)]
        f16h = [self.pf16.alloc(T) for _ in range(2)]
        x1qT = self.p16.alloc(KD * T).rearrange("p (k t) -> p k t", k=KD)
        isrep = self.p16.alloc(T)
        r1b = [self.p32.alloc(D) for _ in range(2)]
        sqb = [self.p32.alloc(D) for _ in range(2)]
        xtk = [self.p32.alloc(D) for _ in range(2)]
        self.p16.push()
        Yb = self.p16.alloc(KI * Tc).rearrange("p (k t) -> p k t", k=KI)
        attn = self.p16.alloc(KD * Tc).rearrange("p (k t) -> p k t", k=KD)
        xqb = [self.p16.alloc(D) for _ in range(2)]
        wop = [self.p16.alloc(KI * 128) for _ in range(2)]

        def layer_norm(tile, pr, out, wait_extra=(), wait_out=()):
            ev(lambda e: e.reduce_sum(smv[:, 0:1], tile, axis=AX.X),
               wait=list(wait_extra), drain=True, inc=(eV, 1))
            es(lambda e, pr=pr: e.activation(sqb[pr], tile, AF.Square),
               wait=[(eV, None)], inc=(eS, 1))
            ev(lambda e, pr=pr: e.reduce_sum(smv[:, 1:2], sqb[pr], axis=AX.X),
               wait=[(eS, None)], drain=True, inc=(eV, 1))
            ev(lambda e: e.tensor_scalar(smv[:, 0:2], smv[:, 0:2], 1.0 / D, None,
                                         ALU.mult), drain=True, inc=(eV, 1))
            ev(lambda e: e.tensor_mul(smv[:, 2:3], smv[:, 0:1], smv[:, 0:1]),
               drain=True, inc=(eV, 1))
            ev(lambda e: e.tensor_sub(smv[:, 1:2], smv[:, 1:2], smv[:, 2:3]),
               drain=True, inc=(eV, 1))
            ev(lambda e: e.tensor_scalar(smv[:, 1:2], smv[:, 1:2], 1e-5, None,
                                         ALU.add), drain=True, inc=(eV, 1))
            ev(lambda e: e.reciprocal(smv[:, 2:3], smv[:, 1:2]), drain=True,
               inc=(eV, 1))
            es(lambda e: e.activation(smv[:, 2:3], smv[:, 2:3], AF.Sqrt),
               wait=[(eV, None)], drain=True, inc=(eS, 1))
            ev(lambda e: e.tensor_scalar(out, tile, smv[:, 0:1], smv[:, 2:3],
                                         ALU.subtract, ALU.mult),
               wait=[(eS, None)] + list(wait_out), drain=True, inc=(eV, 1))

        self.warm()
        wop_use = [0, 0]
        op_ev = [0, 0]
        ptok_use = [0, 0]
        xtk_use = [0, 0]
        pxq_ev = 0
        for tc in range(c.NTC):
            tcs = tc * Tc
            yt = 0
            for kc in range(KI):
                yt = dma(Yb[:, kc, :], self.y_d[b, kc, :, tcs:tcs + Tc],
                         wait=[(eT, None), (qst, None)])
            for m in range(KD):
                pr = m % 2
                tW = dma(wop[pr], self.opw_d[m], wait=[(eT, wop_use[pr])])
                for h in range(c.HB):
                    HBS = c.HBS
                    fpr = (m * c.HB + h) % 2
                    for kc in range(KI):
                        ep(lambda e, pr=pr, fpr=fpr, kc=kc, h=h, HBS=HBS: e.matmul(
                            ps[:, fpr * 512:fpr * 512 + HBS],
                            wop[pr][:, kc * 128:(kc + 1) * 128],
                            Yb[:, kc, h * HBS:(h + 1) * HBS],
                            start=(kc == 0), stop=(kc == KI - 1)),
                            wait=[(qio, max(tW, yt)), (eS, op_ev[fpr])],
                            inc=(eT, 1) if kc == KI - 1 else None)
                    es(lambda e, m=m, h=h, fpr=fpr, HBS=HBS: e.copy(
                        attn[:, m, h * HBS:(h + 1) * HBS],
                        ps[:, fpr * 512:fpr * 512 + HBS]),
                       wait=[(eT, None)], inc=(eS, 1))
                    op_ev[fpr] = eS.n
                wop_use[pr] = eT.n
            attn_done = eS.n
            for tt in range(TT_C):
                pr = tt % 2
                gtt = tc * TT_C + tt
                pt0 = 1024 + pr * 1024
                for m in range(KD):
                    ep(lambda e, m=m, tt=tt, pt0=pt0: e.transpose(
                        self.psum_b[:, 2 * pt0 + m * 128:2 * pt0 + (m + 1) * 128],
                        attn[:, m, tt * 128:(tt + 1) * 128], self.idb[:]),
                        wait=[(eS, attn_done), (eV, ptok_use[pr])],
                        inc=(eT, 1) if m == KD - 1 else None)
                tX = dma(xtk[pr], self.x_h[b, tcs + tt * 128:tcs + (tt + 1) * 128, :],
                         wait=[(eV, xtk_use[pr])])
                ev(lambda e, pr=pr, pt0=pt0: e.tensor_add(
                    r1b[pr], self.psum_b[:, 2 * pt0:2 * pt0 + D], xtk[pr]),
                   wait=[(eT, None), (qio, tX)], drain=True, inc=(eV, 1))
                ptok_use[pr] = eV.n
                xtk_use[pr] = eV.n
                layer_norm(r1b[pr], pr, r1b[pr])
                dma(self.x1_d[b, tcs + tt * 128:tcs + (tt + 1) * 128, :], r1b[pr],
                    wait=[(eV, None)])
                ev(lambda e, pr=pr: e.reduce_max(smv[:, 3:4], r1b[pr], axis=AX.X,
                                                 apply_absolute_value=True),
                   drain=True, inc=(eV, 1))
                ev(lambda e: e.tensor_scalar(smv[:, 3:4], smv[:, 3:4], 1e-5, None,
                                             ALU.max), drain=True, inc=(eV, 1))
                ev(lambda e: e.reciprocal(smv[:, 4:5], smv[:, 3:4]), drain=True,
                   inc=(eV, 1))
                ev(lambda e: e.tensor_scalar(smv[:, 4:5], smv[:, 4:5], 127.0, None,
                                             ALU.mult), drain=True, inc=(eV, 1))
                ev(lambda e, gtt=gtt: e.tensor_mul(self.itok[:, gtt:gtt + 1],
                                                   smv[:, 3:4], self.iw1[:]),
                   drain=True, inc=(eV, 1))
                ev(lambda e, pr=pr: e.tensor_scalar(r1b[pr], r1b[pr], smv[:, 4:5],
                                                    None, ALU.mult),
                   wait=[(qst, None)], drain=True, inc=(eV, 1))
                ev(lambda e, pr=pr: e.tensor_scalar(f16q[pr], r1b[pr],
                                                    MAGIC16, None, ALU.add),
                   drain=True, inc=(eV, 1))
                ev(lambda e, pr=pr: e.tensor_scalar(xqb[pr], f16q[pr],
                                                    MAGIC16, None, ALU.subtract),
                   drain=True, inc=(eV, 1))
                for kc in range(KD):
                    ep(lambda e, pr=pr, kc=kc: e.transpose(
                        self.psum_b[:, 6144 + kc * 128:6144 + (kc + 1) * 128],
                        xqb[pr][:, kc * 128:(kc + 1) * 128], self.idb[:]),
                        wait=[(eV, None), (eS, pxq_ev)],
                        inc=(eT, 1) if kc == KD - 1 else None)
                es(lambda e, gtt=gtt: e.copy(
                    x1qT[:, :, gtt * 128:(gtt + 1) * 128],
                    self.psum_b[:, 6144:6144 + D].rearrange("p (k t) -> p k t", k=KD)),
                   wait=[(eT, None)], inc=(eS, 1))
                pxq_ev = eS.n
        dma(self.itok_d, self.itok[:, 0:NTT], wait=[(eV, None)])
        dma(self.is_row[0:1, :],
            self.itok_d.transpose([1, 0]).unsqueeze(0),
            wait=[(qst, None)], slow=True)
        ev(lambda e: e.tensor_copy(self.is_rowb, self.is_row),
           wait=[(qio, None)], drain=True, inc=(eV, 1))
        dma(self.rowb_d[0:1, :], self.is_rowb, wait=[(eV, None)])
        dma(isrep, self.rowb_d[0:1, :].broadcast_to([128, T]), wait=[(qst, None)])
        is_done = None

        self.p16.pop()
        hst = [self.p16.alloc(T) for _ in range(2)]
        w1t = [self.p16.alloc(KD * 128) for _ in range(2)]
        h2t = [self.p16.alloc(T) for _ in range(2)]
        sc2rep = self.p16.alloc(T)
        iv2rep = self.p16.alloc(T)
        w2t = [self.p16.alloc(KD * 128) for _ in range(2)]
        h2s = [self.p16.alloc(512) for _ in range(2)]
        outb = [self.p32.alloc(D) for _ in range(2)]
        self.p32.push()
        hpad = [self.p32.alloc(T + 2) for _ in range(2)]
        hcv = [self.p32.alloc(T) for _ in range(2)]

        gtok = self.p32.alloc(c.NTT)
        tmax = self.p32.alloc(c.NTT)
        ev(lambda e: e.memset(gtok, 0.0), inc=(eV, 1))
        x1qT_done = pxq_ev
        self.warm(wait=[(self.eS, x1qT_done)])
        gm_ev = 0
        w1_use = [0, 0]
        h_ev = [0, 0]
        hc_use = [0, 0]
        hs_use = [0, 0]
        for mh in range(KH):
            pr = mh % 2
            tW = dma(w1t[pr], self.w1t_d[mh], wait=[(eT, w1_use[pr])])
            ev(lambda e, pr=pr: e.memset(hpad[pr][:, 0:1], 0.0),
               wait=[(eV, hc_use[pr])], inc=(eV, 1))
            ev(lambda e, pr=pr: e.memset(hpad[pr][:, T + 1:T + 2], 0.0), inc=(eV, 1))
            pad_ev = eV.n
            for fc in range(NFC):
                fpr = fc % 2
                for kc in range(KD):
                    ep(lambda e, pr=pr, fpr=fpr, kc=kc, fc=fc: e.matmul(
                        ps[:, fpr * 512:(fpr + 1) * 512],
                        w1t[pr][:, kc * 128:(kc + 1) * 128],
                        x1qT[:, kc, fc * 512:(fc + 1) * 512],
                        start=(kc == 0), stop=(kc == KD - 1)),
                        wait=[(qio, tW), (eV, h_ev[fpr]), (eS, x1qT_done)],
                        inc=(eT, 1) if kc == KD - 1 else None)
                ev(lambda e, pr=pr, fc=fc, fpr=fpr: e.tensor_mul(
                    hpad[pr][:, 1 + fc * 512:1 + (fc + 1) * 512],
                    ps[:, fpr * 512:(fpr + 1) * 512],
                    isrep[:, fc * 512:(fc + 1) * 512]),
                   wait=[(eT, None), (qio, None), (eV, pad_ev)], drain=True,
                   inc=(eV, 1))
                h_ev[fpr] = eV.n
            w1_use[pr] = eT.n
            ev(lambda e, pr=pr, mh=mh: e.tensor_scalar(
                hcv[pr], hpad[pr][:, 0:T], self.fcw[:, mh, 0:1], None, ALU.mult),
               drain=True, inc=(eV, 1))
            for k in range(1, 3):
                ev(lambda e, pr=pr, mh=mh, k=k: e.scalar_tensor_tensor(
                    hcv[pr], hpad[pr][:, k:k + T], self.fcw[:, mh, k:k + 1],
                    hcv[pr], ALU.mult, ALU.add), drain=True, inc=(eV, 1))
            es(lambda e, pr=pr, mh=mh: e.activation(
                hst[pr], hcv[pr], AF.Silu, bias=self.fcb[:, mh:mh + 1], scale=1.0),
               wait=[(eV, None), (qst, None)], drain=True, inc=(eS, 1))
            hc_use[pr] = eS.n
            dma(self.hs_d[b, mh], hst[pr], wait=[(eS, None)])
            hs_use[pr] = self.qio.n
            for tt in range(c.NTT):
                ep(lambda e, pr=pr, tt=tt: e.transpose(
                    self.psum_b[:, 4096 + tt * 128:4096 + (tt + 1) * 128],
                    hst[pr][:, tt * 128:(tt + 1) * 128], self.idb[:]),
                    wait=[(eS, None), (eV, gm_ev)],
                    inc=(eT, 1) if tt == c.NTT - 1 else None)
            ev(lambda e: e.reduce_max(
                tmax, self.psum_b[:, 4096:4096 + c.NTT * 128]
                .rearrange("p (a b) -> p a b", a=c.NTT), axis=AX.X,
                apply_absolute_value=True),
               wait=[(eT, None)], drain=True, inc=(eV, 1))
            ev(lambda e: e.tensor_max(gtok, gtok, tmax), drain=True, inc=(eV, 1))
            gm_ev = eV.n
        self.p32.pop()
        self.p32.push()
        qbf = [self.p32.alloc(T) for _ in range(2)]
        sc2f = self.p32.alloc(T)[0:1, :]
        dma(self.gtok_d, gtok, wait=[(eV, None)])
        dma(self.gm_row, self.gtok_d.transpose([1, 0]).unsqueeze(0),
            wait=[(qst, None)], slow=True)
        ev(lambda e: e.tensor_scalar(self.gm_row, self.gm_row, 1e-5, None,
                                     ALU.max), wait=[(qio, None)], drain=True,
           inc=(eV, 1))
        ev(lambda e: e.reciprocal(sc2f, self.gm_row), drain=True,
           inc=(eV, 1))
        ev(lambda e: e.tensor_scalar(sc2f, sc2f, 127.0, None,
                                     ALU.mult), drain=True, inc=(eV, 1))
        ev(lambda e: e.tensor_scalar(self.iv2_rowb, self.gm_row,
                                     self.iw2[0:1, 0:1], None, ALU.mult),
           drain=True, inc=(eV, 1))
        ev(lambda e: e.tensor_copy(self.sc2_rowb, sc2f),
           drain=True, inc=(eV, 1))
        dma(self.rowb_d[1:2, :], self.sc2_rowb, wait=[(eV, None)])
        dma(self.rowb_d[2:3, :], self.iv2_rowb)
        dma(sc2rep, self.rowb_d[1:2, :].broadcast_to([128, T]), wait=[(qst, None)])
        dma(iv2rep, self.rowb_d[2:3, :].broadcast_to([128, T]))
        q2_done = None
        h2_use = [0, 0]
        for kh in range(KH):
            pr = kh % 2
            th = dma(h2t[pr], self.hs_d[b, kh], wait=[(eV, h2_use[pr]), (qst, None)])
            ev(lambda e, pr=pr: e.tensor_mul(qbf[pr], h2t[pr], sc2rep),
               wait=[(qio, th)], drain=True, inc=(eV, 1))
            ev(lambda e, pr=pr: e.tensor_scalar(f16h[pr], qbf[pr],
                                                MAGIC16, None, ALU.add),
               drain=True, inc=(eV, 1))
            ev(lambda e, pr=pr: e.tensor_scalar(h2t[pr], f16h[pr],
                                                MAGIC16, None, ALU.subtract),
               drain=True, inc=(eV, 1))
            h2_use[pr] = eV.n
            dma(self.h2q_d[b, kh], h2t[pr], wait=[(eV, None)])
        h2q_tgt = self.qio.n
        self.p32.pop()
        self.p32.push()
        ffn_f = self.p32.alloc(KD * 512).rearrange("p (m t) -> p m t", m=KD)
        ffn_ev = 0
        fc_clear = 0
        pt_use = [0, 0]
        w2_use = [0, 0]
        self.warm(wait=[(self.eV, None)])
        for fc in range(NFC):
            for kh in range(KH):
                pr = kh % 2
                th = dma(h2s[pr], self.h2q_d[b, kh, :, fc * 512:(fc + 1) * 512],
                         wait=[(eT, w2_use[pr]), (qst, None)])
                tW = dma(w2t[pr], self.w2t_d[kh])
                for m in range(KD):
                    wl = [(qio, max(th, tW)), (eV, fc_clear)]
                    if fc == 0 and kh == 0 and m == KD - 1:
                        wl.append((eT, None))
                    ep(lambda e, pr=pr, m=m, kh=kh: e.matmul(
                        ps[:, m * 512:(m + 1) * 512],
                        w2t[pr][:, m * 128:(m + 1) * 128], h2s[pr],
                        start=(kh == 0), stop=(kh == KH - 1)),
                        wait=wl,
                        inc=(eT, 1) if m == KD - 1 else None)
                w2_use[pr] = eT.n
            for m in range(KD):
                ev(lambda e, m=m, fc=fc: e.tensor_mul(
                    ffn_f[:, m, :], ps[:, m * 512:(m + 1) * 512],
                    iv2rep[:, fc * 512:(fc + 1) * 512]),
                   wait=[(eT, None)], drain=True, inc=(eV, 1))
            ffn_ev = eV.n
            for tt in range(4):
                pr = tt % 2
                rows = fc * 512 + tt * 128
                for m in range(KD):
                    ep(lambda e, m=m, tt=tt, pr=pr: e.transpose(
                        ps[:, pr * 1024 + m * 128:pr * 1024 + (m + 1) * 128],
                        ffn_f[:, m, tt * 128:(tt + 1) * 128], self.idf[:]),
                        wait=[(eV, ffn_ev), (eV, pt_use[pr])],
                        inc=(eT, 1) if m == KD - 1 else None)
                tX = dma(xtk[pr], self.x1_d[b, rows:rows + 128, :],
                         wait=[(eV, xtk_use[pr]), (qst, None)])
                ev(lambda e, pr=pr: e.tensor_add(r1b[pr], ps[:, pr * 1024:pr * 1024 + D],
                                                 xtk[pr]),
                   wait=[(eT, None), (qio, tX)], drain=True, inc=(eV, 1))
                pt_use[pr] = eV.n
                xtk_use[pr] = eV.n
                layer_norm(r1b[pr], pr, outb[pr], wait_out=[(qout, None)])
                dma(self.out_h[b, rows:rows + 128, :], outb[pr],
                    wait=[(eV, None)], sem=qout)
            fc_clear = eV.n
        self.p32.pop()
        self.p32.pop()
        self.p16.pop()
        self.pf16.pop()


def build_program(cfg, dbg=False):
    return Builder(cfg).build(dbg=dbg)


def kernel(**inputs):
    import ml_dtypes
    cfg = Cfg(small=False)
    x = np.ascontiguousarray(np.asarray(inputs["L0_attn_x"], dtype=np.float32))
    n_cores = 8
    shards = np.split(x, n_cores, axis=0)
    base = {k: np.ascontiguousarray(np.asarray(v, dtype=np.float32))
            for k, v in inputs.items() if k != "L0_attn_x"}
    ident_f = np.eye(128, dtype=np.float32)
    ident_b = np.eye(128, dtype=ml_dtypes.bfloat16)
    nc = build_program(cfg)
    in_maps = []
    for i in range(n_cores):
        m = dict(base)
        m["L0_attn_x"] = shards[i]
        m["ident_f32"] = ident_f
        m["ident_bf16"] = ident_b
        in_maps.append(m)
    res = run_bass_kernel_spmd(nc, in_maps, core_ids=list(range(n_cores)))
    return np.concatenate([r["out"] for r in res.results], axis=0)


# version 22
# speedup vs baseline: 1.0040x; 1.0040x over previous
import numpy as np

import concourse.bass as bass
import concourse.mybir as mybir
import concourse.library_config as libcfg
from concourse.bass_isa_rust import ReduceOp
from concourse.bass_utils import run_bass_kernel_spmd

F32 = mybir.dt.float32
BF16 = mybir.dt.bfloat16
F16 = mybir.dt.float16
ALU = mybir.AluOpType
AF = mybir.ActivationFunctionType
AX = mybir.AxisListType

MAGIC = float(np.float32(3 * 2 ** 22))
MAGIC16 = 1536.0


class Cfg:
    def __init__(self, small=False):
        if small:
            self.B, self.T, self.D, self.DI, self.DH, self.Tc = 1, 512, 512, 512, 1024, 256
        else:
            self.B, self.T, self.D, self.DI, self.DH, self.Tc = 2, 2048, 1024, 2048, 4096, 1024
        self.NS = 64
        self.KD = self.D // 128
        self.KI = self.DI // 128
        self.KH = self.DH // 128
        self.NTC = self.T // self.Tc
        self.NFC = self.T // 512
        self.NTT = self.T // 128
        self.TT_C = self.Tc // 128
        self.NDH = 2
        self.DPH = self.KI // self.NDH
        self.GD = min(4, self.DPH)
        self.NDG = self.DPH // self.GD
        self.NG = 4
        self.NNG = self.NS // self.NG
        self.HBS = min(512, self.Tc)
        self.HB = self.Tc // self.HBS
        assert self.GD * self.Tc * 4 <= 8 * 2048


ENGINES = ("pe", "ve", "se", "ge", "sy")


class Sem:
    __slots__ = ("h", "n", "name", "dma", "last_waited", "boundary")

    def __init__(self, h, name, dma=False):
        self.h, self.n, self.name, self.dma = h, 0, name, dma
        self.last_waited = 0
        self.boundary = 0

    def bump(self, k):
        self.n += k
        return self.n


class Pool:
    def __init__(self, t, elems):
        self.t, self.elems, self.off, self.marks = t, elems, 0, []

    def alloc(self, n):
        assert self.off + n <= self.elems, f"pool overflow: {self.off}+{n}>{self.elems}"
        o = self.off
        self.off += n
        return self.t[:, o:o + n]

    def push(self):
        self.marks.append(self.off)

    def pop(self):
        self.off = self.marks.pop()


class Builder:
    def __init__(self, cfg):
        self.c = cfg
        self.nc = bass.Bass("TRN2", target_bir_lowering=False)
        self.ops = {e: [] for e in ENGINES}
        self.lastwait = {e: {} for e in ENGINES}
        self._guards = []
        self.dma_sems = []

    def emit(self, e, fn, wait=(), drain=False, inc=None):
        if drain:
            sem = getattr(self, "engine_sem", {}).get(e)
            if sem is not None:
                wait = list(wait) + [(sem, sem.n)]
                drain = False
        wl = []
        for sem, cnt in wait:
            if cnt is None or sem.dma:
                cnt = sem.n
            if sem.dma:
                sem.last_waited = max(sem.last_waited, cnt)
            if cnt > self.lastwait[e].get(id(sem), -1):
                wl.append((sem, cnt))
                self.lastwait[e][id(sem)] = cnt
        if e == "sy" and wl:
            for s in self.dma_sems:
                s.boundary = s.n
        if inc is not None:
            inc[0].bump(inc[1])

        def op(eng, wl=wl, drain=drain, fn=fn, inc=inc):
            for sem, cnt in wl:
                eng.wait_ge(sem.h, cnt)
            if drain:
                eng.drain()
            r = fn(eng)
            if inc is not None:
                r.then_inc(inc[0].h, inc[1])

        self.ops[e].append(op)

    def barrier(self):
        b = self.bar
        tgt = b.n + len(ENGINES)
        b.n = tgt
        dwaits = [(s.h, s.n) for s in self.dma_sems if s.n > 0]
        for s in self.dma_sems:
            s.last_waited = max(s.last_waited, s.n)
            s.boundary = s.n
        for e in ENGINES:
            def op(eng, tgt=tgt, b=b, dwaits=dwaits):
                for h, n in dwaits:
                    eng.wait_ge(h, n)
                eng.drain()
                eng.sem_inc(b.h, 1)
                eng.wait_ge(b.h, tgt)
            self.ops[e].append(op)
            self.lastwait[e] = {}

    def dma(self, dst, src, wait=(), sem=None, slow=False):
        if sem is None:
            try:
                is_store = "DRAM" in str(dst.space)
            except Exception:
                is_store = False
            sem = self.qst if is_store else self.qio
        wait = list(wait)
        if sem.last_waited > sem.boundary:
            wait.append((sem, sem.last_waited))
            self.lastwait["sy"].pop(id(sem), None)
        self.emit("sy", lambda e, d=dst, s=src, sl=slow: e.dma_start(
            out=d, in_=s, allow_slow_non_contiguous=sl),
                  wait=wait, inc=(sem, 16))
        return sem.n

    def ev(self, fn, **kw):
        self.emit("ve", fn, **kw)

    def es(self, fn, **kw):
        self.emit("se", fn, **kw)

    def ep(self, fn, **kw):
        self.emit("pe", fn, **kw)

    def eg(self, fn, **kw):
        self.emit("ge", fn, **kw)

    def sb(self, name, shape, dtype):
        g = self.nc.sbuf_tensor(name, list(shape), dtype)
        t = g.__enter__()
        self._guards.append(g)
        return t

    def build(self, dbg=False):
        nc, c = self.nc, self.c
        B, T, D, DI, DH, NS = c.B, c.T, c.D, c.DI, c.DH, c.NS
        KD, KI, KH, Tc = c.KD, c.KI, c.KH, c.Tc

        din = lambda n, s: nc.dram_tensor(n, list(s), F32, kind="ExternalInput").ap()
        self.x_h = din("L0_attn_x", (B, T, D))
        self.ipw_h = din("L0_attn_in_proj_w", (D, 2 * DI))
        self.mcw_h = din("L0_attn_conv_w", (DI, 1, 4))
        self.mcb_h = din("L0_attn_conv_b", (DI,))
        self.xpw_h = din("L0_attn_x_proj_w", (DI, 2 * NS + 1))
        self.dtw_h = din("L0_attn_dt_w", (1, DI))
        self.dtb_h = din("L0_attn_dt_b", (DI,))
        self.alog_h = din("L0_attn_A_log", (DI, NS))
        self.dvec_h = din("L0_attn_D", (DI,))
        self.opw_h = din("L0_attn_out_proj_w", (DI, D))
        self.w1_h = din("L0_ffn_w1", (DH, D))
        self.fcw_h = din("L0_ffn_conv_w", (DH, 1, 3))
        self.fcb_h = din("L0_ffn_conv_b", (DH,))
        self.w2_h = din("L0_ffn_w2", (D, DH))
        self.idf_h = din("ident_f32", (128, 128))
        self.idb_h = nc.dram_tensor("ident_bf16", [128, 128], BF16,
                                    kind="ExternalInput").ap()
        self.out_h = nc.dram_tensor("out", [B, T, D], F32, kind="ExternalOutput").ap()

        scr = lambda n, s, d: nc.dram_tensor(n, list(s), d, kind="Internal").ap()
        self.ipw_d = scr("ipw_bf", (2 * KI, 128, KD * 128), BF16)
        self.opw_d = scr("opw_bf", (KD, 128, KI * 128), BF16)
        self.w1t_d = scr("w1qT", (KH, 128, KD * 128), BF16)
        self.w2t_d = scr("w2qT", (KH, 128, KD * 128), BF16)
        self.z_d = scr("z_dram", (B, KI, 128, T), BF16)
        self.xs_d = scr("xs_dram", (B, KI, 128, T), BF16)
        self.bmat_d = scr("bmat_dram", (B, NS, T), BF16)
        self.cmat_d = scr("cmat_dram", (B, NS, T), BF16)
        self.y_d = scr("y_dram", (B, KI, 128, T), BF16)
        self.x1_d = scr("x1_dram", (B, T, D), F32)
        self.hs_d = scr("hs_dram", (B, KH, 128, T), BF16)
        self.h2q_d = scr("h2q_dram", (B, KH, 128, T), BF16)
        self.itok_d = scr("itok_dram", (128, c.NTT), F32)
        self.gtok_d = scr("gtok_dram", (128, c.NTT), F32)
        self.row_d = scr("row_dram", (2, T), F32)
        self.rowb_d = scr("rowb_dram", (4, T), BF16)
        self.scl_d = scr("scl_dram", (2, 2), F32)

        self.idf = self.sb("idf", (128, 128), F32)
        self.idb = self.sb("idb", (128, 128), BF16)
        self.mcw = self.sb("mcw", (128, KI, 4), F32)
        self.mcb = self.sb("mcb", (128, KI), F32)
        self.dtw = self.sb("dtw", (128, KI), F32)
        self.dtb = self.sb("dtb", (128, KI), F32)
        self.dvec = self.sb("dvec", (128, KI), F32)
        self.aneg = self.sb("aneg", (128, KI, NS), F32)
        self.fcw = self.sb("fcw", (128, KH, 3), F32)
        self.fcb = self.sb("fcb", (128, KH), F32)
        self.xpw = self.sb("xpw", (128, KI, 144), BF16)
        self.sw1 = self.sb("sw1", (128, 1), F32)
        self.iw1 = self.sb("iw1", (128, 1), F32)
        self.sw2 = self.sb("sw2", (128, 1), F32)
        self.iw2 = self.sb("iw2", (128, 1), F32)
        self.rowbuf = self.sb("rowbuf", (128, T), F32)
        self.gm_row = self.rowbuf[0:1, 0:T]
        self.dtr_row = self.rowbuf[32:33, 0:T]
        self.is_row = self.rowbuf[64:65, 0:T]
        self.rowb16 = self.sb("rowb16", (128, 2 * T), BF16)
        self.sc2_rowb = self.rowb16[0:1, 0:T]
        self.iv2_rowb = self.rowb16[0:1, T:2 * T]
        self.is_rowb = self.rowb16[64:65, 0:T]
        self.smv = self.sb("smv", (128, 16), F32)
        self.smb = self.sb("smb", (128, 5 * c.TT_C), F32)
        self.itok = self.sb("itok", (128, c.NTT), F32)

        self.p32 = Pool(self.sb("p32", (128, 16640), F32), 16640)
        self.p16 = Pool(self.sb("p16", (128, 49408), BF16), 49408)
        self.pf16 = Pool(self.sb("pf16", (128, 8192), F16), 8192)
        pg = self.nc.psum_tensor("psum", [128, 4096], F32)
        self.psum = pg.__enter__()
        self._guards.append(pg)
        self.psum_b = self.psum[:, :].bitcast(BF16)

        self.sems = {}
        for name in ("bar", "qio", "qst", "qout", "eV", "eS", "eT", "eG"):
            g = nc.semaphore(name)
            h = g.__enter__()
            self._guards.append(g)
            self.sems[name] = Sem(h, name, dma=name.startswith("q"))
        self.bar = self.sems["bar"]
        self.qio = self.sems["qio"]
        self.qst = self.sems["qst"]
        self.qout = self.sems["qout"]
        self.dma_sems = [self.sems[n] for n in ("qio", "qst", "qout")]
        self.eV = self.sems["eV"]
        self.eS = self.sems["eS"]
        self.eT = self.sems["eT"]
        self.eG = self.sems["eG"]
        self.engine_sem = {"ve": self.eV, "se": self.eS, "pe": self.eT,
                           "ge": self.eG}

        self.phase0()
        self.barrier()
        for b in range(B):
            self.phase1(b)
            self.barrier()
            self.scan(b)
            self.barrier()
            self.phase34(b)
            self.barrier()
        if dbg:
            for name, t in (("d_xs", self.xs_d), ("d_z", self.z_d),
                            ("d_bm", self.bmat_d), ("d_cm", self.cmat_d),
                            ("d_y", self.y_d), ("d_x1", self.x1_d),
                            ("d_hs", self.hs_d), ("d_h2q", self.h2q_d)):
                o = nc.dram_tensor(name, list(t.shape), t.dtype,
                                   kind="ExternalOutput").ap()
                self.dma(o, t, sem=self.qout)
        self.emit("sy", lambda e: e.nop(), wait=[(self.qout, None)])

        with nc.Block() as block:
            block.sync(lambda e: [op(e) for op in self.ops["sy"]])
            block.vector(lambda e: [op(e) for op in self.ops["ve"]])
            block.scalar(lambda e: [op(e) for op in self.ops["se"]])
            block.tensor(lambda e: [op(e) for op in self.ops["pe"]])
            block.gpsimd(lambda e: [op(e) for op in self.ops["ge"]])

        for g in reversed(self._guards):
            g.__exit__(None, None, None)
        return nc

    def phase0(self):
        c, KD, KI, KH = self.c, self.c.KD, self.c.KI, self.c.KH
        D, DH, NS = self.c.D, self.c.DH, self.c.NS
        qio, qst = self.qio, self.qst
        eV, eS, eT, eG = self.eV, self.eS, self.eT, self.eG
        dma, ev, es, ep, eg = self.dma, self.ev, self.es, self.ep, self.eg

        dma(self.idf[:], self.idf_h)
        dma(self.idb[:], self.idb_h)
        for kc in range(KI):
            dma(self.mcw[:, kc, :], self.mcw_h[kc * 128:(kc + 1) * 128, 0, :])
            dma(self.aneg[:, kc, :], self.alog_h[kc * 128:(kc + 1) * 128, :])
        dma(self.mcb[:], self.mcb_h.rearrange("(c p) -> p c", p=128), slow=True)
        dma(self.dtw[:], self.dtw_h[0, :].rearrange("(c p) -> p c", p=128), slow=True)
        dma(self.dtb[:], self.dtb_h.rearrange("(c p) -> p c", p=128), slow=True)
        dma(self.dvec[:], self.dvec_h.rearrange("(c p) -> p c", p=128), slow=True)
        for kc in range(KH):
            dma(self.fcw[:, kc, :], self.fcw_h[kc * 128:(kc + 1) * 128, 0, :])
        dma(self.fcb[:], self.fcb_h.rearrange("(c p) -> p c", p=128), slow=True)
        ld0 = qio.n

        es(lambda e: e.activation(self.aneg[:], self.aneg[:], AF.Exp),
           wait=[(qio, ld0)], inc=(eS, 1))
        ev(lambda e: e.tensor_scalar(self.aneg[:], self.aneg[:], -1.0, None, ALU.mult),
           wait=[(eS, None)], inc=(eV, 1))

        self.p32.push()
        stg = self.p32.alloc(2 * NS + 1)
        for kc in range(KI):
            t = dma(stg, self.xpw_h[kc * 128:(kc + 1) * 128, :], wait=[(eV, None)])
            ev(lambda e, kc=kc: e.tensor_copy(self.xpw[:, kc, 0:2 * NS + 1], stg),
               wait=[(qio, t)], drain=True, inc=(eV, 1))
        self.p32.pop()

        DI2 = 2 * self.c.DI
        self.p32.push()
        self.p16.push()
        wstg = [self.p32.alloc(max(DI2, DH)) for _ in range(2)]
        wbig = self.p16.alloc(KD * DI2).rearrange("p (k m) -> p k m", k=KD)
        mstg = [self.p16.alloc(KI * 128) for _ in range(2)]
        cast_ev = [eV.n, eV.n]
        for kc in range(KD):
            pr = kc % 2
            t = dma(wstg[pr][:, :DI2], self.ipw_h[kc * 128:(kc + 1) * 128, :],
                    wait=[(eV, cast_ev[pr])])
            ev(lambda e, pr=pr, kc=kc: e.tensor_copy(wbig[:, kc, :],
                                                     wstg[pr][:, :DI2]),
               wait=[(qio, t)], inc=(eV, 1))
            cast_ev[pr] = eV.n
        for m in range(2 * KI):
            pr = m % 2
            ev(lambda e, pr=pr, m=m: e.tensor_copy(
                mstg[pr][:, :KD * 128],
                wbig[:, :, m * 128:(m + 1) * 128]),
               wait=[(eV, None), (qst, None)], inc=(eV, 1))
            dma(self.ipw_d[m], mstg[pr][:, :KD * 128], wait=[(eV, None)])
        cast_ev = [eV.n, eV.n]
        for kc in range(KI):
            pr = kc % 2
            t = dma(wstg[pr][:, :D], self.opw_h[kc * 128:(kc + 1) * 128, :],
                    wait=[(eV, cast_ev[pr])])
            ev(lambda e, pr=pr, kc=kc: e.tensor_copy(
                wbig.rearrange("p k m -> p (k m)")[:, kc * D:(kc + 1) * D],
                wstg[pr][:, :D]),
               wait=[(qio, t), (qst, None)], inc=(eV, 1))
            cast_ev[pr] = eV.n
        obig = wbig.rearrange("p k m -> p (k m)")[:, :KI * D].rearrange(
            "p (k m) -> p k m", k=KI)
        for m in range(KD):
            pr = m % 2
            ev(lambda e, pr=pr, m=m: e.tensor_copy(
                mstg[pr], obig[:, :, m * 128:(m + 1) * 128]),
               wait=[(eV, None), (qst, None)], inc=(eV, 1))
            dma(self.opw_d[m], mstg[pr], wait=[(eV, None)])

        acc = self.p32.alloc(4)
        ev(lambda e: e.memset(acc[:, 0:2], 0.0), inc=(eV, 1))
        for src_, col, rows, cols in ((self.w1_h, 0, KH, D), (self.w2_h, 1, KD, DH)):
            for r in range(rows):
                pr = r % 2
                t = dma(wstg[pr][:, :cols], src_[r * 128:(r + 1) * 128, :],
                        wait=[(eV, None)])
                ev(lambda e, pr=pr, col=col, cols=cols: e.reduce_sum(
                    acc[:, 2 + col:3 + col], wstg[pr][:, :cols], axis=AX.X,
                    apply_absolute_value=True),
                   wait=[(qio, t)], drain=True, inc=(eV, 1))
                ev(lambda e, col=col: e.tensor_add(
                    acc[:, col:col + 1], acc[:, col:col + 1], acc[:, 2 + col:3 + col]),
                   drain=True, inc=(eV, 1))
        ep(lambda e: e.transpose(self.psum[0:2, 0:128], acc[:, 0:2], self.idf[:]),
           wait=[(eV, None), (eS, None)], inc=(eT, 1))
        ev(lambda e: e.reduce_sum(acc[0:2, 2:3], self.psum[0:2, 0:128], axis=AX.X),
           wait=[(eT, None)], drain=True, inc=(eV, 1))
        scale = 1.0 / (DH * D)
        ev(lambda e: e.tensor_scalar(acc[0:2, 2:3], acc[0:2, 2:3], scale, 1e-5,
                                     ALU.mult, ALU.max), drain=True, inc=(eV, 1))
        ev(lambda e: e.tensor_scalar(acc[0:2, 3:4], acc[0:2, 2:3], 1.0 / 127.0, None,
                                     ALU.mult), drain=True, inc=(eV, 1))
        ev(lambda e: e.reciprocal(acc[0:2, 0:1], acc[0:2, 2:3]), drain=True,
           inc=(eV, 1))
        dma(self.scl_d[0:1, :], acc[0:2, 0:1], wait=[(eV, None)], slow=True)
        dma(self.scl_d[1:2, :], acc[0:2, 3:4], wait=[(eV, None)], slow=True)
        dma(self.sw1[:], self.scl_d[0:1, 0:1].broadcast_to([128, 1]), slow=True,
            wait=[(qst, None)])
        dma(self.sw2[:], self.scl_d[0:1, 1:2].broadcast_to([128, 1]), slow=True)
        dma(self.iw1[:], self.scl_d[1:2, 0:1].broadcast_to([128, 1]), slow=True)
        dma(self.iw2[:], self.scl_d[1:2, 1:2].broadcast_to([128, 1]), slow=True)

        self.pf16.push()
        f16w = self.pf16.alloc(max(D, DH))
        qb = [self.p16.alloc(max(D, DH)) for _ in range(2)]
        rslab = [self.p16.alloc(KD * 128) for _ in range(2)]
        w2big = wbig.rearrange("p k m -> p (k m)")[:, :KH * KD * 128].rearrange(
            "p (h m q) -> p h m q", h=KH, m=KD)
        for src_, rows, cols, scol, w2 in ((self.w1_h, KH, D, self.sw1, False),
                                           (self.w2_h, KD, DH, self.sw2, True)):
            for r in range(rows):
                pr = r % 2
                t = dma(wstg[pr][:, :cols], src_[r * 128:(r + 1) * 128, :],
                        wait=[(eV, None)])
                ev(lambda e, pr=pr, cols=cols, scol=scol: e.tensor_scalar(
                    wstg[pr][:, :cols], wstg[pr][:, :cols], scol[:, 0:1], None,
                    ALU.mult), wait=[(qio, t), (eV, None), (eT, None)], drain=True,
                   inc=(eV, 1))
                ev(lambda e, pr=pr, cols=cols: e.tensor_scalar(
                    wstg[pr][:, :cols], wstg[pr][:, :cols], 1.49, -1.49,
                    ALU.min, ALU.max), drain=True, inc=(eV, 1))
                ev(lambda e, pr=pr, cols=cols: e.tensor_scalar(
                    f16w[:, :cols], wstg[pr][:, :cols], MAGIC16, None,
                    ALU.add), drain=True, inc=(eV, 1))
                ev(lambda e, pr=pr, cols=cols: e.tensor_scalar(
                    qb[pr][:, :cols], f16w[:, :cols], MAGIC16, None,
                    ALU.subtract), drain=True, inc=(eV, 1))
                nblk = cols // 128
                for g0 in range(0, nblk, 4):
                    gn = min(4, nblk - g0)
                    for j in range(gn):
                        kc = g0 + j
                        self.ep(lambda e, pr=pr, kc=kc, j=j: e.transpose(
                            self.psum_b[:, j * 128:(j + 1) * 128],
                            qb[pr][:, kc * 128:(kc + 1) * 128], self.idb[:]),
                            wait=[(eV, None), (eS, None)],
                            inc=(eT, 1) if j == gn - 1 else None)
                    if w2:
                        es(lambda e, r=r, g0=g0, gn=gn: e.copy(
                            w2big[:, g0:g0 + gn, r, :],
                            self.psum_b[:, :gn * 128].rearrange(
                                "p (a q) -> p a q", a=gn)),
                           wait=[(eT, None)], inc=(eS, 1))
                    else:
                        es(lambda e, pr=pr, g0=g0, gn=gn: e.copy(
                            rslab[pr][:, g0 * 128:(g0 + gn) * 128],
                            self.psum_b[:, :gn * 128]),
                           wait=[(eT, None), (qst, None)], inc=(eS, 1))
                if not w2:
                    dma(self.w1t_d[r], rslab[pr], wait=[(eS, None)])
        for kh in range(KH):
            dma(self.w2t_d[kh],
                w2big.rearrange("p h m q -> p (h m q)")
                [:, kh * KD * 128:(kh + 1) * KD * 128],
                wait=[(eS, None)])
        self.p32.pop()
        self.p16.pop()
        self.pf16.pop()
        self.w_prep_tgt = self.qio.n

    def phase1(self, b):
        c = self.c
        KD, KI, T, D, NS, NFC, NTT = c.KD, c.KI, c.T, c.D, c.NS, c.NFC, c.NTT
        qio, qst = self.qio, self.qst
        eV, eS, eT, eG = self.eV, self.eS, self.eT, self.eG
        dma, ev, es, ep, eg = self.dma, self.ev, self.es, self.ep, self.eg
        ps = self.psum

        self.p32.push()
        self.p16.push()
        xT = self.p16.alloc(KD * T).rearrange("p (k t) -> p k t", k=KD)
        xtok = [self.p32.alloc(D) for _ in range(2)]
        self.warm()
        tr_ev = [0, 0]
        ev_ev = [0, 0]
        for tt in range(NTT):
            pr = tt % 2
            t = dma(xtok[pr], self.x_h[b, tt * 128:(tt + 1) * 128, :],
                    wait=[(eT, tr_ev[pr])])
            for g in range(KD // 4):
                for j in range(4):
                    kc = g * 4 + j
                    ep(lambda e, pr=pr, kc=kc, j=j: e.transpose(
                        ps[:, pr * 1024 + (kc % 4) * 128 + (kc // 4) * 512:
                           pr * 1024 + (kc % 4) * 128 + (kc // 4) * 512 + 128],
                        xtok[pr][:, kc * 128:(kc + 1) * 128], self.idf[:]),
                        wait=[(qio, t), (eS, ev_ev[pr])],
                        inc=(eT, 1) if (g == KD // 4 - 1 and j == 3) else None)
            tr_ev[pr] = eT.n
            for g in range(KD // 4):
                es(lambda e, pr=pr, g=g, tt=tt: e.copy(
                    xT[:, g * 4:(g + 1) * 4, tt * 128:(tt + 1) * 128],
                    ps[:, pr * 1024 + g * 512:pr * 1024 + (g + 1) * 512]
                    .rearrange("p (k t) -> p k t", k=4)),
                   wait=[(eT, None)], inc=(eS, 1))
            ev_ev[pr] = eS.n
        xT_done = eS.n

        xs_pad = [self.p16.alloc(T + 3) for _ in range(2)]
        xss = [self.p16.alloc(T) for _ in range(2)]
        xc = [self.p32.alloc(T) for _ in range(2)]
        bcacc = self.p32.alloc(T)
        wt = [self.p16.alloc(KD * 128) for _ in range(2)]
        zst = [self.p16.alloc(512) for _ in range(2)]
        wt_use = [0, 0]
        ip_ev = [0, 0]
        xss_use = [0, 0]
        silu_done = [0, 0]
        conv_done = [0, 0]
        xs_spill = [0, 0]
        zspill = [0, 0]
        dt_ev = [0, 0]
        bc_ev = [0, 0]
        ev(lambda e: e.memset(self.dtr_row[:], 0.0), inc=(eV, 1))

        def xp_block(dch):
            pr = dch % 2
            for fc in range(NFC):
                bpr = (dch * NFC + fc) % 2
                ep(lambda e, pr=pr, dch=dch, fc=fc, bpr=bpr: e.matmul(
                    ps[:, 2048 + bpr * 512:2048 + (bpr + 1) * 512],
                    self.xpw[:, dch, 0:128], xss[pr][:, fc * 512:(fc + 1) * 512],
                    start=True, stop=True),
                    wait=[(eS, silu_done[pr]), (eV, bc_ev[bpr])], inc=(eT, 1))
                if dch == 0:
                    ev(lambda e, fc=fc, bpr=bpr: e.tensor_copy(
                        bcacc[:, fc * 512:(fc + 1) * 512],
                        ps[:, 2048 + bpr * 512:2048 + (bpr + 1) * 512]),
                       wait=[(eT, None)], drain=True, inc=(eV, 1))
                else:
                    ev(lambda e, fc=fc, bpr=bpr: e.scalar_tensor_tensor(
                        bcacc[:, fc * 512:(fc + 1) * 512],
                        ps[:, 2048 + bpr * 512:2048 + (bpr + 1) * 512], 1.0,
                        bcacc[:, fc * 512:(fc + 1) * 512], ALU.mult, ALU.add),
                       wait=[(eT, None)], drain=True, inc=(eV, 1))
                bc_ev[bpr] = eV.n
                dpr = (dch * NFC + fc) % 2
                ep(lambda e, pr=pr, dch=dch, fc=fc, dpr=dpr: e.matmul(
                    ps[0:1, 1024 + dpr * 512:1024 + dpr * 512 + 512],
                    self.xpw[:, dch, 128:129], xss[pr][:, fc * 512:(fc + 1) * 512],
                    start=True, stop=True),
                    wait=[(eV, dt_ev[dpr])], inc=(eT, 1))
                ev(lambda e, fc=fc, dpr=dpr: e.scalar_tensor_tensor(
                    self.dtr_row[0:1, fc * 512:(fc + 1) * 512],
                    ps[0:1, 1024 + dpr * 512:1024 + dpr * 512 + 512], 1.0,
                    self.dtr_row[0:1, fc * 512:(fc + 1) * 512], ALU.mult, ALU.add),
                   wait=[(eT, None)], drain=True, inc=(eV, 1))
                dt_ev[dpr] = eV.n
            xss_use[pr] = eT.n

        for dch in range(KI):
            pr = dch % 2
            tW = dma(wt[pr], self.ipw_d[dch],
                     wait=[(eT, wt_use[pr]), (qst, None)])
            ev(lambda e, pr=pr: e.memset(xs_pad[pr][:, 0:3], 0.0), inc=(eV, 1))
            pad_ev = eV.n
            for fc in range(NFC):
                fpr = fc % 2
                for kc in range(KD):
                    ep(lambda e, pr=pr, fpr=fpr, kc=kc, fc=fc: e.matmul(
                        ps[:, fpr * 512:(fpr + 1) * 512],
                        wt[pr][:, kc * 128:(kc + 1) * 128],
                        xT[:, kc, fc * 512:(fc + 1) * 512],
                        start=(kc == 0), stop=(kc == KD - 1)),
                        wait=[(qio, tW), (eS, ip_ev[fpr]), (eS, xT_done)],
                        inc=(eT, 1) if kc == KD - 1 else None)
                es(lambda e, pr=pr, fpr=fpr, fc=fc: e.copy(
                    xs_pad[pr][:, 3 + fc * 512:3 + (fc + 1) * 512],
                    ps[:, fpr * 512:(fpr + 1) * 512]),
                   wait=[(eT, None), (eV, conv_done[pr])], inc=(eS, 1))
                ip_ev[fpr] = eS.n
            wt_use[pr] = eT.n
            ev(lambda e, pr=pr, dch=dch: e.tensor_scalar(
                xc[pr], xs_pad[pr][:, 0:T], self.mcw[:, dch, 0:1], None, ALU.mult),
               wait=[(eS, None), (eV, pad_ev)], inc=(eV, 1))
            for k in range(1, 4):
                ev(lambda e, pr=pr, k=k, dch=dch: e.scalar_tensor_tensor(
                    xc[pr], xs_pad[pr][:, k:k + T], self.mcw[:, dch, k:k + 1],
                    xc[pr], ALU.mult, ALU.add), drain=True, inc=(eV, 1))
            es(lambda e, pr=pr, dch=dch: e.activation(
                xss[pr], xc[pr], AF.Silu, bias=self.mcb[:, dch:dch + 1], scale=1.0),
               wait=[(eV, None), (eT, xss_use[pr]), (qst, None)],
               inc=(eS, 1))
            silu_done[pr] = eS.n
            xs_spill[pr] = dma(self.xs_d[b, dch], xss[pr], wait=[(eS, None)])
            if dch > 0:
                xp_block(dch - 1)
        xp_block(KI - 1)
        bst = self.p16.alloc(512)
        cst = self.p16.alloc(512)
        for fc in range(NFC):
            es(lambda e, fc=fc: e.copy(bst[0:64, :],
                                       bcacc[0:64, fc * 512:(fc + 1) * 512]),
               wait=[(eV, None), (qst, None)], inc=(eS, 1))
            es(lambda e, fc=fc: e.copy(cst[64:128, :],
                                       bcacc[64:128, fc * 512:(fc + 1) * 512]),
               wait=[(qst, None)], inc=(eS, 1))
            t1 = dma(self.bmat_d[b, :, fc * 512:(fc + 1) * 512], bst[0:64, :],
                     wait=[(eS, None)])
            t2 = dma(self.cmat_d[b, :, fc * 512:(fc + 1) * 512], cst[64:128, :])
        zp_ev = [ip_ev[0], ip_ev[1]]
        for mz in range(KI):
            pr = mz % 2
            tW = dma(wt[pr], self.ipw_d[KI + mz], wait=[(eT, wt_use[pr])])
            for fc in range(NFC):
                fpr = fc % 2
                for kc in range(KD):
                    ep(lambda e, pr=pr, fpr=fpr, kc=kc, fc=fc: e.matmul(
                        ps[:, fpr * 512:(fpr + 1) * 512],
                        wt[pr][:, kc * 128:(kc + 1) * 128],
                        xT[:, kc, fc * 512:(fc + 1) * 512],
                        start=(kc == 0), stop=(kc == KD - 1)),
                        wait=[(qio, tW), (eS, zp_ev[fpr])],
                        inc=(eT, 1) if kc == KD - 1 else None)
                es(lambda e, fpr=fpr: e.copy(zst[fpr], ps[:, fpr * 512:(fpr + 1) * 512]),
                   wait=[(eT, None), (qst, None)], inc=(eS, 1))
                zspill[fpr] = dma(self.z_d[b, mz, :, fc * 512:(fc + 1) * 512],
                                  zst[fpr], wait=[(eS, None)])
                zp_ev[fpr] = eS.n
            wt_use[pr] = eT.n
        self.bc_tgt = self.qio.n
        self.p32.pop()
        self.p16.pop()

    def scan(self, b):
        c = self.c
        KI, T, NS, Tc = c.KI, c.T, c.NS, c.Tc
        NDH, DPH, GD, NDG, NG, NNG, HB = c.NDH, c.DPH, c.GD, c.NDG, c.NG, c.NNG, c.HB
        qio, qst = self.qio, self.qst
        eV, eS, eT, eG = self.eV, self.eS, self.eT, self.eG
        dma, ev, es, ep, eg = self.dma, self.ev, self.es, self.ep, self.eg
        ps = self.psum

        self.p32.push()
        self.p16.push()
        self.pf16.push()
        dtrep = self.p32.alloc(T)
        dtt = self.p32.alloc(GD * Tc).rearrange("p (d t) -> p d t", d=GD)
        spt = self.p32.alloc(Tc)
        yfull = self.p32.alloc(Tc)
        gz = self.p32.alloc(Tc)
        ST = self.p32.alloc(KI * NS)
        dtx = self.p16.alloc(GD * Tc).rearrange("p (d t) -> p d t", d=GD)
        xssl = self.p16.alloc(GD * Tc).rearrange("p (d t) -> p d t", d=GD)
        brep = self.p16.alloc(2 * NG * Tc).rearrange("p (r n t) -> p r n t", r=2, n=NG)
        crep = self.p16.alloc(2 * NG * Tc).rearrange("p (r n t) -> p r n t", r=2, n=NG)
        ubuf = self.p16.alloc(GD * Tc).rearrange("p (d t) -> p d t", d=GD)
        sbig = self.p16.alloc(GD * Tc).rearrange("p (d t) -> p d t", d=GD)
        scb = self.p16.alloc(2 * GD * Tc).rearrange("p (r d t) -> p r d t", r=2, d=GD)
        zsl = [self.p16.alloc(Tc) for _ in range(2)]
        ysl = [self.p16.alloc(Tc) for _ in range(2)]

        dma(self.row_d[0:1, :], self.dtr_row, wait=[(eV, None)])
        dma(dtrep, self.row_d[0:1, :].broadcast_to([128, T]), wait=[(qst, None)])
        dt_bcast = None

        bc_use = [0, 0]
        bc_useG = [0, 0]
        dab_use = [0, 0]
        scb_use = [0, 0]
        it = 0
        y_ev = 0
        ysl_dma = [0, 0]
        for tc in range(c.NTC):
            tcs = tc * Tc
            for dh in range(NDH):
                dch0 = dh * DPH
                for dg in range(NDG):
                    xs_t = []
                    for i in range(GD):
                        tx = dma(xssl[:, i, :],
                                 self.xs_d[b, dch0 + dg * GD + i, :, tcs:tcs + Tc],
                                 wait=[(eV, None), (qst, None)])
                        xs_t.append(tx)
                    for i in range(GD):
                        dch = dch0 + dg * GD + i
                        es(lambda e, i=i, dch=dch, tcs=tcs: e.activation(
                            spt, dtrep[:, tcs:tcs + Tc], AF.Abs,
                            scale=self.dtw[:, dch:dch + 1],
                            bias=self.dtb[:, dch:dch + 1]),
                           wait=[(qio, None), (eV, None)], drain=True, inc=(eS, 1))
                        es(lambda e: e.activation(spt, spt, AF.Exp, scale=-1.0),
                           drain=True, inc=(eS, 1))
                        es(lambda e: e.activation(spt, spt, AF.Ln, bias=1.0),
                           drain=True, inc=(eS, 1))
                        es(lambda e, i=i, dch=dch, tcs=tcs: e.activation(
                            dtt[:, i, :], dtrep[:, tcs:tcs + Tc], AF.Relu,
                            scale=self.dtw[:, dch:dch + 1],
                            bias=self.dtb[:, dch:dch + 1]),
                           drain=True, inc=(eS, 1))
                        ev(lambda e, i=i: e.tensor_add(dtt[:, i, :], dtt[:, i, :],
                                                       spt),
                           wait=[(eS, None)], drain=True, inc=(eV, 1))
                        eg(lambda e, i=i: e.tensor_mul(dtx[:, i, :], dtt[:, i, :],
                                                       xssl[:, i, :]),
                           wait=[(qio, xs_t[i]), (eV, None)], drain=True, inc=(eG, 1))
                    dt_built = eV.n
                    jds = list(range(GD))
                    for ng in range(NNG):
                        gpr = ng % 2
                        n0 = ng * NG
                        tB = dma(brep[:, gpr],
                                 self.bmat_d[b, n0:n0 + NG, tcs:tcs + Tc]
                                 .unsqueeze(0).broadcast_to([128, NG, Tc]),
                                 wait=[(eV, bc_use[gpr]), (eG, bc_useG[gpr]),
                                       (qst, None)])
                        tC = dma(crep[:, gpr],
                                 self.cmat_d[b, n0:n0 + NG, tcs:tcs + Tc]
                                 .unsqueeze(0).broadcast_to([128, NG, Tc]))
                        for ni in range(NG):
                            n = n0 + ni
                            pr = it % 2
                            for j in jds:
                                dch = dch0 + dg * GD + j
                                es(lambda e, pr=pr, j=j, dch=dch, n=n: e.activation(
                                    self.dab(pr, j), dtt[:, j, :], AF.Exp,
                                    scale=self.aneg[:, dch, n:n + 1]),
                                   wait=[(eV, dab_use[pr]), (eV, dt_built)],
                                   inc=(eS, 1) if j == jds[-1] else None)
                            ev(lambda e, gpr=gpr, ni=ni: e.tensor_mul(
                                ubuf[:, :, :],
                                dtx[:, :, :],
                                brep[:, gpr, ni, :].unsqueeze(1)
                                .broadcast_to([128, GD, Tc])),
                               wait=[(qio, tB), (eS, None), (eG, None)],
                               inc=(eV, 1))
                            for j in jds:
                                dch = dch0 + dg * GD + j
                                jj = j
                                if tc == 0:
                                    init = 0.0
                                else:
                                    init = ST[:, dch * NS + n:dch * NS + n + 1]
                                ev(lambda e, jj=jj, pr=pr, init=init: e.tensor_tensor_scan(
                                    sbig[:, jj, :], self.dab(pr, jj), ubuf[:, jj, :],
                                    init, ALU.mult, ALU.add),
                                   wait=[(eG, None)], drain=(jj == 0),
                                   inc=(eV, 1) if jj == GD - 1 else None)
                            dab_use[pr] = eV.n
                            ev(lambda e, pr=pr, gpr=gpr, ni=ni: e.tensor_mul(
                                scb[:, pr, :, :],
                                sbig[:, :, :],
                                crep[:, gpr, ni, :].unsqueeze(1)
                                .broadcast_to([128, GD, Tc])),
                               wait=[(qio, tC), (eT, scb_use[pr])], drain=True,
                               inc=(eV, 1))
                            if tc < c.NTC - 1:
                                eg(lambda e, dch0=dch0, dg=dg, n=n: e.tensor_copy(
                                    ST.rearrange("p (d n) -> p d n", n=NS)
                                    [:, dch0 + dg * GD:dch0 + (dg + 1) * GD, n:n + 1],
                                    sbig[:, :, Tc - 1:Tc]),
                                   wait=[(eV, None)], inc=(eG, 1))
                            sC_done = eV.n
                            first = (ng == 0 and ni == 0)
                            last = (ng == NNG - 1 and ni == NG - 1)
                            HBS = c.HBS
                            ybank = max(Tc, 512)
                            for j in range(GD):
                                for h in range(HB):
                                    ep(lambda e, j=j, h=h, pr=pr, first=first,
                                       last=last, HBS=HBS, ybank=ybank: e.matmul(
                                           ps[:, j * ybank + h * HBS:
                                              j * ybank + (h + 1) * HBS],
                                           self.idb[:],
                                           scb[:, pr, j, h * HBS:(h + 1) * HBS],
                                           start=first, stop=last),
                                       wait=[(eV, sC_done), (eV, y_ev)],
                                       inc=(eT, 1) if (j == GD - 1 and h == HB - 1)
                                       else None)
                            scb_use[pr] = eT.n
                            it += 1
                        bc_use[gpr] = eV.n
                        bc_useG[gpr] = eG.n
                    for j in range(GD):
                        dch = dch0 + dg * GD + j
                        zpr = j % 2
                        tz = dma(zsl[zpr], self.z_d[b, dch, :, tcs:tcs + Tc],
                                 wait=[(eS, None), (qst, None)])
                        ev(lambda e, j=j, dch=dch: e.scalar_tensor_tensor(
                            yfull, xssl[:, j, :],
                            self.dvec[:, dch:dch + 1],
                            ps[:, j * max(Tc, 512):j * max(Tc, 512) + Tc],
                            ALU.mult, ALU.add),
                           wait=[(eT, None)], drain=True, inc=(eV, 1))
                        es(lambda e, zpr=zpr: e.activation(gz, zsl[zpr], AF.Silu),
                           wait=[(qio, tz), (eV, None)], drain=True, inc=(eS, 1))
                        ev(lambda e, zpr=zpr: e.tensor_mul(ysl[zpr], yfull, gz),
                           wait=[(eS, None), (qst, None)], drain=True,
                           inc=(eV, 1))
                        ysl_dma[zpr] = dma(self.y_d[b, dch, :, tcs:tcs + Tc],
                                           ysl[zpr], wait=[(eV, None)])
                    y_ev = eV.n
        self.y_tgt = self.qio.n
        self.p32.pop()
        self.p16.pop()
        self.pf16.pop()

    def warm(self, n=24, wait=()):
        ps = self.psum
        for i in range(n):
            self.ep(lambda e, i=i, n=n: e.matmul(
                ps[:, 3968:4096], self.idb[:], self.idb[:],
                start=(i == 0), stop=(i == n - 1)),
                wait=([(self.eT, None)] + list(wait)) if i == 0 else (),
                inc=(self.eT, 1))

    def dab(self, pr, j):
        Tc, GD = self.c.Tc, self.c.GD
        off = (pr * GD + j) * Tc
        return self.pf16.t[:, off:off + Tc]

    def phase34(self, b):
        c = self.c
        KD, KI, KH, T, D, DH, Tc = c.KD, c.KI, c.KH, c.T, c.D, c.DH, c.Tc
        NFC, NTT, TT_C = c.NFC, c.NTT, c.TT_C
        qio, qst, qout = self.qio, self.qst, self.qout
        eV, eS, eT, eG = self.eV, self.eS, self.eT, self.eG
        dma, ev, es, ep, eg = self.dma, self.ev, self.es, self.ep, self.eg
        ps = self.psum
        smv = self.smv

        self.p32.push()
        self.p16.push()
        self.pf16.push()
        f16q = [self.pf16.alloc(D) for _ in range(2)]
        f16h = [self.pf16.alloc(T) for _ in range(2)]
        x1qT = self.p16.alloc(KD * T).rearrange("p (k t) -> p k t", k=KD)
        isrep = self.p16.alloc(T)
        r1b = [self.p32.alloc(D) for _ in range(2)]
        sqb = [self.p32.alloc(D) for _ in range(2)]
        xtk = [self.p32.alloc(D) for _ in range(2)]
        TG = min(8, TT_C)
        self.p32.push()
        r1big = self.p32.alloc(TG * D).rearrange("p (a t) -> p a t", a=TG)
        self.p16.push()
        Yb = self.p16.alloc(KI * Tc).rearrange("p (k t) -> p k t", k=KI)
        attn = self.p16.alloc(KD * Tc).rearrange("p (k t) -> p k t", k=KD)
        xqb = [self.p16.alloc(D) for _ in range(2)]
        wop = [self.p16.alloc(KI * 128) for _ in range(2)]

        def layer_norm(tile, pr, out, wait_extra=(), wait_out=()):
            ev(lambda e: e.reduce_sum(smv[:, 0:1], tile, axis=AX.X),
               wait=list(wait_extra), drain=True, inc=(eV, 1))
            es(lambda e, pr=pr: e.activation(sqb[pr], tile, AF.Square),
               wait=[(eV, None)], inc=(eS, 1))
            ev(lambda e, pr=pr: e.reduce_sum(smv[:, 1:2], sqb[pr], axis=AX.X),
               wait=[(eS, None)], drain=True, inc=(eV, 1))
            ev(lambda e: e.tensor_scalar(smv[:, 0:2], smv[:, 0:2], 1.0 / D, None,
                                         ALU.mult), drain=True, inc=(eV, 1))
            ev(lambda e: e.tensor_mul(smv[:, 2:3], smv[:, 0:1], smv[:, 0:1]),
               drain=True, inc=(eV, 1))
            ev(lambda e: e.tensor_sub(smv[:, 1:2], smv[:, 1:2], smv[:, 2:3]),
               drain=True, inc=(eV, 1))
            ev(lambda e: e.tensor_scalar(smv[:, 1:2], smv[:, 1:2], 1e-5, None,
                                         ALU.add), drain=True, inc=(eV, 1))
            ev(lambda e: e.reciprocal(smv[:, 2:3], smv[:, 1:2]), drain=True,
               inc=(eV, 1))
            es(lambda e: e.activation(smv[:, 2:3], smv[:, 2:3], AF.Sqrt),
               wait=[(eV, None)], drain=True, inc=(eS, 1))
            ev(lambda e: e.tensor_scalar(out, tile, smv[:, 0:1], smv[:, 2:3],
                                         ALU.subtract, ALU.mult),
               wait=[(eS, None)] + list(wait_out), drain=True, inc=(eV, 1))

        self.warm()
        wop_use = [0, 0]
        op_ev = [0, 0]
        ptok_use = [0, 0]
        xtk_use = [0, 0]
        pxq_ev = 0
        for tc in range(c.NTC):
            tcs = tc * Tc
            yt = 0
            for kc in range(KI):
                yt = dma(Yb[:, kc, :], self.y_d[b, kc, :, tcs:tcs + Tc],
                         wait=[(eT, None), (qst, None)])
            for m in range(KD):
                pr = m % 2
                tW = dma(wop[pr], self.opw_d[m], wait=[(eT, wop_use[pr])])
                for h in range(c.HB):
                    HBS = c.HBS
                    fpr = (m * c.HB + h) % 2
                    for kc in range(KI):
                        ep(lambda e, pr=pr, fpr=fpr, kc=kc, h=h, HBS=HBS: e.matmul(
                            ps[:, fpr * 512:fpr * 512 + HBS],
                            wop[pr][:, kc * 128:(kc + 1) * 128],
                            Yb[:, kc, h * HBS:(h + 1) * HBS],
                            start=(kc == 0), stop=(kc == KI - 1)),
                            wait=[(qio, max(tW, yt)), (eS, op_ev[fpr])],
                            inc=(eT, 1) if kc == KI - 1 else None)
                    es(lambda e, m=m, h=h, fpr=fpr, HBS=HBS: e.copy(
                        attn[:, m, h * HBS:(h + 1) * HBS],
                        ps[:, fpr * 512:fpr * 512 + HBS]),
                       wait=[(eT, None)], inc=(eS, 1))
                    op_ev[fpr] = eS.n
                wop_use[pr] = eT.n
            attn_done = eS.n
            TT = min(8, TT_C)
            for tg in range(0, TT_C, TT):
              for j in range(TT):
                tt = tg + j
                pr = tt % 2
                pt0 = 1024 + pr * 1024
                for m in range(KD):
                    ep(lambda e, m=m, tt=tt, pt0=pt0: e.transpose(
                        self.psum_b[:, 2 * pt0 + m * 128:2 * pt0 + (m + 1) * 128],
                        attn[:, m, tt * 128:(tt + 1) * 128], self.idb[:]),
                        wait=[(eS, attn_done), (eV, ptok_use[pr])],
                        inc=(eT, 1) if m == KD - 1 else None)
                tX = dma(xtk[pr], self.x_h[b, tcs + tt * 128:tcs + (tt + 1) * 128, :],
                         wait=[(eV, xtk_use[pr])])
                ev(lambda e, pr=pr, pt0=pt0, j=j: e.tensor_add(
                    r1big[:, j, :], self.psum_b[:, 2 * pt0:2 * pt0 + D], xtk[pr]),
                   wait=[(eT, None), (qio, tX), (qst, None)], drain=True,
                   inc=(eV, 1))
                ptok_use[pr] = eV.n
                xtk_use[pr] = eV.n
              smb = self.smb
              gs = tc * TT_C + tg
              ev(lambda e: e.reduce_sum(smb[:, 0:TT], r1big[:, :, :], axis=AX.X),
                 drain=True, inc=(eV, 1))
              for j in range(TT):
                  qr = j % 2
                  es(lambda e, qr=qr, j=j: e.activation(sqb[qr], r1big[:, j, :],
                                                        AF.Square),
                     wait=[(eV, None)], inc=(eS, 1))
                  ev(lambda e, qr=qr, j=j: e.reduce_sum(
                      smb[:, TT + j:TT + j + 1], sqb[qr], axis=AX.X),
                     wait=[(eS, None)], drain=True, inc=(eV, 1))
              ev(lambda e: e.tensor_scalar(smb[:, 0:2 * TT], smb[:, 0:2 * TT],
                                           1.0 / D, None, ALU.mult),
                 drain=True, inc=(eV, 1))
              ev(lambda e: e.tensor_mul(smb[:, 2 * TT:3 * TT], smb[:, 0:TT],
                                        smb[:, 0:TT]), drain=True, inc=(eV, 1))
              ev(lambda e: e.tensor_sub(smb[:, TT:2 * TT], smb[:, TT:2 * TT],
                                        smb[:, 2 * TT:3 * TT]), drain=True,
                 inc=(eV, 1))
              ev(lambda e: e.tensor_scalar(smb[:, TT:2 * TT], smb[:, TT:2 * TT],
                                           1e-5, None, ALU.add), drain=True,
                 inc=(eV, 1))
              ev(lambda e: e.reciprocal(smb[:, 2 * TT:3 * TT], smb[:, TT:2 * TT]),
                 drain=True, inc=(eV, 1))
              es(lambda e: e.activation(smb[:, 2 * TT:3 * TT],
                                        smb[:, 2 * TT:3 * TT], AF.Sqrt),
                 wait=[(eV, None)], drain=True, inc=(eS, 1))
              for j in range(TT):
                  tt = tg + j
                  ev(lambda e, j=j: e.tensor_scalar(
                      r1big[:, j, :], r1big[:, j, :], smb[:, j:j + 1],
                      smb[:, 2 * TT + j:2 * TT + j + 1], ALU.subtract, ALU.mult),
                     wait=[(eS, None)], drain=True, inc=(eV, 1))
                  dma(self.x1_d[b, tcs + tt * 128:tcs + (tt + 1) * 128, :],
                      r1big[:, j, :], wait=[(eV, None)])
              ev(lambda e: e.reduce_max(smb[:, 3 * TT:4 * TT], r1big[:, :, :],
                                        axis=AX.X, apply_absolute_value=True),
                 drain=True, inc=(eV, 1))
              ev(lambda e: e.tensor_scalar(smb[:, 3 * TT:4 * TT],
                                           smb[:, 3 * TT:4 * TT], 1e-5, None,
                                           ALU.max), drain=True, inc=(eV, 1))
              ev(lambda e: e.reciprocal(smb[:, 4 * TT:5 * TT],
                                        smb[:, 3 * TT:4 * TT]),
                 drain=True, inc=(eV, 1))
              ev(lambda e: e.tensor_scalar(smb[:, 4 * TT:5 * TT],
                                           smb[:, 4 * TT:5 * TT], 127.0, None,
                                           ALU.mult), drain=True, inc=(eV, 1))
              ev(lambda e, gs=gs, TT=TT: e.tensor_mul(
                  self.itok[:, gs:gs + TT], smb[:, 3 * TT:4 * TT],
                  self.iw1[:].broadcast_to([128, TT])), drain=True, inc=(eV, 1))
              for j in range(TT):
                  pr = j % 2
                  gtt = gs + j
                  ev(lambda e, pr=pr, j=j: e.tensor_scalar(
                      f16q[pr], r1big[:, j, :],
                      smb[:, 4 * TT + j:4 * TT + j + 1],
                      None, ALU.mult), drain=True, inc=(eV, 1))
                  ev(lambda e, pr=pr: e.tensor_scalar(f16q[pr], f16q[pr],
                                                      MAGIC16, None, ALU.add),
                     drain=True, inc=(eV, 1))
                  ev(lambda e, pr=pr: e.tensor_scalar(xqb[pr], f16q[pr],
                                                      MAGIC16, None, ALU.subtract),
                     drain=True, inc=(eV, 1))
                  for kc in range(KD):
                      ep(lambda e, pr=pr, kc=kc: e.transpose(
                          self.psum_b[:, 6144 + kc * 128:6144 + (kc + 1) * 128],
                          xqb[pr][:, kc * 128:(kc + 1) * 128], self.idb[:]),
                          wait=[(eV, None), (eS, pxq_ev)],
                          inc=(eT, 1) if kc == KD - 1 else None)
                  es(lambda e, gtt=gtt: e.copy(
                      x1qT[:, :, gtt * 128:(gtt + 1) * 128],
                      self.psum_b[:, 6144:6144 + D]
                      .rearrange("p (k t) -> p k t", k=KD)),
                     wait=[(eT, None)], inc=(eS, 1))
                  pxq_ev = eS.n
        dma(self.itok_d, self.itok[:, 0:NTT], wait=[(eV, None)])
        dma(self.is_row[0:1, :],
            self.itok_d.transpose([1, 0]).unsqueeze(0),
            wait=[(qst, None)], slow=True)
        ev(lambda e: e.tensor_copy(self.is_rowb, self.is_row),
           wait=[(qio, None)], drain=True, inc=(eV, 1))
        dma(self.rowb_d[0:1, :], self.is_rowb, wait=[(eV, None)])
        dma(isrep, self.rowb_d[0:1, :].broadcast_to([128, T]), wait=[(qst, None)])
        is_done = None

        self.p32.pop()
        self.p16.pop()
        hst = [self.p16.alloc(T) for _ in range(2)]
        w1t = [self.p16.alloc(KD * 128) for _ in range(2)]
        h2t = [self.p16.alloc(T) for _ in range(2)]
        sc2rep = self.p16.alloc(T)
        iv2rep = self.p16.alloc(T)
        w2t = [self.p16.alloc(KD * 128) for _ in range(2)]
        h2s = [self.p16.alloc(512) for _ in range(2)]
        outb = [self.p32.alloc(D) for _ in range(2)]
        self.p32.push()
        hpad = [self.p32.alloc(T + 2) for _ in range(2)]
        hcv = [self.p32.alloc(T) for _ in range(2)]

        gtok = self.p32.alloc(c.NTT)
        tmax = self.p32.alloc(c.NTT)
        ev(lambda e: e.memset(gtok, 0.0), inc=(eV, 1))
        x1qT_done = pxq_ev
        self.warm(wait=[(self.eS, x1qT_done)])
        gm_ev = 0
        w1_use = [0, 0]
        h_ev = [0, 0]
        hc_use = [0, 0]
        hs_use = [0, 0]
        for mh in range(KH):
            pr = mh % 2
            tW = dma(w1t[pr], self.w1t_d[mh], wait=[(eT, w1_use[pr])])
            ev(lambda e, pr=pr: e.memset(hpad[pr][:, 0:1], 0.0),
               wait=[(eV, hc_use[pr])], inc=(eV, 1))
            ev(lambda e, pr=pr: e.memset(hpad[pr][:, T + 1:T + 2], 0.0), inc=(eV, 1))
            pad_ev = eV.n
            for fc in range(NFC):
                fpr = fc % 2
                for kc in range(KD):
                    ep(lambda e, pr=pr, fpr=fpr, kc=kc, fc=fc: e.matmul(
                        ps[:, fpr * 512:(fpr + 1) * 512],
                        w1t[pr][:, kc * 128:(kc + 1) * 128],
                        x1qT[:, kc, fc * 512:(fc + 1) * 512],
                        start=(kc == 0), stop=(kc == KD - 1)),
                        wait=[(qio, tW), (eV, h_ev[fpr]), (eS, x1qT_done)],
                        inc=(eT, 1) if kc == KD - 1 else None)
                ev(lambda e, pr=pr, fc=fc, fpr=fpr: e.tensor_mul(
                    hpad[pr][:, 1 + fc * 512:1 + (fc + 1) * 512],
                    ps[:, fpr * 512:(fpr + 1) * 512],
                    isrep[:, fc * 512:(fc + 1) * 512]),
                   wait=[(eT, None), (qio, None), (eV, pad_ev)], drain=True,
                   inc=(eV, 1))
                h_ev[fpr] = eV.n
            w1_use[pr] = eT.n
            ev(lambda e, pr=pr, mh=mh: e.tensor_scalar(
                hcv[pr], hpad[pr][:, 0:T], self.fcw[:, mh, 0:1], None, ALU.mult),
               drain=True, inc=(eV, 1))
            for k in range(1, 3):
                ev(lambda e, pr=pr, mh=mh, k=k: e.scalar_tensor_tensor(
                    hcv[pr], hpad[pr][:, k:k + T], self.fcw[:, mh, k:k + 1],
                    hcv[pr], ALU.mult, ALU.add), drain=True, inc=(eV, 1))
            es(lambda e, pr=pr, mh=mh: e.activation(
                hst[pr], hcv[pr], AF.Silu, bias=self.fcb[:, mh:mh + 1], scale=1.0),
               wait=[(eV, None), (qst, None)], drain=True, inc=(eS, 1))
            hc_use[pr] = eS.n
            dma(self.hs_d[b, mh], hst[pr], wait=[(eS, None)])
            hs_use[pr] = self.qio.n
            for tt in range(c.NTT):
                ep(lambda e, pr=pr, tt=tt: e.transpose(
                    self.psum_b[:, 4096 + tt * 128:4096 + (tt + 1) * 128],
                    hst[pr][:, tt * 128:(tt + 1) * 128], self.idb[:]),
                    wait=[(eS, None), (eV, gm_ev)],
                    inc=(eT, 1) if tt == c.NTT - 1 else None)
            ev(lambda e: e.reduce_max(
                tmax, self.psum_b[:, 4096:4096 + c.NTT * 128]
                .rearrange("p (a b) -> p a b", a=c.NTT), axis=AX.X,
                apply_absolute_value=True),
               wait=[(eT, None)], drain=True, inc=(eV, 1))
            ev(lambda e: e.tensor_max(gtok, gtok, tmax), drain=True, inc=(eV, 1))
            gm_ev = eV.n
        self.p32.pop()
        self.p32.push()
        qbf = [self.p32.alloc(T) for _ in range(2)]
        sc2f = self.p32.alloc(T)[0:1, :]
        dma(self.gtok_d, gtok, wait=[(eV, None)])
        dma(self.gm_row, self.gtok_d.transpose([1, 0]).unsqueeze(0),
            wait=[(qst, None)], slow=True)
        ev(lambda e: e.tensor_scalar(self.gm_row, self.gm_row, 1e-5, None,
                                     ALU.max), wait=[(qio, None)], drain=True,
           inc=(eV, 1))
        ev(lambda e: e.reciprocal(sc2f, self.gm_row), drain=True,
           inc=(eV, 1))
        ev(lambda e: e.tensor_scalar(sc2f, sc2f, 127.0, None,
                                     ALU.mult), drain=True, inc=(eV, 1))
        ev(lambda e: e.tensor_scalar(self.iv2_rowb, self.gm_row,
                                     self.iw2[0:1, 0:1], None, ALU.mult),
           drain=True, inc=(eV, 1))
        ev(lambda e: e.tensor_copy(self.sc2_rowb, sc2f),
           drain=True, inc=(eV, 1))
        dma(self.rowb_d[1:2, :], self.sc2_rowb, wait=[(eV, None)])
        dma(self.rowb_d[2:3, :], self.iv2_rowb)
        dma(sc2rep, self.rowb_d[1:2, :].broadcast_to([128, T]), wait=[(qst, None)])
        dma(iv2rep, self.rowb_d[2:3, :].broadcast_to([128, T]))
        q2_done = None
        h2_use = [0, 0]
        for kh in range(KH):
            pr = kh % 2
            th = dma(h2t[pr], self.hs_d[b, kh], wait=[(eV, h2_use[pr]), (qst, None)])
            ev(lambda e, pr=pr: e.tensor_mul(qbf[pr], h2t[pr], sc2rep),
               wait=[(qio, th)], drain=True, inc=(eV, 1))
            ev(lambda e, pr=pr: e.tensor_scalar(f16h[pr], qbf[pr],
                                                MAGIC16, None, ALU.add),
               drain=True, inc=(eV, 1))
            ev(lambda e, pr=pr: e.tensor_scalar(h2t[pr], f16h[pr],
                                                MAGIC16, None, ALU.subtract),
               drain=True, inc=(eV, 1))
            h2_use[pr] = eV.n
            dma(self.h2q_d[b, kh], h2t[pr], wait=[(eV, None)])
        h2q_tgt = self.qio.n
        self.p32.pop()
        self.p32.push()
        ffn_f = self.p32.alloc(KD * 512).rearrange("p (m t) -> p m t", m=KD)
        ffn_ev = 0
        fc_clear = 0
        pt_use = [0, 0]
        w2_use = [0, 0]
        self.warm(wait=[(self.eV, None)])
        for fc in range(NFC):
            for kh in range(KH):
                pr = kh % 2
                th = dma(h2s[pr], self.h2q_d[b, kh, :, fc * 512:(fc + 1) * 512],
                         wait=[(eT, w2_use[pr]), (qst, None)])
                tW = dma(w2t[pr], self.w2t_d[kh])
                for m in range(KD):
                    wl = [(qio, max(th, tW)), (eV, fc_clear)]
                    if fc == 0 and kh == 0 and m == KD - 1:
                        wl.append((eT, None))
                    ep(lambda e, pr=pr, m=m, kh=kh: e.matmul(
                        ps[:, m * 512:(m + 1) * 512],
                        w2t[pr][:, m * 128:(m + 1) * 128], h2s[pr],
                        start=(kh == 0), stop=(kh == KH - 1)),
                        wait=wl,
                        inc=(eT, 1) if m == KD - 1 else None)
                w2_use[pr] = eT.n
            for m in range(KD):
                ev(lambda e, m=m, fc=fc: e.tensor_mul(
                    ffn_f[:, m, :], ps[:, m * 512:(m + 1) * 512],
                    iv2rep[:, fc * 512:(fc + 1) * 512]),
                   wait=[(eT, None)], drain=True, inc=(eV, 1))
            ffn_ev = eV.n
            for tt in range(4):
                pr = tt % 2
                rows = fc * 512 + tt * 128
                for m in range(KD):
                    ep(lambda e, m=m, tt=tt, pr=pr: e.transpose(
                        ps[:, pr * 1024 + m * 128:pr * 1024 + (m + 1) * 128],
                        ffn_f[:, m, tt * 128:(tt + 1) * 128], self.idf[:]),
                        wait=[(eV, ffn_ev), (eV, pt_use[pr])],
                        inc=(eT, 1) if m == KD - 1 else None)
                tX = dma(xtk[pr], self.x1_d[b, rows:rows + 128, :],
                         wait=[(eV, xtk_use[pr]), (qst, None)])
                ev(lambda e, pr=pr: e.tensor_add(r1b[pr], ps[:, pr * 1024:pr * 1024 + D],
                                                 xtk[pr]),
                   wait=[(eT, None), (qio, tX)], drain=True, inc=(eV, 1))
                pt_use[pr] = eV.n
                xtk_use[pr] = eV.n
                layer_norm(r1b[pr], pr, outb[pr], wait_out=[(qout, None)])
                dma(self.out_h[b, rows:rows + 128, :], outb[pr],
                    wait=[(eV, None)], sem=qout)
            fc_clear = eV.n
        self.p32.pop()
        self.p32.pop()
        self.p16.pop()
        self.pf16.pop()


def build_program(cfg, dbg=False):
    return Builder(cfg).build(dbg=dbg)


def kernel(**inputs):
    import ml_dtypes
    cfg = Cfg(small=False)
    x = np.ascontiguousarray(np.asarray(inputs["L0_attn_x"], dtype=np.float32))
    n_cores = 8
    shards = np.split(x, n_cores, axis=0)
    base = {k: np.ascontiguousarray(np.asarray(v, dtype=np.float32))
            for k, v in inputs.items() if k != "L0_attn_x"}
    ident_f = np.eye(128, dtype=np.float32)
    ident_b = np.eye(128, dtype=ml_dtypes.bfloat16)
    nc = build_program(cfg)
    in_maps = []
    for i in range(n_cores):
        m = dict(base)
        m["L0_attn_x"] = shards[i]
        m["ident_f32"] = ident_f
        m["ident_bf16"] = ident_b
        in_maps.append(m)
    res = run_bass_kernel_spmd(nc, in_maps, core_ids=list(range(n_cores)))
    return np.concatenate([r["out"] for r in res.results], axis=0)


# version 23
# speedup vs baseline: 1.0259x; 1.0218x over previous
import numpy as np

import concourse.bass as bass
import concourse.mybir as mybir
import concourse.library_config as libcfg
from concourse.bass_isa_rust import ReduceOp
from concourse.bass_utils import run_bass_kernel_spmd

F32 = mybir.dt.float32
BF16 = mybir.dt.bfloat16
F16 = mybir.dt.float16
ALU = mybir.AluOpType
AF = mybir.ActivationFunctionType
AX = mybir.AxisListType

MAGIC = float(np.float32(3 * 2 ** 22))
MAGIC16 = 1536.0


class Cfg:
    def __init__(self, small=False):
        if small:
            self.B, self.T, self.D, self.DI, self.DH, self.Tc = 1, 512, 512, 512, 1024, 256
        else:
            self.B, self.T, self.D, self.DI, self.DH, self.Tc = 2, 2048, 1024, 2048, 4096, 1024
        self.NS = 64
        self.KD = self.D // 128
        self.KI = self.DI // 128
        self.KH = self.DH // 128
        self.NTC = self.T // self.Tc
        self.NFC = self.T // 512
        self.NTT = self.T // 128
        self.TT_C = self.Tc // 128
        self.NDH = 2
        self.DPH = self.KI // self.NDH
        self.GD = min(4, self.DPH)
        self.NDG = self.DPH // self.GD
        self.NG = 4
        self.NNG = self.NS // self.NG
        self.HBS = min(512, self.Tc)
        self.HB = self.Tc // self.HBS
        assert self.GD * self.Tc * 4 <= 8 * 2048


ENGINES = ("pe", "ve", "se", "ge", "sy")


class Sem:
    __slots__ = ("h", "n", "name", "dma", "last_waited", "boundary")

    def __init__(self, h, name, dma=False):
        self.h, self.n, self.name, self.dma = h, 0, name, dma
        self.last_waited = 0
        self.boundary = 0

    def bump(self, k):
        self.n += k
        return self.n


class Pool:
    def __init__(self, t, elems):
        self.t, self.elems, self.off, self.marks = t, elems, 0, []

    def alloc(self, n):
        assert self.off + n <= self.elems, f"pool overflow: {self.off}+{n}>{self.elems}"
        o = self.off
        self.off += n
        return self.t[:, o:o + n]

    def push(self):
        self.marks.append(self.off)

    def pop(self):
        self.off = self.marks.pop()


class Builder:
    def __init__(self, cfg):
        self.c = cfg
        self.nc = bass.Bass("TRN2", target_bir_lowering=False)
        self.ops = {e: [] for e in ENGINES}
        self.lastwait = {e: {} for e in ENGINES}
        self._guards = []
        self.dma_sems = []

    def emit(self, e, fn, wait=(), drain=False, inc=None):
        if drain:
            sem = getattr(self, "engine_sem", {}).get(e)
            if sem is not None:
                wait = list(wait) + [(sem, sem.n)]
                drain = False
        wl = []
        for sem, cnt in wait:
            if cnt is None or sem.dma:
                cnt = sem.n
            if sem.dma:
                sem.last_waited = max(sem.last_waited, cnt)
            if cnt > self.lastwait[e].get(id(sem), -1):
                wl.append((sem, cnt))
                self.lastwait[e][id(sem)] = cnt
        if e == "sy" and wl:
            for s in self.dma_sems:
                s.boundary = s.n
        if inc is not None:
            inc[0].bump(inc[1])

        def op(eng, wl=wl, drain=drain, fn=fn, inc=inc):
            for sem, cnt in wl:
                eng.wait_ge(sem.h, cnt)
            if drain:
                eng.drain()
            r = fn(eng)
            if inc is not None:
                r.then_inc(inc[0].h, inc[1])

        self.ops[e].append(op)

    def barrier(self):
        b = self.bar
        tgt = b.n + len(ENGINES)
        b.n = tgt
        dwaits = [(s.h, s.n) for s in self.dma_sems if s.n > 0]
        for s in self.dma_sems:
            s.last_waited = max(s.last_waited, s.n)
            s.boundary = s.n
        for e in ENGINES:
            def op(eng, tgt=tgt, b=b, dwaits=dwaits):
                for h, n in dwaits:
                    eng.wait_ge(h, n)
                eng.drain()
                eng.sem_inc(b.h, 1)
                eng.wait_ge(b.h, tgt)
            self.ops[e].append(op)
            self.lastwait[e] = {}

    def dma(self, dst, src, wait=(), sem=None, slow=False):
        if sem is None:
            try:
                is_store = "DRAM" in str(dst.space)
            except Exception:
                is_store = False
            sem = self.qst if is_store else self.qio
        wait = list(wait)
        if sem.last_waited > sem.boundary:
            wait.append((sem, sem.last_waited))
            self.lastwait["sy"].pop(id(sem), None)
        self.emit("sy", lambda e, d=dst, s=src, sl=slow: e.dma_start(
            out=d, in_=s, allow_slow_non_contiguous=sl),
                  wait=wait, inc=(sem, 16))
        return sem.n

    def ev(self, fn, **kw):
        self.emit("ve", fn, **kw)

    def es(self, fn, **kw):
        self.emit("se", fn, **kw)

    def ep(self, fn, **kw):
        self.emit("pe", fn, **kw)

    def eg(self, fn, **kw):
        self.emit("ge", fn, **kw)

    def sb(self, name, shape, dtype):
        g = self.nc.sbuf_tensor(name, list(shape), dtype)
        t = g.__enter__()
        self._guards.append(g)
        return t

    def build(self, dbg=False):
        nc, c = self.nc, self.c
        B, T, D, DI, DH, NS = c.B, c.T, c.D, c.DI, c.DH, c.NS
        KD, KI, KH, Tc = c.KD, c.KI, c.KH, c.Tc

        din = lambda n, s: nc.dram_tensor(n, list(s), F32, kind="ExternalInput").ap()
        self.x_h = din("L0_attn_x", (B, T, D))
        self.ipw_h = din("L0_attn_in_proj_w", (D, 2 * DI))
        self.mcw_h = din("L0_attn_conv_w", (DI, 1, 4))
        self.mcb_h = din("L0_attn_conv_b", (DI,))
        self.xpw_h = din("L0_attn_x_proj_w", (DI, 2 * NS + 1))
        self.dtw_h = din("L0_attn_dt_w", (1, DI))
        self.dtb_h = din("L0_attn_dt_b", (DI,))
        self.alog_h = din("L0_attn_A_log", (DI, NS))
        self.dvec_h = din("L0_attn_D", (DI,))
        self.opw_h = din("L0_attn_out_proj_w", (DI, D))
        self.w1_h = din("L0_ffn_w1", (DH, D))
        self.fcw_h = din("L0_ffn_conv_w", (DH, 1, 3))
        self.fcb_h = din("L0_ffn_conv_b", (DH,))
        self.w2_h = din("L0_ffn_w2", (D, DH))
        self.idf_h = din("ident_f32", (128, 128))
        self.idb_h = nc.dram_tensor("ident_bf16", [128, 128], BF16,
                                    kind="ExternalInput").ap()
        self.out_h = nc.dram_tensor("out", [B, T, D], F32, kind="ExternalOutput").ap()

        scr = lambda n, s, d: nc.dram_tensor(n, list(s), d, kind="Internal").ap()
        self.ipw_d = scr("ipw_bf", (2 * KI, 128, KD * 128), BF16)
        self.opw_d = scr("opw_bf", (KD, 128, KI * 128), BF16)
        self.w1t_d = scr("w1qT", (KH, 128, KD * 128), BF16)
        self.w2t_d = scr("w2qT", (KH, 128, KD * 128), BF16)
        self.z_d = scr("z_dram", (B, KI, 128, T), BF16)
        self.xs_d = scr("xs_dram", (B, KI, 128, T), BF16)
        self.bmat_d = scr("bmat_dram", (B, NS, T), BF16)
        self.cmat_d = scr("cmat_dram", (B, NS, T), BF16)
        self.y_d = scr("y_dram", (B, KI, 128, T), BF16)
        self.x1_d = scr("x1_dram", (B, T, D), F32)
        self.hs_d = scr("hs_dram", (B, KH, 128, T), BF16)
        self.h2q_d = scr("h2q_dram", (B, KH, 128, T), BF16)
        self.itok_d = scr("itok_dram", (128, c.NTT), F32)
        self.gtok_d = scr("gtok_dram", (128, c.NTT), F32)
        self.row_d = scr("row_dram", (2, T), F32)
        self.rowb_d = scr("rowb_dram", (4, T), BF16)
        self.scl_d = scr("scl_dram", (2, 2), F32)

        self.idf = self.sb("idf", (128, 128), F32)
        self.idb = self.sb("idb", (128, 128), BF16)
        self.mcw = self.sb("mcw", (128, KI, 4), F32)
        self.mcb = self.sb("mcb", (128, KI), F32)
        self.dtw = self.sb("dtw", (128, KI), F32)
        self.dtb = self.sb("dtb", (128, KI), F32)
        self.dvec = self.sb("dvec", (128, KI), F32)
        self.aneg = self.sb("aneg", (128, KI, NS), F32)
        self.fcw = self.sb("fcw", (128, KH, 3), F32)
        self.fcb = self.sb("fcb", (128, KH), F32)
        self.xpw = self.sb("xpw", (128, KI, 144), BF16)
        self.sw1 = self.sb("sw1", (128, 1), F32)
        self.iw1 = self.sb("iw1", (128, 1), F32)
        self.sw2 = self.sb("sw2", (128, 1), F32)
        self.iw2 = self.sb("iw2", (128, 1), F32)
        self.rowbuf = self.sb("rowbuf", (128, T), F32)
        self.gm_row = self.rowbuf[0:1, 0:T]
        self.dtr_row = self.rowbuf[32:33, 0:T]
        self.is_row = self.rowbuf[64:65, 0:T]
        self.rowb16 = self.sb("rowb16", (128, 2 * T), BF16)
        self.sc2_rowb = self.rowb16[0:1, 0:T]
        self.iv2_rowb = self.rowb16[0:1, T:2 * T]
        self.is_rowb = self.rowb16[64:65, 0:T]
        self.smv = self.sb("smv", (128, 16), F32)
        self.smb = self.sb("smb", (128, 5 * c.TT_C), F32)
        self.itok = self.sb("itok", (128, c.NTT), F32)

        self.p32 = Pool(self.sb("p32", (128, 16640), F32), 16640)
        self.p16 = Pool(self.sb("p16", (128, 49408), BF16), 49408)
        self.pf16 = Pool(self.sb("pf16", (128, 8192), F16), 8192)
        pg = self.nc.psum_tensor("psum", [128, 4096], F32)
        self.psum = pg.__enter__()
        self._guards.append(pg)
        self.psum_b = self.psum[:, :].bitcast(BF16)

        self.sems = {}
        for name in ("bar", "qio", "qst", "qout", "eV", "eS", "eT", "eG"):
            g = nc.semaphore(name)
            h = g.__enter__()
            self._guards.append(g)
            self.sems[name] = Sem(h, name, dma=name.startswith("q"))
        self.bar = self.sems["bar"]
        self.qio = self.sems["qio"]
        self.qst = self.sems["qst"]
        self.qout = self.sems["qout"]
        self.dma_sems = [self.sems[n] for n in ("qio", "qst", "qout")]
        self.eV = self.sems["eV"]
        self.eS = self.sems["eS"]
        self.eT = self.sems["eT"]
        self.eG = self.sems["eG"]
        self.engine_sem = {"ve": self.eV, "se": self.eS, "pe": self.eT,
                           "ge": self.eG}

        self.phase0()
        self.barrier()
        for b in range(B):
            self.phase1(b)
            self.barrier()
            self.scan(b)
            self.barrier()
            self.phase34(b)
            self.barrier()
        if dbg:
            for name, t in (("d_xs", self.xs_d), ("d_z", self.z_d),
                            ("d_bm", self.bmat_d), ("d_cm", self.cmat_d),
                            ("d_y", self.y_d), ("d_x1", self.x1_d),
                            ("d_hs", self.hs_d), ("d_h2q", self.h2q_d)):
                o = nc.dram_tensor(name, list(t.shape), t.dtype,
                                   kind="ExternalOutput").ap()
                self.dma(o, t, sem=self.qout)
        self.emit("sy", lambda e: e.nop(), wait=[(self.qout, None)])

        with nc.Block() as block:
            block.sync(lambda e: [op(e) for op in self.ops["sy"]])
            block.vector(lambda e: [op(e) for op in self.ops["ve"]])
            block.scalar(lambda e: [op(e) for op in self.ops["se"]])
            block.tensor(lambda e: [op(e) for op in self.ops["pe"]])
            block.gpsimd(lambda e: [op(e) for op in self.ops["ge"]])

        for g in reversed(self._guards):
            g.__exit__(None, None, None)
        return nc

    def phase0(self):
        c, KD, KI, KH = self.c, self.c.KD, self.c.KI, self.c.KH
        D, DH, NS = self.c.D, self.c.DH, self.c.NS
        qio, qst = self.qio, self.qst
        eV, eS, eT, eG = self.eV, self.eS, self.eT, self.eG
        dma, ev, es, ep, eg = self.dma, self.ev, self.es, self.ep, self.eg

        dma(self.idf[:], self.idf_h)
        dma(self.idb[:], self.idb_h)
        for kc in range(KI):
            dma(self.mcw[:, kc, :], self.mcw_h[kc * 128:(kc + 1) * 128, 0, :])
            dma(self.aneg[:, kc, :], self.alog_h[kc * 128:(kc + 1) * 128, :])
        dma(self.mcb[:], self.mcb_h.rearrange("(c p) -> p c", p=128), slow=True)
        dma(self.dtw[:], self.dtw_h[0, :].rearrange("(c p) -> p c", p=128), slow=True)
        dma(self.dtb[:], self.dtb_h.rearrange("(c p) -> p c", p=128), slow=True)
        dma(self.dvec[:], self.dvec_h.rearrange("(c p) -> p c", p=128), slow=True)
        for kc in range(KH):
            dma(self.fcw[:, kc, :], self.fcw_h[kc * 128:(kc + 1) * 128, 0, :])
        dma(self.fcb[:], self.fcb_h.rearrange("(c p) -> p c", p=128), slow=True)
        ld0 = qio.n

        es(lambda e: e.activation(self.aneg[:], self.aneg[:], AF.Exp),
           wait=[(qio, ld0)], inc=(eS, 1))
        ev(lambda e: e.tensor_scalar(self.aneg[:], self.aneg[:], -1.0, None, ALU.mult),
           wait=[(eS, None)], inc=(eV, 1))

        self.p32.push()
        stg = self.p32.alloc(2 * NS + 1)
        for kc in range(KI):
            t = dma(stg, self.xpw_h[kc * 128:(kc + 1) * 128, :], wait=[(eV, None)])
            ev(lambda e, kc=kc: e.tensor_copy(self.xpw[:, kc, 0:2 * NS + 1], stg),
               wait=[(qio, t)], drain=True, inc=(eV, 1))
        self.p32.pop()

        DI2 = 2 * self.c.DI
        self.p32.push()
        self.p16.push()
        wstg = [self.p32.alloc(max(DI2, DH)) for _ in range(2)]
        wbig = self.p16.alloc(KD * DI2).rearrange("p (k m) -> p k m", k=KD)
        mstg = [self.p16.alloc(KI * 128) for _ in range(2)]
        cast_ev = [eV.n, eV.n]
        for kc in range(KD):
            pr = kc % 2
            t = dma(wstg[pr][:, :DI2], self.ipw_h[kc * 128:(kc + 1) * 128, :],
                    wait=[(eV, cast_ev[pr])])
            ev(lambda e, pr=pr, kc=kc: e.tensor_copy(wbig[:, kc, :],
                                                     wstg[pr][:, :DI2]),
               wait=[(qio, t)], inc=(eV, 1))
            cast_ev[pr] = eV.n
        for m in range(2 * KI):
            pr = m % 2
            ev(lambda e, pr=pr, m=m: e.tensor_copy(
                mstg[pr][:, :KD * 128],
                wbig[:, :, m * 128:(m + 1) * 128]),
               wait=[(eV, None), (qst, None)], inc=(eV, 1))
            dma(self.ipw_d[m], mstg[pr][:, :KD * 128], wait=[(eV, None)])
        cast_ev = [eV.n, eV.n]
        for kc in range(KI):
            pr = kc % 2
            t = dma(wstg[pr][:, :D], self.opw_h[kc * 128:(kc + 1) * 128, :],
                    wait=[(eV, cast_ev[pr])])
            ev(lambda e, pr=pr, kc=kc: e.tensor_copy(
                wbig.rearrange("p k m -> p (k m)")[:, kc * D:(kc + 1) * D],
                wstg[pr][:, :D]),
               wait=[(qio, t), (qst, None)], inc=(eV, 1))
            cast_ev[pr] = eV.n
        obig = wbig.rearrange("p k m -> p (k m)")[:, :KI * D].rearrange(
            "p (k m) -> p k m", k=KI)
        for m in range(KD):
            pr = m % 2
            ev(lambda e, pr=pr, m=m: e.tensor_copy(
                mstg[pr], obig[:, :, m * 128:(m + 1) * 128]),
               wait=[(eV, None), (qst, None)], inc=(eV, 1))
            dma(self.opw_d[m], mstg[pr], wait=[(eV, None)])

        acc = self.p32.alloc(4)
        ev(lambda e: e.memset(acc[:, 0:2], 0.0), inc=(eV, 1))
        for src_, col, rows, cols in ((self.w1_h, 0, KH, D), (self.w2_h, 1, KD, DH)):
            for r in range(rows):
                pr = r % 2
                t = dma(wstg[pr][:, :cols], src_[r * 128:(r + 1) * 128, :],
                        wait=[(eV, None)])
                ev(lambda e, pr=pr, col=col, cols=cols: e.reduce_sum(
                    acc[:, 2 + col:3 + col], wstg[pr][:, :cols], axis=AX.X,
                    apply_absolute_value=True),
                   wait=[(qio, t)], drain=True, inc=(eV, 1))
                ev(lambda e, col=col: e.tensor_add(
                    acc[:, col:col + 1], acc[:, col:col + 1], acc[:, 2 + col:3 + col]),
                   drain=True, inc=(eV, 1))
        ep(lambda e: e.transpose(self.psum[0:2, 0:128], acc[:, 0:2], self.idf[:]),
           wait=[(eV, None), (eS, None)], inc=(eT, 1))
        ev(lambda e: e.reduce_sum(acc[0:2, 2:3], self.psum[0:2, 0:128], axis=AX.X),
           wait=[(eT, None)], drain=True, inc=(eV, 1))
        scale = 1.0 / (DH * D)
        ev(lambda e: e.tensor_scalar(acc[0:2, 2:3], acc[0:2, 2:3], scale, 1e-5,
                                     ALU.mult, ALU.max), drain=True, inc=(eV, 1))
        ev(lambda e: e.tensor_scalar(acc[0:2, 3:4], acc[0:2, 2:3], 1.0 / 127.0, None,
                                     ALU.mult), drain=True, inc=(eV, 1))
        ev(lambda e: e.reciprocal(acc[0:2, 0:1], acc[0:2, 2:3]), drain=True,
           inc=(eV, 1))
        dma(self.scl_d[0:1, :], acc[0:2, 0:1], wait=[(eV, None)], slow=True)
        dma(self.scl_d[1:2, :], acc[0:2, 3:4], wait=[(eV, None)], slow=True)
        dma(self.sw1[:], self.scl_d[0:1, 0:1].broadcast_to([128, 1]), slow=True,
            wait=[(qst, None)])
        dma(self.sw2[:], self.scl_d[0:1, 1:2].broadcast_to([128, 1]), slow=True)
        dma(self.iw1[:], self.scl_d[1:2, 0:1].broadcast_to([128, 1]), slow=True)
        dma(self.iw2[:], self.scl_d[1:2, 1:2].broadcast_to([128, 1]), slow=True)

        self.pf16.push()
        f16w = self.pf16.alloc(max(D, DH))
        qb = [self.p16.alloc(max(D, DH)) for _ in range(2)]
        rslab = [self.p16.alloc(KD * 128) for _ in range(2)]
        w2big = wbig.rearrange("p k m -> p (k m)")[:, :KH * KD * 128].rearrange(
            "p (h m q) -> p h m q", h=KH, m=KD)
        for src_, rows, cols, scol, w2 in ((self.w1_h, KH, D, self.sw1, False),
                                           (self.w2_h, KD, DH, self.sw2, True)):
            for r in range(rows):
                pr = r % 2
                t = dma(wstg[pr][:, :cols], src_[r * 128:(r + 1) * 128, :],
                        wait=[(eV, None)])
                ev(lambda e, pr=pr, cols=cols, scol=scol: e.tensor_scalar(
                    wstg[pr][:, :cols], wstg[pr][:, :cols], scol[:, 0:1], None,
                    ALU.mult), wait=[(qio, t), (eV, None), (eT, None)], drain=True,
                   inc=(eV, 1))
                ev(lambda e, pr=pr, cols=cols: e.tensor_scalar(
                    wstg[pr][:, :cols], wstg[pr][:, :cols], 1.49, -1.49,
                    ALU.min, ALU.max), drain=True, inc=(eV, 1))
                ev(lambda e, pr=pr, cols=cols: e.tensor_scalar(
                    f16w[:, :cols], wstg[pr][:, :cols], MAGIC16, None,
                    ALU.add), drain=True, inc=(eV, 1))
                ev(lambda e, pr=pr, cols=cols: e.tensor_scalar(
                    qb[pr][:, :cols], f16w[:, :cols], MAGIC16, None,
                    ALU.subtract), drain=True, inc=(eV, 1))
                nblk = cols // 128
                for g0 in range(0, nblk, 4):
                    gn = min(4, nblk - g0)
                    for j in range(gn):
                        kc = g0 + j
                        self.ep(lambda e, pr=pr, kc=kc, j=j: e.transpose(
                            self.psum_b[:, j * 128:(j + 1) * 128],
                            qb[pr][:, kc * 128:(kc + 1) * 128], self.idb[:]),
                            wait=[(eV, None), (eS, None)],
                            inc=(eT, 1) if j == gn - 1 else None)
                    if w2:
                        es(lambda e, r=r, g0=g0, gn=gn: e.copy(
                            w2big[:, g0:g0 + gn, r, :],
                            self.psum_b[:, :gn * 128].rearrange(
                                "p (a q) -> p a q", a=gn)),
                           wait=[(eT, None)], inc=(eS, 1))
                    else:
                        es(lambda e, pr=pr, g0=g0, gn=gn: e.copy(
                            rslab[pr][:, g0 * 128:(g0 + gn) * 128],
                            self.psum_b[:, :gn * 128]),
                           wait=[(eT, None), (qst, None)], inc=(eS, 1))
                if not w2:
                    dma(self.w1t_d[r], rslab[pr], wait=[(eS, None)])
        for kh in range(KH):
            dma(self.w2t_d[kh],
                w2big.rearrange("p h m q -> p (h m q)")
                [:, kh * KD * 128:(kh + 1) * KD * 128],
                wait=[(eS, None)])
        self.p32.pop()
        self.p16.pop()
        self.pf16.pop()
        self.w_prep_tgt = self.qio.n

    def phase1(self, b):
        c = self.c
        KD, KI, T, D, NS, NFC, NTT = c.KD, c.KI, c.T, c.D, c.NS, c.NFC, c.NTT
        qio, qst = self.qio, self.qst
        eV, eS, eT, eG = self.eV, self.eS, self.eT, self.eG
        dma, ev, es, ep, eg = self.dma, self.ev, self.es, self.ep, self.eg
        ps = self.psum

        self.p32.push()
        self.p16.push()
        xT = self.p16.alloc(KD * T).rearrange("p (k t) -> p k t", k=KD)
        xtok = [self.p32.alloc(D) for _ in range(2)]
        self.warm()
        tr_ev = [0, 0]
        ev_ev = [0, 0]
        for tt in range(NTT):
            pr = tt % 2
            t = dma(xtok[pr], self.x_h[b, tt * 128:(tt + 1) * 128, :],
                    wait=[(eT, tr_ev[pr])])
            for g in range(KD // 4):
                for j in range(4):
                    kc = g * 4 + j
                    ep(lambda e, pr=pr, kc=kc, j=j: e.transpose(
                        ps[:, pr * 1024 + (kc % 4) * 128 + (kc // 4) * 512:
                           pr * 1024 + (kc % 4) * 128 + (kc // 4) * 512 + 128],
                        xtok[pr][:, kc * 128:(kc + 1) * 128], self.idf[:]),
                        wait=[(qio, t), (eS, ev_ev[pr])],
                        inc=(eT, 1) if (g == KD // 4 - 1 and j == 3) else None)
            tr_ev[pr] = eT.n
            for g in range(KD // 4):
                es(lambda e, pr=pr, g=g, tt=tt: e.copy(
                    xT[:, g * 4:(g + 1) * 4, tt * 128:(tt + 1) * 128],
                    ps[:, pr * 1024 + g * 512:pr * 1024 + (g + 1) * 512]
                    .rearrange("p (k t) -> p k t", k=4)),
                   wait=[(eT, None)], inc=(eS, 1))
            ev_ev[pr] = eS.n
        xT_done = eS.n

        xs_pad = [self.p16.alloc(T + 3) for _ in range(2)]
        xss = [self.p16.alloc(T) for _ in range(2)]
        xc = [self.p32.alloc(T) for _ in range(2)]
        bcacc = self.p32.alloc(T)
        wt = [self.p16.alloc(KD * 128) for _ in range(2)]
        zst = [self.p16.alloc(512) for _ in range(2)]
        wt_use = [0, 0]
        ip_ev = [0, 0]
        xss_use = [0, 0]
        silu_done = [0, 0]
        conv_done = [0, 0]
        xs_spill = [0, 0]
        zspill = [0, 0]
        dt_ev = [0, 0]
        bc_ev = [0, 0]
        ev(lambda e: e.memset(self.dtr_row[:], 0.0), inc=(eV, 1))

        def xp_block(dch):
            pr = dch % 2
            for fc in range(NFC):
                bpr = (dch * NFC + fc) % 2
                ep(lambda e, pr=pr, dch=dch, fc=fc, bpr=bpr: e.matmul(
                    ps[:, 2048 + bpr * 512:2048 + (bpr + 1) * 512],
                    self.xpw[:, dch, 0:128], xss[pr][:, fc * 512:(fc + 1) * 512],
                    start=True, stop=True),
                    wait=[(eS, silu_done[pr]), (eV, bc_ev[bpr])], inc=(eT, 1))
                if dch == 0:
                    ev(lambda e, fc=fc, bpr=bpr: e.tensor_copy(
                        bcacc[:, fc * 512:(fc + 1) * 512],
                        ps[:, 2048 + bpr * 512:2048 + (bpr + 1) * 512]),
                       wait=[(eT, None)], drain=True, inc=(eV, 1))
                else:
                    ev(lambda e, fc=fc, bpr=bpr: e.scalar_tensor_tensor(
                        bcacc[:, fc * 512:(fc + 1) * 512],
                        ps[:, 2048 + bpr * 512:2048 + (bpr + 1) * 512], 1.0,
                        bcacc[:, fc * 512:(fc + 1) * 512], ALU.mult, ALU.add),
                       wait=[(eT, None)], drain=True, inc=(eV, 1))
                bc_ev[bpr] = eV.n
                dpr = (dch * NFC + fc) % 2
                ep(lambda e, pr=pr, dch=dch, fc=fc, dpr=dpr: e.matmul(
                    ps[0:1, 1024 + dpr * 512:1024 + dpr * 512 + 512],
                    self.xpw[:, dch, 128:129], xss[pr][:, fc * 512:(fc + 1) * 512],
                    start=True, stop=True),
                    wait=[(eV, dt_ev[dpr])], inc=(eT, 1))
                ev(lambda e, fc=fc, dpr=dpr: e.scalar_tensor_tensor(
                    self.dtr_row[0:1, fc * 512:(fc + 1) * 512],
                    ps[0:1, 1024 + dpr * 512:1024 + dpr * 512 + 512], 1.0,
                    self.dtr_row[0:1, fc * 512:(fc + 1) * 512], ALU.mult, ALU.add),
                   wait=[(eT, None)], drain=True, inc=(eV, 1))
                dt_ev[dpr] = eV.n
            xss_use[pr] = eT.n

        for dch in range(KI):
            pr = dch % 2
            tW = dma(wt[pr], self.ipw_d[dch],
                     wait=[(eT, wt_use[pr]), (qst, None)])
            ev(lambda e, pr=pr: e.memset(xs_pad[pr][:, 0:3], 0.0), inc=(eV, 1))
            pad_ev = eV.n
            for fc in range(NFC):
                fpr = fc % 2
                for kc in range(KD):
                    ep(lambda e, pr=pr, fpr=fpr, kc=kc, fc=fc: e.matmul(
                        ps[:, fpr * 512:(fpr + 1) * 512],
                        wt[pr][:, kc * 128:(kc + 1) * 128],
                        xT[:, kc, fc * 512:(fc + 1) * 512],
                        start=(kc == 0), stop=(kc == KD - 1)),
                        wait=[(qio, tW), (eS, ip_ev[fpr]), (eS, xT_done)],
                        inc=(eT, 1) if kc == KD - 1 else None)
                es(lambda e, pr=pr, fpr=fpr, fc=fc: e.copy(
                    xs_pad[pr][:, 3 + fc * 512:3 + (fc + 1) * 512],
                    ps[:, fpr * 512:(fpr + 1) * 512]),
                   wait=[(eT, None), (eV, conv_done[pr])], inc=(eS, 1))
                ip_ev[fpr] = eS.n
            wt_use[pr] = eT.n
            ev(lambda e, pr=pr, dch=dch: e.tensor_scalar(
                xc[pr], xs_pad[pr][:, 0:T], self.mcw[:, dch, 0:1], None, ALU.mult),
               wait=[(eS, None), (eV, pad_ev)], inc=(eV, 1))
            for k in range(1, 4):
                ev(lambda e, pr=pr, k=k, dch=dch: e.scalar_tensor_tensor(
                    xc[pr], xs_pad[pr][:, k:k + T], self.mcw[:, dch, k:k + 1],
                    xc[pr], ALU.mult, ALU.add), drain=True, inc=(eV, 1))
            es(lambda e, pr=pr, dch=dch: e.activation(
                xss[pr], xc[pr], AF.Silu, bias=self.mcb[:, dch:dch + 1], scale=1.0),
               wait=[(eV, None), (eT, xss_use[pr]), (qst, None)],
               inc=(eS, 1))
            silu_done[pr] = eS.n
            xs_spill[pr] = dma(self.xs_d[b, dch], xss[pr], wait=[(eS, None)])
            if dch > 0:
                xp_block(dch - 1)
        xp_block(KI - 1)
        bst = self.p16.alloc(512)
        cst = self.p16.alloc(512)
        for fc in range(NFC):
            es(lambda e, fc=fc: e.copy(bst[0:64, :],
                                       bcacc[0:64, fc * 512:(fc + 1) * 512]),
               wait=[(eV, None), (qst, None)], inc=(eS, 1))
            es(lambda e, fc=fc: e.copy(cst[64:128, :],
                                       bcacc[64:128, fc * 512:(fc + 1) * 512]),
               wait=[(qst, None)], inc=(eS, 1))
            t1 = dma(self.bmat_d[b, :, fc * 512:(fc + 1) * 512], bst[0:64, :],
                     wait=[(eS, None)])
            t2 = dma(self.cmat_d[b, :, fc * 512:(fc + 1) * 512], cst[64:128, :])
        zp_ev = [ip_ev[0], ip_ev[1]]
        for mz in range(KI):
            pr = mz % 2
            tW = dma(wt[pr], self.ipw_d[KI + mz], wait=[(eT, wt_use[pr])])
            for fc in range(NFC):
                fpr = fc % 2
                for kc in range(KD):
                    ep(lambda e, pr=pr, fpr=fpr, kc=kc, fc=fc: e.matmul(
                        ps[:, fpr * 512:(fpr + 1) * 512],
                        wt[pr][:, kc * 128:(kc + 1) * 128],
                        xT[:, kc, fc * 512:(fc + 1) * 512],
                        start=(kc == 0), stop=(kc == KD - 1)),
                        wait=[(qio, tW), (eS, zp_ev[fpr])],
                        inc=(eT, 1) if kc == KD - 1 else None)
                es(lambda e, fpr=fpr: e.copy(zst[fpr], ps[:, fpr * 512:(fpr + 1) * 512]),
                   wait=[(eT, None), (qst, None)], inc=(eS, 1))
                zspill[fpr] = dma(self.z_d[b, mz, :, fc * 512:(fc + 1) * 512],
                                  zst[fpr], wait=[(eS, None)])
                zp_ev[fpr] = eS.n
            wt_use[pr] = eT.n
        self.bc_tgt = self.qio.n
        self.p32.pop()
        self.p16.pop()

    def scan(self, b):
        c = self.c
        KI, T, NS, Tc = c.KI, c.T, c.NS, c.Tc
        NDH, DPH, GD, NDG, NG, NNG, HB = c.NDH, c.DPH, c.GD, c.NDG, c.NG, c.NNG, c.HB
        qio, qst = self.qio, self.qst
        eV, eS, eT, eG = self.eV, self.eS, self.eT, self.eG
        dma, ev, es, ep, eg = self.dma, self.ev, self.es, self.ep, self.eg
        ps = self.psum

        self.p32.push()
        self.p16.push()
        self.pf16.push()
        dtrep = self.p32.alloc(T)
        dtt = self.p32.alloc(GD * Tc).rearrange("p (d t) -> p d t", d=GD)
        spt = self.p32.alloc(Tc)
        yfull = self.p32.alloc(Tc)
        gz = self.p32.alloc(Tc)
        ST = self.p32.alloc(KI * NS)
        dtx = self.p16.alloc(GD * Tc).rearrange("p (d t) -> p d t", d=GD)
        xssl = self.p16.alloc(GD * Tc).rearrange("p (d t) -> p d t", d=GD)
        brep = self.p16.alloc(2 * NG * Tc).rearrange("p (r n t) -> p r n t", r=2, n=NG)
        crep = self.p16.alloc(2 * NG * Tc).rearrange("p (r n t) -> p r n t", r=2, n=NG)
        ubuf = self.p16.alloc(GD * Tc).rearrange("p (d t) -> p d t", d=GD)
        sbig = self.p16.alloc(GD * Tc).rearrange("p (d t) -> p d t", d=GD)
        scb = self.p16.alloc(2 * GD * Tc).rearrange("p (r d t) -> p r d t", r=2, d=GD)
        zsl = [self.p16.alloc(Tc) for _ in range(2)]
        ysl = [self.p16.alloc(Tc) for _ in range(2)]

        dma(self.row_d[0:1, :], self.dtr_row, wait=[(eV, None)])
        dma(dtrep, self.row_d[0:1, :].broadcast_to([128, T]), wait=[(qst, None)])
        dt_bcast = None

        bc_use = [0, 0]
        bc_useG = [0, 0]
        dab_use = [0, 0]
        scb_use = [0, 0]
        it = 0
        y_ev = 0
        ysl_dma = [0, 0]
        for tc in range(c.NTC):
            tcs = tc * Tc
            for dh in range(NDH):
                dch0 = dh * DPH
                for dg in range(NDG):
                    xs_t = []
                    for i in range(GD):
                        tx = dma(xssl[:, i, :],
                                 self.xs_d[b, dch0 + dg * GD + i, :, tcs:tcs + Tc],
                                 wait=[(eV, None), (qst, None)])
                        xs_t.append(tx)
                    for i in range(GD):
                        dch = dch0 + dg * GD + i
                        es(lambda e, i=i, dch=dch, tcs=tcs: e.activation(
                            spt, dtrep[:, tcs:tcs + Tc], AF.Abs,
                            scale=self.dtw[:, dch:dch + 1],
                            bias=self.dtb[:, dch:dch + 1]),
                           wait=[(qio, None), (eV, None)], drain=True, inc=(eS, 1))
                        es(lambda e: e.activation(spt, spt, AF.Exp, scale=-1.0),
                           drain=True, inc=(eS, 1))
                        es(lambda e: e.activation(spt, spt, AF.Ln, bias=1.0),
                           drain=True, inc=(eS, 1))
                        es(lambda e, i=i, dch=dch, tcs=tcs: e.activation(
                            dtt[:, i, :], dtrep[:, tcs:tcs + Tc], AF.Relu,
                            scale=self.dtw[:, dch:dch + 1],
                            bias=self.dtb[:, dch:dch + 1]),
                           drain=True, inc=(eS, 1))
                        ev(lambda e, i=i: e.tensor_add(dtt[:, i, :], dtt[:, i, :],
                                                       spt),
                           wait=[(eS, None)], drain=True, inc=(eV, 1))
                        eg(lambda e, i=i: e.tensor_mul(dtx[:, i, :], dtt[:, i, :],
                                                       xssl[:, i, :]),
                           wait=[(qio, xs_t[i]), (eV, None)], drain=True, inc=(eG, 1))
                    dt_built = eV.n
                    jds = list(range(GD))
                    for ng in range(NNG):
                        gpr = ng % 2
                        n0 = ng * NG
                        tB = dma(brep[:, gpr],
                                 self.bmat_d[b, n0:n0 + NG, tcs:tcs + Tc]
                                 .unsqueeze(0).broadcast_to([128, NG, Tc]),
                                 wait=[(eV, bc_use[gpr]), (eG, bc_useG[gpr]),
                                       (qst, None)])
                        tC = dma(crep[:, gpr],
                                 self.cmat_d[b, n0:n0 + NG, tcs:tcs + Tc]
                                 .unsqueeze(0).broadcast_to([128, NG, Tc]))
                        for ni in range(NG):
                            n = n0 + ni
                            pr = it % 2
                            for j in jds:
                                dch = dch0 + dg * GD + j
                                es(lambda e, pr=pr, j=j, dch=dch, n=n: e.activation(
                                    self.dab(pr, j), dtt[:, j, :], AF.Exp,
                                    scale=self.aneg[:, dch, n:n + 1]),
                                   wait=[(eV, dab_use[pr]), (eV, dt_built)],
                                   inc=(eS, 1) if j == jds[-1] else None)
                            ev(lambda e, gpr=gpr, ni=ni: e.tensor_mul(
                                ubuf[:, :, :],
                                dtx[:, :, :],
                                brep[:, gpr, ni, :].unsqueeze(1)
                                .broadcast_to([128, GD, Tc])),
                               wait=[(qio, tB), (eS, None), (eG, None)],
                               inc=(eV, 1))
                            for j in jds:
                                dch = dch0 + dg * GD + j
                                jj = j
                                if tc == 0:
                                    init = 0.0
                                else:
                                    init = ST[:, dch * NS + n:dch * NS + n + 1]
                                ev(lambda e, jj=jj, pr=pr, init=init: e.tensor_tensor_scan(
                                    sbig[:, jj, :], self.dab(pr, jj), ubuf[:, jj, :],
                                    init, ALU.mult, ALU.add),
                                   wait=[(eG, None)], drain=(jj == 0),
                                   inc=(eV, 1) if jj == GD - 1 else None)
                            dab_use[pr] = eV.n
                            ev(lambda e, pr=pr, gpr=gpr, ni=ni: e.tensor_mul(
                                scb[:, pr, :, :],
                                sbig[:, :, :],
                                crep[:, gpr, ni, :].unsqueeze(1)
                                .broadcast_to([128, GD, Tc])),
                               wait=[(qio, tC), (eT, scb_use[pr])], drain=True,
                               inc=(eV, 1))
                            if tc < c.NTC - 1:
                                eg(lambda e, dch0=dch0, dg=dg, n=n: e.tensor_copy(
                                    ST.rearrange("p (d n) -> p d n", n=NS)
                                    [:, dch0 + dg * GD:dch0 + (dg + 1) * GD, n:n + 1],
                                    sbig[:, :, Tc - 1:Tc]),
                                   wait=[(eV, None)], inc=(eG, 1))
                            sC_done = eV.n
                            first = (ng == 0 and ni == 0)
                            last = (ng == NNG - 1 and ni == NG - 1)
                            HBS = c.HBS
                            ybank = max(Tc, 512)
                            for j in range(GD):
                                for h in range(HB):
                                    ep(lambda e, j=j, h=h, pr=pr, first=first,
                                       last=last, HBS=HBS, ybank=ybank: e.matmul(
                                           ps[:, j * ybank + h * HBS:
                                              j * ybank + (h + 1) * HBS],
                                           self.idb[:],
                                           scb[:, pr, j, h * HBS:(h + 1) * HBS],
                                           start=first, stop=last),
                                       wait=[(eV, sC_done), (eV, y_ev)],
                                       inc=(eT, 1) if (j == GD - 1 and h == HB - 1)
                                       else None)
                            scb_use[pr] = eT.n
                            it += 1
                        bc_use[gpr] = eV.n
                        bc_useG[gpr] = eG.n
                    for j in range(GD):
                        dch = dch0 + dg * GD + j
                        zpr = j % 2
                        tz = dma(zsl[zpr], self.z_d[b, dch, :, tcs:tcs + Tc],
                                 wait=[(eS, None), (qst, None)])
                        ev(lambda e, j=j, dch=dch: e.scalar_tensor_tensor(
                            yfull, xssl[:, j, :],
                            self.dvec[:, dch:dch + 1],
                            ps[:, j * max(Tc, 512):j * max(Tc, 512) + Tc],
                            ALU.mult, ALU.add),
                           wait=[(eT, None)], drain=True, inc=(eV, 1))
                        es(lambda e, zpr=zpr: e.activation(gz, zsl[zpr], AF.Silu),
                           wait=[(qio, tz), (eV, None)], drain=True, inc=(eS, 1))
                        ev(lambda e, zpr=zpr: e.tensor_mul(ysl[zpr], yfull, gz),
                           wait=[(eS, None), (qst, None)], drain=True,
                           inc=(eV, 1))
                        ysl_dma[zpr] = dma(self.y_d[b, dch, :, tcs:tcs + Tc],
                                           ysl[zpr], wait=[(eV, None)])
                    y_ev = eV.n
        self.y_tgt = self.qio.n
        self.p32.pop()
        self.p16.pop()
        self.pf16.pop()

    def warm(self, n=24, wait=()):
        ps = self.psum
        for i in range(n):
            self.ep(lambda e, i=i, n=n: e.matmul(
                ps[:, 3968:4096], self.idb[:], self.idb[:],
                start=(i == 0), stop=(i == n - 1)),
                wait=([(self.eT, None)] + list(wait)) if i == 0 else (),
                inc=(self.eT, 1))

    def dab(self, pr, j):
        Tc, GD = self.c.Tc, self.c.GD
        off = (pr * GD + j) * Tc
        return self.pf16.t[:, off:off + Tc]

    def phase34(self, b):
        c = self.c
        KD, KI, KH, T, D, DH, Tc = c.KD, c.KI, c.KH, c.T, c.D, c.DH, c.Tc
        NFC, NTT, TT_C = c.NFC, c.NTT, c.TT_C
        qio, qst, qout = self.qio, self.qst, self.qout
        eV, eS, eT, eG = self.eV, self.eS, self.eT, self.eG
        dma, ev, es, ep, eg = self.dma, self.ev, self.es, self.ep, self.eg
        ps = self.psum
        smv = self.smv

        self.p32.push()
        self.p16.push()
        self.pf16.push()
        f16q = [self.pf16.alloc(D) for _ in range(2)]
        f16h = [self.pf16.alloc(T) for _ in range(2)]
        x1qT = self.p16.alloc(KD * T).rearrange("p (k t) -> p k t", k=KD)
        isrep = self.p16.alloc(T)
        r1b = [self.p32.alloc(D) for _ in range(2)]
        sqb = [self.p32.alloc(D) for _ in range(2)]
        xtk = [self.p32.alloc(D) for _ in range(2)]
        TG = min(8, TT_C)
        self.p32.push()
        r1big = self.p32.alloc(TG * D).rearrange("p (a t) -> p a t", a=TG)
        self.p16.push()
        Yb = self.p16.alloc(KI * Tc).rearrange("p (k t) -> p k t", k=KI)
        attn = self.p16.alloc(KD * Tc).rearrange("p (k t) -> p k t", k=KD)
        xqb = [self.p16.alloc(D) for _ in range(2)]
        wop = [self.p16.alloc(KI * 128) for _ in range(2)]

        def layer_norm(tile, pr, out, wait_extra=(), wait_out=()):
            ev(lambda e: e.reduce_sum(smv[:, 0:1], tile, axis=AX.X),
               wait=list(wait_extra), drain=True, inc=(eV, 1))
            es(lambda e, pr=pr: e.activation(sqb[pr], tile, AF.Square),
               wait=[(eV, None)], inc=(eS, 1))
            ev(lambda e, pr=pr: e.reduce_sum(smv[:, 1:2], sqb[pr], axis=AX.X),
               wait=[(eS, None)], drain=True, inc=(eV, 1))
            ev(lambda e: e.tensor_scalar(smv[:, 0:2], smv[:, 0:2], 1.0 / D, None,
                                         ALU.mult), drain=True, inc=(eV, 1))
            ev(lambda e: e.tensor_mul(smv[:, 2:3], smv[:, 0:1], smv[:, 0:1]),
               drain=True, inc=(eV, 1))
            ev(lambda e: e.tensor_sub(smv[:, 1:2], smv[:, 1:2], smv[:, 2:3]),
               drain=True, inc=(eV, 1))
            ev(lambda e: e.tensor_scalar(smv[:, 1:2], smv[:, 1:2], 1e-5, None,
                                         ALU.add), drain=True, inc=(eV, 1))
            ev(lambda e: e.reciprocal(smv[:, 2:3], smv[:, 1:2]), drain=True,
               inc=(eV, 1))
            es(lambda e: e.activation(smv[:, 2:3], smv[:, 2:3], AF.Sqrt),
               wait=[(eV, None)], drain=True, inc=(eS, 1))
            ev(lambda e: e.tensor_scalar(out, tile, smv[:, 0:1], smv[:, 2:3],
                                         ALU.subtract, ALU.mult),
               wait=[(eS, None)] + list(wait_out), drain=True, inc=(eV, 1))

        self.warm()
        wop_use = [0, 0]
        op_ev = [0, 0]
        ptok_use = [0, 0]
        xtk_use = [0, 0]
        pxq_ev = 0
        for tc in range(c.NTC):
            tcs = tc * Tc
            yt = 0
            for kc in range(KI):
                yt = dma(Yb[:, kc, :], self.y_d[b, kc, :, tcs:tcs + Tc],
                         wait=[(eT, None), (qst, None)])
            for m in range(KD):
                pr = m % 2
                tW = dma(wop[pr], self.opw_d[m], wait=[(eT, wop_use[pr])])
                for h in range(c.HB):
                    HBS = c.HBS
                    fpr = (m * c.HB + h) % 2
                    for kc in range(KI):
                        ep(lambda e, pr=pr, fpr=fpr, kc=kc, h=h, HBS=HBS: e.matmul(
                            ps[:, fpr * 512:fpr * 512 + HBS],
                            wop[pr][:, kc * 128:(kc + 1) * 128],
                            Yb[:, kc, h * HBS:(h + 1) * HBS],
                            start=(kc == 0), stop=(kc == KI - 1)),
                            wait=[(qio, max(tW, yt)), (eS, op_ev[fpr])],
                            inc=(eT, 1) if kc == KI - 1 else None)
                    es(lambda e, m=m, h=h, fpr=fpr, HBS=HBS: e.copy(
                        attn[:, m, h * HBS:(h + 1) * HBS],
                        ps[:, fpr * 512:fpr * 512 + HBS]),
                       wait=[(eT, None)], inc=(eS, 1))
                    op_ev[fpr] = eS.n
                wop_use[pr] = eT.n
            attn_done = eS.n
            TT = min(8, TT_C)
            for tg in range(0, TT_C, TT):
              for j in range(TT):
                tt = tg + j
                pr = tt % 2
                pt0 = 1024 + pr * 1024
                for m in range(KD):
                    ep(lambda e, m=m, tt=tt, pt0=pt0: e.transpose(
                        self.psum_b[:, 2 * pt0 + m * 128:2 * pt0 + (m + 1) * 128],
                        attn[:, m, tt * 128:(tt + 1) * 128], self.idb[:]),
                        wait=[(eS, attn_done), (eV, ptok_use[pr])],
                        inc=(eT, 1) if m == KD - 1 else None)
                tX = dma(xtk[pr], self.x_h[b, tcs + tt * 128:tcs + (tt + 1) * 128, :],
                         wait=[(eV, xtk_use[pr])])
                ev(lambda e, pr=pr, pt0=pt0, j=j: e.tensor_add(
                    r1big[:, j, :], self.psum_b[:, 2 * pt0:2 * pt0 + D], xtk[pr]),
                   wait=[(eT, None), (qio, tX), (qst, None)], drain=True,
                   inc=(eV, 1))
                ptok_use[pr] = eV.n
                xtk_use[pr] = eV.n
              smb = self.smb
              gs = tc * TT_C + tg
              ev(lambda e: e.reduce_sum(smb[:, 0:TT], r1big[:, :, :], axis=AX.X),
                 drain=True, inc=(eV, 1))
              for j in range(TT):
                  qr = j % 2
                  es(lambda e, qr=qr, j=j: e.activation(sqb[qr], r1big[:, j, :],
                                                        AF.Square),
                     wait=[(eV, None)], inc=(eS, 1))
                  ev(lambda e, qr=qr, j=j: e.reduce_sum(
                      smb[:, TT + j:TT + j + 1], sqb[qr], axis=AX.X),
                     wait=[(eS, None)], drain=True, inc=(eV, 1))
              ev(lambda e: e.tensor_scalar(smb[:, 0:2 * TT], smb[:, 0:2 * TT],
                                           1.0 / D, None, ALU.mult),
                 drain=True, inc=(eV, 1))
              ev(lambda e: e.tensor_mul(smb[:, 2 * TT:3 * TT], smb[:, 0:TT],
                                        smb[:, 0:TT]), drain=True, inc=(eV, 1))
              ev(lambda e: e.tensor_sub(smb[:, TT:2 * TT], smb[:, TT:2 * TT],
                                        smb[:, 2 * TT:3 * TT]), drain=True,
                 inc=(eV, 1))
              ev(lambda e: e.tensor_scalar(smb[:, TT:2 * TT], smb[:, TT:2 * TT],
                                           1e-5, None, ALU.add), drain=True,
                 inc=(eV, 1))
              ev(lambda e: e.reciprocal(smb[:, 2 * TT:3 * TT], smb[:, TT:2 * TT]),
                 drain=True, inc=(eV, 1))
              es(lambda e: e.activation(smb[:, 2 * TT:3 * TT],
                                        smb[:, 2 * TT:3 * TT], AF.Sqrt),
                 wait=[(eV, None)], drain=True, inc=(eS, 1))
              for j in range(TT):
                  tt = tg + j
                  ev(lambda e, j=j: e.tensor_scalar(
                      r1big[:, j, :], r1big[:, j, :], smb[:, j:j + 1],
                      smb[:, 2 * TT + j:2 * TT + j + 1], ALU.subtract, ALU.mult),
                     wait=[(eS, None)], drain=True, inc=(eV, 1))
                  dma(self.x1_d[b, tcs + tt * 128:tcs + (tt + 1) * 128, :],
                      r1big[:, j, :], wait=[(eV, None)])
              ev(lambda e: e.reduce_max(smb[:, 3 * TT:4 * TT], r1big[:, :, :],
                                        axis=AX.X, apply_absolute_value=True),
                 drain=True, inc=(eV, 1))
              ev(lambda e: e.tensor_scalar(smb[:, 3 * TT:4 * TT],
                                           smb[:, 3 * TT:4 * TT], 1e-5, None,
                                           ALU.max), drain=True, inc=(eV, 1))
              ev(lambda e: e.reciprocal(smb[:, 4 * TT:5 * TT],
                                        smb[:, 3 * TT:4 * TT]),
                 drain=True, inc=(eV, 1))
              ev(lambda e: e.tensor_scalar(smb[:, 4 * TT:5 * TT],
                                           smb[:, 4 * TT:5 * TT], 127.0, None,
                                           ALU.mult), drain=True, inc=(eV, 1))
              ev(lambda e, gs=gs, TT=TT: e.tensor_mul(
                  self.itok[:, gs:gs + TT], smb[:, 3 * TT:4 * TT],
                  self.iw1[:].broadcast_to([128, TT])), drain=True, inc=(eV, 1))
              for j in range(TT):
                  pr = j % 2
                  gtt = gs + j
                  ev(lambda e, pr=pr, j=j: e.tensor_scalar(
                      f16q[pr], r1big[:, j, :],
                      smb[:, 4 * TT + j:4 * TT + j + 1],
                      None, ALU.mult), drain=True, inc=(eV, 1))
                  ev(lambda e, pr=pr: e.tensor_scalar(f16q[pr], f16q[pr],
                                                      MAGIC16, None, ALU.add),
                     drain=True, inc=(eV, 1))
                  ev(lambda e, pr=pr: e.tensor_scalar(xqb[pr], f16q[pr],
                                                      MAGIC16, None, ALU.subtract),
                     drain=True, inc=(eV, 1))
                  for kc in range(KD):
                      ep(lambda e, pr=pr, kc=kc: e.transpose(
                          self.psum_b[:, 6144 + kc * 128:6144 + (kc + 1) * 128],
                          xqb[pr][:, kc * 128:(kc + 1) * 128], self.idb[:]),
                          wait=[(eV, None), (eS, pxq_ev)],
                          inc=(eT, 1) if kc == KD - 1 else None)
                  es(lambda e, gtt=gtt: e.copy(
                      x1qT[:, :, gtt * 128:(gtt + 1) * 128],
                      self.psum_b[:, 6144:6144 + D]
                      .rearrange("p (k t) -> p k t", k=KD)),
                     wait=[(eT, None)], inc=(eS, 1))
                  pxq_ev = eS.n
        dma(self.itok_d, self.itok[:, 0:NTT], wait=[(eV, None)])
        dma(self.is_row[0:1, :],
            self.itok_d.transpose([1, 0]).unsqueeze(0),
            wait=[(qst, None)], slow=True)
        ev(lambda e: e.tensor_copy(self.is_rowb, self.is_row),
           wait=[(qio, None)], drain=True, inc=(eV, 1))
        dma(self.rowb_d[0:1, :], self.is_rowb, wait=[(eV, None)])
        dma(isrep, self.rowb_d[0:1, :].broadcast_to([128, T]), wait=[(qst, None)])
        is_done = None

        self.p32.pop()
        self.p16.pop()
        hst = [self.p16.alloc(T) for _ in range(2)]
        w1t = [self.p16.alloc(KD * 128) for _ in range(2)]
        h2r = [self.p16.alloc(512) for _ in range(2)]
        sc2rep = self.p16.alloc(T)
        iv2rep = self.p16.alloc(T)
        w2t = [self.p16.alloc(KD * 128) for _ in range(2)]
        h2s = [self.p16.alloc(512) for _ in range(2)]
        outb = [self.p32.alloc(D) for _ in range(2)]
        self.p32.push()
        hpad = [self.p32.alloc(T + 2) for _ in range(2)]
        hcv = [self.p32.alloc(T) for _ in range(2)]

        gtok = self.p32.alloc(c.NTT)
        tmax = self.p32.alloc(c.NTT)
        ev(lambda e: e.memset(gtok, 0.0), inc=(eV, 1))
        x1qT_done = pxq_ev
        self.warm(wait=[(self.eS, x1qT_done)])
        gm_ev = 0
        w1_use = [0, 0]
        h_ev = [0, 0]
        hc_use = [0, 0]
        hs_use = [0, 0]
        for mh in range(KH):
            pr = mh % 2
            tW = dma(w1t[pr], self.w1t_d[mh], wait=[(eT, w1_use[pr])])
            ev(lambda e, pr=pr: e.memset(hpad[pr][:, 0:1], 0.0),
               wait=[(eV, hc_use[pr])], inc=(eV, 1))
            ev(lambda e, pr=pr: e.memset(hpad[pr][:, T + 1:T + 2], 0.0), inc=(eV, 1))
            pad_ev = eV.n
            for fc in range(NFC):
                fpr = fc % 2
                for kc in range(KD):
                    ep(lambda e, pr=pr, fpr=fpr, kc=kc, fc=fc: e.matmul(
                        ps[:, fpr * 512:(fpr + 1) * 512],
                        w1t[pr][:, kc * 128:(kc + 1) * 128],
                        x1qT[:, kc, fc * 512:(fc + 1) * 512],
                        start=(kc == 0), stop=(kc == KD - 1)),
                        wait=[(qio, tW), (eV, h_ev[fpr]), (eS, x1qT_done)],
                        inc=(eT, 1) if kc == KD - 1 else None)
                ev(lambda e, pr=pr, fc=fc, fpr=fpr: e.tensor_mul(
                    hpad[pr][:, 1 + fc * 512:1 + (fc + 1) * 512],
                    ps[:, fpr * 512:(fpr + 1) * 512],
                    isrep[:, fc * 512:(fc + 1) * 512]),
                   wait=[(eT, None), (qio, None), (eV, pad_ev)], drain=True,
                   inc=(eV, 1))
                h_ev[fpr] = eV.n
            w1_use[pr] = eT.n
            ev(lambda e, pr=pr, mh=mh: e.tensor_scalar(
                hcv[pr], hpad[pr][:, 0:T], self.fcw[:, mh, 0:1], None, ALU.mult),
               drain=True, inc=(eV, 1))
            for k in range(1, 3):
                ev(lambda e, pr=pr, mh=mh, k=k: e.scalar_tensor_tensor(
                    hcv[pr], hpad[pr][:, k:k + T], self.fcw[:, mh, k:k + 1],
                    hcv[pr], ALU.mult, ALU.add), drain=True, inc=(eV, 1))
            es(lambda e, pr=pr, mh=mh: e.activation(
                hst[pr], hcv[pr], AF.Silu, bias=self.fcb[:, mh:mh + 1], scale=1.0),
               wait=[(eV, None), (qst, None)], drain=True, inc=(eS, 1))
            hc_use[pr] = eS.n
            dma(self.hs_d[b, mh], hst[pr], wait=[(eS, None)])
            hs_use[pr] = self.qio.n
            for tt in range(c.NTT):
                ep(lambda e, pr=pr, tt=tt: e.transpose(
                    self.psum_b[:, 4096 + tt * 128:4096 + (tt + 1) * 128],
                    hst[pr][:, tt * 128:(tt + 1) * 128], self.idb[:]),
                    wait=[(eS, None), (eV, gm_ev)],
                    inc=(eT, 1) if tt == c.NTT - 1 else None)
            ev(lambda e: e.reduce_max(
                tmax, self.psum_b[:, 4096:4096 + c.NTT * 128]
                .rearrange("p (a b) -> p a b", a=c.NTT), axis=AX.X,
                apply_absolute_value=True),
               wait=[(eT, None)], drain=True, inc=(eV, 1))
            ev(lambda e: e.tensor_max(gtok, gtok, tmax), drain=True, inc=(eV, 1))
            gm_ev = eV.n
        self.p32.pop()
        self.p32.push()
        sc2f = self.p32.alloc(T)[0:1, :]
        dma(self.gtok_d, gtok, wait=[(eV, None)])
        dma(self.gm_row, self.gtok_d.transpose([1, 0]).unsqueeze(0),
            wait=[(qst, None)], slow=True)
        ev(lambda e: e.tensor_scalar(self.gm_row, self.gm_row, 1e-5, None,
                                     ALU.max), wait=[(qio, None)], drain=True,
           inc=(eV, 1))
        ev(lambda e: e.reciprocal(sc2f, self.gm_row), drain=True,
           inc=(eV, 1))
        ev(lambda e: e.tensor_scalar(sc2f, sc2f, 127.0, None,
                                     ALU.mult), drain=True, inc=(eV, 1))
        ev(lambda e: e.tensor_scalar(self.iv2_rowb, self.gm_row,
                                     self.iw2[0:1, 0:1], None, ALU.mult),
           drain=True, inc=(eV, 1))
        ev(lambda e: e.tensor_copy(self.sc2_rowb, sc2f),
           drain=True, inc=(eV, 1))
        dma(self.rowb_d[1:2, :], self.sc2_rowb, wait=[(eV, None)])
        dma(self.rowb_d[2:3, :], self.iv2_rowb)
        dma(sc2rep, self.rowb_d[1:2, :].broadcast_to([128, T]), wait=[(qst, None)])
        dma(iv2rep, self.rowb_d[2:3, :].broadcast_to([128, T]))
        self.p32.pop()
        self.p32.push()
        ffn_f = self.p32.alloc(KD * 512).rearrange("p (m t) -> p m t", m=KD)
        qv = [self.p32.alloc(512) for _ in range(2)]
        ffn_ev = 0
        fc_clear = 0
        pt_use = [0, 0]
        w2_use = [0, 0]
        self.warm(wait=[(self.eV, None)])
        for fc in range(NFC):
            for kh in range(KH):
                pr = kh % 2
                th = dma(h2r[pr], self.hs_d[b, kh, :, fc * 512:(fc + 1) * 512],
                         wait=[(eT, w2_use[pr]), (qst, None)])
                tW = dma(w2t[pr], self.w2t_d[kh])
                ev(lambda e, pr=pr, fc=fc: e.tensor_mul(
                    qv[pr], h2r[pr], sc2rep[:, fc * 512:(fc + 1) * 512]),
                   wait=[(qio, th), (eT, w2_use[pr])], drain=True, inc=(eV, 1))
                ev(lambda e, pr=pr: e.tensor_scalar(
                    f16h[pr][:, 0:512], qv[pr], MAGIC16, None, ALU.add),
                   drain=True, inc=(eV, 1))
                ev(lambda e, pr=pr: e.tensor_scalar(
                    h2s[pr], f16h[pr][:, 0:512], MAGIC16, None, ALU.subtract),
                   drain=True, inc=(eV, 1))
                hq_done = eV.n
                for m in range(KD):
                    wl = [(qio, tW), (eV, hq_done), (eV, fc_clear)]
                    if fc == 0 and kh == 0 and m == KD - 1:
                        wl.append((eT, None))
                    ep(lambda e, pr=pr, m=m, kh=kh: e.matmul(
                        ps[:, m * 512:(m + 1) * 512],
                        w2t[pr][:, m * 128:(m + 1) * 128], h2s[pr],
                        start=(kh == 0), stop=(kh == KH - 1)),
                        wait=wl,
                        inc=(eT, 1) if m == KD - 1 else None)
                w2_use[pr] = eT.n
            for m in range(KD):
                ev(lambda e, m=m, fc=fc: e.tensor_mul(
                    ffn_f[:, m, :], ps[:, m * 512:(m + 1) * 512],
                    iv2rep[:, fc * 512:(fc + 1) * 512]),
                   wait=[(eT, None)], drain=True, inc=(eV, 1))
            ffn_ev = eV.n
            for tt in range(4):
                pr = tt % 2
                rows = fc * 512 + tt * 128
                for m in range(KD):
                    ep(lambda e, m=m, tt=tt, pr=pr: e.transpose(
                        ps[:, pr * 1024 + m * 128:pr * 1024 + (m + 1) * 128],
                        ffn_f[:, m, tt * 128:(tt + 1) * 128], self.idf[:]),
                        wait=[(eV, ffn_ev), (eV, pt_use[pr])],
                        inc=(eT, 1) if m == KD - 1 else None)
                tX = dma(xtk[pr], self.x1_d[b, rows:rows + 128, :],
                         wait=[(eV, xtk_use[pr]), (qst, None)])
                ev(lambda e, pr=pr: e.tensor_add(r1b[pr], ps[:, pr * 1024:pr * 1024 + D],
                                                 xtk[pr]),
                   wait=[(eT, None), (qio, tX)], drain=True, inc=(eV, 1))
                pt_use[pr] = eV.n
                xtk_use[pr] = eV.n
                layer_norm(r1b[pr], pr, outb[pr], wait_out=[(qout, None)])
                dma(self.out_h[b, rows:rows + 128, :], outb[pr],
                    wait=[(eV, None)], sem=qout)
            fc_clear = eV.n
        self.p32.pop()
        self.p32.pop()
        self.p16.pop()
        self.pf16.pop()


def build_program(cfg, dbg=False):
    return Builder(cfg).build(dbg=dbg)


def kernel(**inputs):
    import ml_dtypes
    cfg = Cfg(small=False)
    x = np.ascontiguousarray(np.asarray(inputs["L0_attn_x"], dtype=np.float32))
    n_cores = 8
    shards = np.split(x, n_cores, axis=0)
    base = {k: np.ascontiguousarray(np.asarray(v, dtype=np.float32))
            for k, v in inputs.items() if k != "L0_attn_x"}
    ident_f = np.eye(128, dtype=np.float32)
    ident_b = np.eye(128, dtype=ml_dtypes.bfloat16)
    nc = build_program(cfg)
    in_maps = []
    for i in range(n_cores):
        m = dict(base)
        m["L0_attn_x"] = shards[i]
        m["ident_f32"] = ident_f
        m["ident_bf16"] = ident_b
        in_maps.append(m)
    res = run_bass_kernel_spmd(nc, in_maps, core_ids=list(range(n_cores)))
    return np.concatenate([r["out"] for r in res.results], axis=0)


# version 24
# speedup vs baseline: 1.0315x; 1.0055x over previous
import numpy as np

import concourse.bass as bass
import concourse.mybir as mybir
import concourse.library_config as libcfg
from concourse.bass_isa_rust import ReduceOp
from concourse.bass_utils import run_bass_kernel_spmd

F32 = mybir.dt.float32
BF16 = mybir.dt.bfloat16
F16 = mybir.dt.float16
ALU = mybir.AluOpType
AF = mybir.ActivationFunctionType
AX = mybir.AxisListType

MAGIC = float(np.float32(3 * 2 ** 22))
MAGIC16 = 1536.0


class Cfg:
    def __init__(self, small=False):
        if small:
            self.B, self.T, self.D, self.DI, self.DH, self.Tc = 1, 512, 512, 512, 1024, 256
        else:
            self.B, self.T, self.D, self.DI, self.DH, self.Tc = 2, 2048, 1024, 2048, 4096, 1024
        self.NS = 64
        self.KD = self.D // 128
        self.KI = self.DI // 128
        self.KH = self.DH // 128
        self.NTC = self.T // self.Tc
        self.NFC = self.T // 512
        self.NTT = self.T // 128
        self.TT_C = self.Tc // 128
        self.NDH = 2
        self.DPH = self.KI // self.NDH
        self.GD = min(4, self.DPH)
        self.NDG = self.DPH // self.GD
        self.NG = 4
        self.NNG = self.NS // self.NG
        self.HBS = min(512, self.Tc)
        self.HB = self.Tc // self.HBS
        assert self.GD * self.Tc * 4 <= 8 * 2048


ENGINES = ("pe", "ve", "se", "ge", "sy")


class Sem:
    __slots__ = ("h", "n", "name", "dma", "last_waited", "boundary")

    def __init__(self, h, name, dma=False):
        self.h, self.n, self.name, self.dma = h, 0, name, dma
        self.last_waited = 0
        self.boundary = 0

    def bump(self, k):
        self.n += k
        return self.n


class Pool:
    def __init__(self, t, elems):
        self.t, self.elems, self.off, self.marks = t, elems, 0, []

    def alloc(self, n):
        assert self.off + n <= self.elems, f"pool overflow: {self.off}+{n}>{self.elems}"
        o = self.off
        self.off += n
        return self.t[:, o:o + n]

    def push(self):
        self.marks.append(self.off)

    def pop(self):
        self.off = self.marks.pop()


class Builder:
    def __init__(self, cfg):
        self.c = cfg
        self.nc = bass.Bass("TRN2", target_bir_lowering=False)
        self.ops = {e: [] for e in ENGINES}
        self.lastwait = {e: {} for e in ENGINES}
        self._guards = []
        self.dma_sems = []

    def emit(self, e, fn, wait=(), drain=False, inc=None):
        if drain:
            sem = getattr(self, "engine_sem", {}).get(e)
            if sem is not None:
                wait = list(wait) + [(sem, sem.n)]
                drain = False
        wl = []
        for sem, cnt in wait:
            if cnt is None or sem.dma:
                cnt = sem.n
            if sem.dma:
                sem.last_waited = max(sem.last_waited, cnt)
            if cnt > self.lastwait[e].get(id(sem), -1):
                wl.append((sem, cnt))
                self.lastwait[e][id(sem)] = cnt
        if e == "sy" and wl:
            for s in self.dma_sems:
                s.boundary = s.n
        if inc is not None:
            inc[0].bump(inc[1])

        def op(eng, wl=wl, drain=drain, fn=fn, inc=inc):
            for sem, cnt in wl:
                eng.wait_ge(sem.h, cnt)
            if drain:
                eng.drain()
            r = fn(eng)
            if inc is not None:
                r.then_inc(inc[0].h, inc[1])

        self.ops[e].append(op)

    def barrier(self):
        b = self.bar
        tgt = b.n + len(ENGINES)
        b.n = tgt
        dwaits = [(s.h, s.n) for s in self.dma_sems if s.n > 0]
        for s in self.dma_sems:
            s.last_waited = max(s.last_waited, s.n)
            s.boundary = s.n
        for e in ENGINES:
            def op(eng, tgt=tgt, b=b, dwaits=dwaits):
                for h, n in dwaits:
                    eng.wait_ge(h, n)
                eng.drain()
                eng.sem_inc(b.h, 1)
                eng.wait_ge(b.h, tgt)
            self.ops[e].append(op)
            self.lastwait[e] = {}

    def dma(self, dst, src, wait=(), sem=None, slow=False):
        if sem is None:
            try:
                is_store = "DRAM" in str(dst.space)
            except Exception:
                is_store = False
            sem = self.qst if is_store else self.qio
        wait = list(wait)
        if sem.last_waited > sem.boundary:
            wait.append((sem, sem.last_waited))
            self.lastwait["sy"].pop(id(sem), None)
        self.emit("sy", lambda e, d=dst, s=src, sl=slow: e.dma_start(
            out=d, in_=s, allow_slow_non_contiguous=sl),
                  wait=wait, inc=(sem, 16))
        return sem.n

    def ev(self, fn, **kw):
        self.emit("ve", fn, **kw)

    def es(self, fn, **kw):
        self.emit("se", fn, **kw)

    def ep(self, fn, **kw):
        self.emit("pe", fn, **kw)

    def eg(self, fn, **kw):
        self.emit("ge", fn, **kw)

    def sb(self, name, shape, dtype):
        g = self.nc.sbuf_tensor(name, list(shape), dtype)
        t = g.__enter__()
        self._guards.append(g)
        return t

    def build(self, dbg=False):
        nc, c = self.nc, self.c
        B, T, D, DI, DH, NS = c.B, c.T, c.D, c.DI, c.DH, c.NS
        KD, KI, KH, Tc = c.KD, c.KI, c.KH, c.Tc

        din = lambda n, s: nc.dram_tensor(n, list(s), F32, kind="ExternalInput").ap()
        self.x_h = din("L0_attn_x", (B, T, D))
        self.ipw_h = din("L0_attn_in_proj_w", (D, 2 * DI))
        self.mcw_h = din("L0_attn_conv_w", (DI, 1, 4))
        self.mcb_h = din("L0_attn_conv_b", (DI,))
        self.xpw_h = din("L0_attn_x_proj_w", (DI, 2 * NS + 1))
        self.dtw_h = din("L0_attn_dt_w", (1, DI))
        self.dtb_h = din("L0_attn_dt_b", (DI,))
        self.alog_h = din("L0_attn_A_log", (DI, NS))
        self.dvec_h = din("L0_attn_D", (DI,))
        self.opw_h = din("L0_attn_out_proj_w", (DI, D))
        self.w1_h = din("L0_ffn_w1", (DH, D))
        self.fcw_h = din("L0_ffn_conv_w", (DH, 1, 3))
        self.fcb_h = din("L0_ffn_conv_b", (DH,))
        self.w2_h = din("L0_ffn_w2", (D, DH))
        self.idf_h = din("ident_f32", (128, 128))
        self.idb_h = nc.dram_tensor("ident_bf16", [128, 128], BF16,
                                    kind="ExternalInput").ap()
        self.out_h = nc.dram_tensor("out", [B, T, D], F32, kind="ExternalOutput").ap()

        scr = lambda n, s, d: nc.dram_tensor(n, list(s), d, kind="Internal").ap()
        self.ipw_d = scr("ipw_bf", (2 * KI, 128, KD * 128), BF16)
        self.opw_d = scr("opw_bf", (KD, 128, KI * 128), BF16)
        self.w1t_d = scr("w1qT", (KH, 128, KD * 128), BF16)
        self.w2t_d = scr("w2qT", (KH, 128, KD * 128), BF16)
        self.z_d = scr("z_dram", (B, KI, 128, T), BF16)
        self.xs_d = scr("xs_dram", (B, KI, 128, T), BF16)
        self.bmat_d = scr("bmat_dram", (B, NS, T), BF16)
        self.cmat_d = scr("cmat_dram", (B, NS, T), BF16)
        self.y_d = scr("y_dram", (B, KI, 128, T), BF16)
        self.x1_d = scr("x1_dram", (B, T, D), F32)
        self.hs_d = scr("hs_dram", (B, KH, 128, T), BF16)
        self.h2q_d = scr("h2q_dram", (B, KH, 128, T), BF16)
        self.itok_d = scr("itok_dram", (128, c.NTT), F32)
        self.gtok_d = scr("gtok_dram", (128, c.NTT), F32)
        self.row_d = scr("row_dram", (2, T), F32)
        self.rowb_d = scr("rowb_dram", (4, T), BF16)
        self.scl_d = scr("scl_dram", (2, 2), F32)

        self.idf = self.sb("idf", (128, 128), F32)
        self.idb = self.sb("idb", (128, 128), BF16)
        self.mcw = self.sb("mcw", (128, KI, 4), F32)
        self.mcb = self.sb("mcb", (128, KI), F32)
        self.dtw = self.sb("dtw", (128, KI), F32)
        self.dtb = self.sb("dtb", (128, KI), F32)
        self.dvec = self.sb("dvec", (128, KI), F32)
        self.aneg = self.sb("aneg", (128, KI, NS), F32)
        self.fcw = self.sb("fcw", (128, KH, 3), F32)
        self.fcb = self.sb("fcb", (128, KH), F32)
        self.xpw = self.sb("xpw", (128, KI, 144), BF16)
        self.sw1 = self.sb("sw1", (128, 1), F32)
        self.iw1 = self.sb("iw1", (128, 1), F32)
        self.sw2 = self.sb("sw2", (128, 1), F32)
        self.iw2 = self.sb("iw2", (128, 1), F32)
        self.rowbuf = self.sb("rowbuf", (128, T), F32)
        self.gm_row = self.rowbuf[0:1, 0:T]
        self.dtr_row = self.rowbuf[32:33, 0:T]
        self.is_row = self.rowbuf[64:65, 0:T]
        self.rowb16 = self.sb("rowb16", (128, 2 * T), BF16)
        self.sc2_rowb = self.rowb16[0:1, 0:T]
        self.iv2_rowb = self.rowb16[0:1, T:2 * T]
        self.is_rowb = self.rowb16[64:65, 0:T]
        self.smv = self.sb("smv", (128, 16), F32)
        self.smb = self.sb("smb", (128, 5 * c.TT_C), F32)
        self.itok = self.sb("itok", (128, c.NTT), F32)

        self.p32 = Pool(self.sb("p32", (128, 16640), F32), 16640)
        self.p16 = Pool(self.sb("p16", (128, 49408), BF16), 49408)
        self.pf16 = Pool(self.sb("pf16", (128, 8192), F16), 8192)
        pg = self.nc.psum_tensor("psum", [128, 4096], F32)
        self.psum = pg.__enter__()
        self._guards.append(pg)
        self.psum_b = self.psum[:, :].bitcast(BF16)

        self.sems = {}
        for name in ("bar", "qio", "qst", "qout", "eV", "eS", "eT", "eG"):
            g = nc.semaphore(name)
            h = g.__enter__()
            self._guards.append(g)
            self.sems[name] = Sem(h, name, dma=name.startswith("q"))
        self.bar = self.sems["bar"]
        self.qio = self.sems["qio"]
        self.qst = self.sems["qst"]
        self.qout = self.sems["qout"]
        self.dma_sems = [self.sems[n] for n in ("qio", "qst", "qout")]
        self.eV = self.sems["eV"]
        self.eS = self.sems["eS"]
        self.eT = self.sems["eT"]
        self.eG = self.sems["eG"]
        self.engine_sem = {"ve": self.eV, "se": self.eS, "pe": self.eT,
                           "ge": self.eG}

        self.phase0()
        self.barrier()
        for b in range(B):
            self.phase1(b)
            self.barrier()
            self.scan(b)
            self.barrier()
            self.phase34(b)
            self.barrier()
        if dbg:
            for name, t in (("d_xs", self.xs_d), ("d_z", self.z_d),
                            ("d_bm", self.bmat_d), ("d_cm", self.cmat_d),
                            ("d_y", self.y_d), ("d_x1", self.x1_d),
                            ("d_hs", self.hs_d), ("d_h2q", self.h2q_d)):
                o = nc.dram_tensor(name, list(t.shape), t.dtype,
                                   kind="ExternalOutput").ap()
                self.dma(o, t, sem=self.qout)
        self.emit("sy", lambda e: e.nop(), wait=[(self.qout, None)])

        with nc.Block() as block:
            block.sync(lambda e: [op(e) for op in self.ops["sy"]])
            block.vector(lambda e: [op(e) for op in self.ops["ve"]])
            block.scalar(lambda e: [op(e) for op in self.ops["se"]])
            block.tensor(lambda e: [op(e) for op in self.ops["pe"]])
            block.gpsimd(lambda e: [op(e) for op in self.ops["ge"]])

        for g in reversed(self._guards):
            g.__exit__(None, None, None)
        return nc

    def phase0(self):
        c, KD, KI, KH = self.c, self.c.KD, self.c.KI, self.c.KH
        D, DH, NS = self.c.D, self.c.DH, self.c.NS
        qio, qst = self.qio, self.qst
        eV, eS, eT, eG = self.eV, self.eS, self.eT, self.eG
        dma, ev, es, ep, eg = self.dma, self.ev, self.es, self.ep, self.eg

        dma(self.idf[:], self.idf_h)
        dma(self.idb[:], self.idb_h)
        for kc in range(KI):
            dma(self.mcw[:, kc, :], self.mcw_h[kc * 128:(kc + 1) * 128, 0, :])
            dma(self.aneg[:, kc, :], self.alog_h[kc * 128:(kc + 1) * 128, :])
        dma(self.mcb[:], self.mcb_h.rearrange("(c p) -> p c", p=128), slow=True)
        dma(self.dtw[:], self.dtw_h[0, :].rearrange("(c p) -> p c", p=128), slow=True)
        dma(self.dtb[:], self.dtb_h.rearrange("(c p) -> p c", p=128), slow=True)
        dma(self.dvec[:], self.dvec_h.rearrange("(c p) -> p c", p=128), slow=True)
        for kc in range(KH):
            dma(self.fcw[:, kc, :], self.fcw_h[kc * 128:(kc + 1) * 128, 0, :])
        dma(self.fcb[:], self.fcb_h.rearrange("(c p) -> p c", p=128), slow=True)
        ld0 = qio.n

        es(lambda e: e.activation(self.aneg[:], self.aneg[:], AF.Exp),
           wait=[(qio, ld0)], inc=(eS, 1))
        ev(lambda e: e.tensor_scalar(self.aneg[:], self.aneg[:], -1.0, None, ALU.mult),
           wait=[(eS, None)], inc=(eV, 1))

        self.p32.push()
        stg = self.p32.alloc(2 * NS + 1)
        for kc in range(KI):
            t = dma(stg, self.xpw_h[kc * 128:(kc + 1) * 128, :], wait=[(eV, None)])
            ev(lambda e, kc=kc: e.tensor_copy(self.xpw[:, kc, 0:2 * NS + 1], stg),
               wait=[(qio, t)], drain=True, inc=(eV, 1))
        self.p32.pop()

        DI2 = 2 * self.c.DI
        self.p32.push()
        self.p16.push()
        wstg = [self.p32.alloc(max(DI2, DH)) for _ in range(2)]
        wbig = self.p16.alloc(KD * DI2).rearrange("p (k m) -> p k m", k=KD)
        mstg = [self.p16.alloc(KI * 128) for _ in range(2)]
        cast_ev = [eV.n, eV.n]
        for kc in range(KD):
            pr = kc % 2
            t = dma(wstg[pr][:, :DI2], self.ipw_h[kc * 128:(kc + 1) * 128, :],
                    wait=[(eV, cast_ev[pr])])
            ev(lambda e, pr=pr, kc=kc: e.tensor_copy(wbig[:, kc, :],
                                                     wstg[pr][:, :DI2]),
               wait=[(qio, t)], inc=(eV, 1))
            cast_ev[pr] = eV.n
        for m in range(2 * KI):
            pr = m % 2
            ev(lambda e, pr=pr, m=m: e.tensor_copy(
                mstg[pr][:, :KD * 128],
                wbig[:, :, m * 128:(m + 1) * 128]),
               wait=[(eV, None), (qst, None)], inc=(eV, 1))
            dma(self.ipw_d[m], mstg[pr][:, :KD * 128], wait=[(eV, None)])
        cast_ev = [eV.n, eV.n]
        for kc in range(KI):
            pr = kc % 2
            t = dma(wstg[pr][:, :D], self.opw_h[kc * 128:(kc + 1) * 128, :],
                    wait=[(eV, cast_ev[pr])])
            ev(lambda e, pr=pr, kc=kc: e.tensor_copy(
                wbig.rearrange("p k m -> p (k m)")[:, kc * D:(kc + 1) * D],
                wstg[pr][:, :D]),
               wait=[(qio, t), (qst, None)], inc=(eV, 1))
            cast_ev[pr] = eV.n
        obig = wbig.rearrange("p k m -> p (k m)")[:, :KI * D].rearrange(
            "p (k m) -> p k m", k=KI)
        for m in range(KD):
            pr = m % 2
            ev(lambda e, pr=pr, m=m: e.tensor_copy(
                mstg[pr], obig[:, :, m * 128:(m + 1) * 128]),
               wait=[(eV, None), (qst, None)], inc=(eV, 1))
            dma(self.opw_d[m], mstg[pr], wait=[(eV, None)])

        acc = self.p32.alloc(4)
        ev(lambda e: e.memset(acc[:, 0:2], 0.0), inc=(eV, 1))
        for src_, col, rows, cols in ((self.w1_h, 0, KH, D), (self.w2_h, 1, KD, DH)):
            for r in range(rows):
                pr = r % 2
                t = dma(wstg[pr][:, :cols], src_[r * 128:(r + 1) * 128, :],
                        wait=[(eV, None)])
                ev(lambda e, pr=pr, col=col, cols=cols: e.reduce_sum(
                    acc[:, 2 + col:3 + col], wstg[pr][:, :cols], axis=AX.X,
                    apply_absolute_value=True),
                   wait=[(qio, t)], drain=True, inc=(eV, 1))
                ev(lambda e, col=col: e.tensor_add(
                    acc[:, col:col + 1], acc[:, col:col + 1], acc[:, 2 + col:3 + col]),
                   drain=True, inc=(eV, 1))
        ep(lambda e: e.transpose(self.psum[0:2, 0:128], acc[:, 0:2], self.idf[:]),
           wait=[(eV, None), (eS, None)], inc=(eT, 1))
        ev(lambda e: e.reduce_sum(acc[0:2, 2:3], self.psum[0:2, 0:128], axis=AX.X),
           wait=[(eT, None)], drain=True, inc=(eV, 1))
        scale = 1.0 / (DH * D)
        ev(lambda e: e.tensor_scalar(acc[0:2, 2:3], acc[0:2, 2:3], scale, 1e-5,
                                     ALU.mult, ALU.max), drain=True, inc=(eV, 1))
        ev(lambda e: e.tensor_scalar(acc[0:2, 3:4], acc[0:2, 2:3], 1.0 / 127.0, None,
                                     ALU.mult), drain=True, inc=(eV, 1))
        ev(lambda e: e.reciprocal(acc[0:2, 0:1], acc[0:2, 2:3]), drain=True,
           inc=(eV, 1))
        dma(self.scl_d[0:1, :], acc[0:2, 0:1], wait=[(eV, None)], slow=True)
        dma(self.scl_d[1:2, :], acc[0:2, 3:4], wait=[(eV, None)], slow=True)
        dma(self.sw1[:], self.scl_d[0:1, 0:1].broadcast_to([128, 1]), slow=True,
            wait=[(qst, None)])
        dma(self.sw2[:], self.scl_d[0:1, 1:2].broadcast_to([128, 1]), slow=True)
        dma(self.iw1[:], self.scl_d[1:2, 0:1].broadcast_to([128, 1]), slow=True)
        dma(self.iw2[:], self.scl_d[1:2, 1:2].broadcast_to([128, 1]), slow=True)

        self.pf16.push()
        f16w = self.pf16.alloc(max(D, DH))
        qb = [self.p16.alloc(max(D, DH)) for _ in range(2)]
        rslab = [self.p16.alloc(KD * 128) for _ in range(2)]
        w2big = wbig.rearrange("p k m -> p (k m)")[:, :KH * KD * 128].rearrange(
            "p (h m q) -> p h m q", h=KH, m=KD)
        for src_, rows, cols, scol, w2 in ((self.w1_h, KH, D, self.sw1, False),
                                           (self.w2_h, KD, DH, self.sw2, True)):
            for r in range(rows):
                pr = r % 2
                t = dma(wstg[pr][:, :cols], src_[r * 128:(r + 1) * 128, :],
                        wait=[(eV, None)])
                ev(lambda e, pr=pr, cols=cols, scol=scol: e.tensor_scalar(
                    wstg[pr][:, :cols], wstg[pr][:, :cols], scol[:, 0:1], None,
                    ALU.mult), wait=[(qio, t), (eV, None), (eT, None)], drain=True,
                   inc=(eV, 1))
                ev(lambda e, pr=pr, cols=cols: e.tensor_scalar(
                    wstg[pr][:, :cols], wstg[pr][:, :cols], 1.49, -1.49,
                    ALU.min, ALU.max), drain=True, inc=(eV, 1))
                ev(lambda e, pr=pr, cols=cols: e.tensor_scalar(
                    f16w[:, :cols], wstg[pr][:, :cols], MAGIC16, None,
                    ALU.add), drain=True, inc=(eV, 1))
                ev(lambda e, pr=pr, cols=cols: e.tensor_scalar(
                    qb[pr][:, :cols], f16w[:, :cols], MAGIC16, None,
                    ALU.subtract), drain=True, inc=(eV, 1))
                nblk = cols // 128
                for g0 in range(0, nblk, 4):
                    gn = min(4, nblk - g0)
                    for j in range(gn):
                        kc = g0 + j
                        self.ep(lambda e, pr=pr, kc=kc, j=j: e.transpose(
                            self.psum_b[:, j * 128:(j + 1) * 128],
                            qb[pr][:, kc * 128:(kc + 1) * 128], self.idb[:]),
                            wait=[(eV, None), (eS, None)],
                            inc=(eT, 1) if j == gn - 1 else None)
                    if w2:
                        es(lambda e, r=r, g0=g0, gn=gn: e.copy(
                            w2big[:, g0:g0 + gn, r, :],
                            self.psum_b[:, :gn * 128].rearrange(
                                "p (a q) -> p a q", a=gn)),
                           wait=[(eT, None)], inc=(eS, 1))
                    else:
                        es(lambda e, pr=pr, g0=g0, gn=gn: e.copy(
                            rslab[pr][:, g0 * 128:(g0 + gn) * 128],
                            self.psum_b[:, :gn * 128]),
                           wait=[(eT, None), (qst, None)], inc=(eS, 1))
                if not w2:
                    dma(self.w1t_d[r], rslab[pr], wait=[(eS, None)])
        for kh in range(KH):
            dma(self.w2t_d[kh],
                w2big.rearrange("p h m q -> p (h m q)")
                [:, kh * KD * 128:(kh + 1) * KD * 128],
                wait=[(eS, None)])
        self.p32.pop()
        self.p16.pop()
        self.pf16.pop()
        self.w_prep_tgt = self.qio.n

    def phase1(self, b):
        c = self.c
        KD, KI, T, D, NS, NFC, NTT = c.KD, c.KI, c.T, c.D, c.NS, c.NFC, c.NTT
        qio, qst = self.qio, self.qst
        eV, eS, eT, eG = self.eV, self.eS, self.eT, self.eG
        dma, ev, es, ep, eg = self.dma, self.ev, self.es, self.ep, self.eg
        ps = self.psum

        self.p32.push()
        self.p16.push()
        xT = self.p16.alloc(KD * T).rearrange("p (k t) -> p k t", k=KD)
        xtok = [self.p32.alloc(D) for _ in range(2)]
        self.warm()
        tr_ev = [0, 0]
        ev_ev = [0, 0]
        for tt in range(NTT):
            pr = tt % 2
            t = dma(xtok[pr], self.x_h[b, tt * 128:(tt + 1) * 128, :],
                    wait=[(eT, tr_ev[pr])])
            for g in range(KD // 4):
                for j in range(4):
                    kc = g * 4 + j
                    ep(lambda e, pr=pr, kc=kc, j=j: e.transpose(
                        ps[:, pr * 1024 + (kc % 4) * 128 + (kc // 4) * 512:
                           pr * 1024 + (kc % 4) * 128 + (kc // 4) * 512 + 128],
                        xtok[pr][:, kc * 128:(kc + 1) * 128], self.idf[:]),
                        wait=[(qio, t), (eS, ev_ev[pr])],
                        inc=(eT, 1) if (g == KD // 4 - 1 and j == 3) else None)
            tr_ev[pr] = eT.n
            for g in range(KD // 4):
                es(lambda e, pr=pr, g=g, tt=tt: e.copy(
                    xT[:, g * 4:(g + 1) * 4, tt * 128:(tt + 1) * 128],
                    ps[:, pr * 1024 + g * 512:pr * 1024 + (g + 1) * 512]
                    .rearrange("p (k t) -> p k t", k=4)),
                   wait=[(eT, None)], inc=(eS, 1))
            ev_ev[pr] = eS.n
        xT_done = eS.n

        xs_pad = [self.p16.alloc(T + 3) for _ in range(2)]
        xss = [self.p16.alloc(T) for _ in range(2)]
        xc = [self.p32.alloc(T) for _ in range(2)]
        bcacc = self.p32.alloc(T)
        wt = [self.p16.alloc(KD * 128) for _ in range(2)]
        zst = [self.p16.alloc(512) for _ in range(2)]
        wt_use = [0, 0]
        ip_ev = [0, 0]
        xss_use = [0, 0]
        silu_done = [0, 0]
        conv_done = [0, 0]
        xs_spill = [0, 0]
        zspill = [0, 0]
        zp_ev = [0, 0]
        dt_ev = [0, 0]
        bc_ev = [0, 0]
        ev(lambda e: e.memset(self.dtr_row[:], 0.0), inc=(eV, 1))

        def xp_block(dch):
            pr = dch % 2
            for fc in range(NFC):
                bpr = (dch * NFC + fc) % 2
                ep(lambda e, pr=pr, dch=dch, fc=fc, bpr=bpr: e.matmul(
                    ps[:, 2048 + bpr * 512:2048 + (bpr + 1) * 512],
                    self.xpw[:, dch, 0:128], xss[pr][:, fc * 512:(fc + 1) * 512],
                    start=True, stop=True),
                    wait=[(eS, silu_done[pr]), (eV, bc_ev[bpr])], inc=(eT, 1))
                if dch == 0:
                    ev(lambda e, fc=fc, bpr=bpr: e.tensor_copy(
                        bcacc[:, fc * 512:(fc + 1) * 512],
                        ps[:, 2048 + bpr * 512:2048 + (bpr + 1) * 512]),
                       wait=[(eT, None)], drain=True, inc=(eV, 1))
                else:
                    ev(lambda e, fc=fc, bpr=bpr: e.scalar_tensor_tensor(
                        bcacc[:, fc * 512:(fc + 1) * 512],
                        ps[:, 2048 + bpr * 512:2048 + (bpr + 1) * 512], 1.0,
                        bcacc[:, fc * 512:(fc + 1) * 512], ALU.mult, ALU.add),
                       wait=[(eT, None)], drain=True, inc=(eV, 1))
                bc_ev[bpr] = eV.n
                dpr = (dch * NFC + fc) % 2
                ep(lambda e, pr=pr, dch=dch, fc=fc, dpr=dpr: e.matmul(
                    ps[0:1, 1024 + dpr * 512:1024 + dpr * 512 + 512],
                    self.xpw[:, dch, 128:129], xss[pr][:, fc * 512:(fc + 1) * 512],
                    start=True, stop=True),
                    wait=[(eV, dt_ev[dpr])], inc=(eT, 1))
                ev(lambda e, fc=fc, dpr=dpr: e.scalar_tensor_tensor(
                    self.dtr_row[0:1, fc * 512:(fc + 1) * 512],
                    ps[0:1, 1024 + dpr * 512:1024 + dpr * 512 + 512], 1.0,
                    self.dtr_row[0:1, fc * 512:(fc + 1) * 512], ALU.mult, ALU.add),
                   wait=[(eT, None)], drain=True, inc=(eV, 1))
                dt_ev[dpr] = eV.n
            xss_use[pr] = eT.n

        zwt = [self.p16.alloc(KD * 128) for _ in range(2)]
        zwt_use = [0, 0]

        def z_block(mz):
            pr = mz % 2
            tW = dma(zwt[pr], self.ipw_d[KI + mz], wait=[(eT, zwt_use[pr])])
            for fc in range(NFC):
                fpr = fc % 2
                for kc in range(KD):
                    wl = [(qio, tW), (eS, zp_ev[fpr])]
                    if mz == 0 and fc == 1 and kc == 0:
                        wl.append((eT, None))
                    ep(lambda e, pr=pr, fpr=fpr, kc=kc, fc=fc: e.matmul(
                        ps[:, 3072 + fpr * 512:3072 + (fpr + 1) * 512],
                        zwt[pr][:, kc * 128:(kc + 1) * 128],
                        xT[:, kc, fc * 512:(fc + 1) * 512],
                        start=(kc == 0), stop=(kc == KD - 1)),
                        wait=wl,
                        inc=(eT, 1) if kc == KD - 1 else None)
                es(lambda e, fpr=fpr: e.copy(zst[fpr],
                                             ps[:, 3072 + fpr * 512:
                                                3072 + (fpr + 1) * 512]),
                   wait=[(eT, None), (qst, None)], inc=(eS, 1))
                zspill[fpr] = dma(self.z_d[b, mz, :, fc * 512:(fc + 1) * 512],
                                  zst[fpr], wait=[(eS, None)])
                zp_ev[fpr] = eS.n
            zwt_use[pr] = eT.n


        for dch in range(KI):
            pr = dch % 2
            tW = dma(wt[pr], self.ipw_d[dch],
                     wait=[(eT, wt_use[pr]), (qst, None)])
            ev(lambda e, pr=pr: e.memset(xs_pad[pr][:, 0:3], 0.0), inc=(eV, 1))
            pad_ev = eV.n
            for fc in range(NFC):
                fpr = fc % 2
                for kc in range(KD):
                    ep(lambda e, pr=pr, fpr=fpr, kc=kc, fc=fc: e.matmul(
                        ps[:, fpr * 512:(fpr + 1) * 512],
                        wt[pr][:, kc * 128:(kc + 1) * 128],
                        xT[:, kc, fc * 512:(fc + 1) * 512],
                        start=(kc == 0), stop=(kc == KD - 1)),
                        wait=[(qio, tW), (eS, ip_ev[fpr]), (eS, xT_done)],
                        inc=(eT, 1) if kc == KD - 1 else None)
                es(lambda e, pr=pr, fpr=fpr, fc=fc: e.copy(
                    xs_pad[pr][:, 3 + fc * 512:3 + (fc + 1) * 512],
                    ps[:, fpr * 512:(fpr + 1) * 512]),
                   wait=[(eT, None), (eV, conv_done[pr])], inc=(eS, 1))
                ip_ev[fpr] = eS.n
            wt_use[pr] = eT.n
            ev(lambda e, pr=pr, dch=dch: e.tensor_scalar(
                xc[pr], xs_pad[pr][:, 0:T], self.mcw[:, dch, 0:1], None, ALU.mult),
               wait=[(eS, None), (eV, pad_ev)], inc=(eV, 1))
            for k in range(1, 4):
                ev(lambda e, pr=pr, k=k, dch=dch: e.scalar_tensor_tensor(
                    xc[pr], xs_pad[pr][:, k:k + T], self.mcw[:, dch, k:k + 1],
                    xc[pr], ALU.mult, ALU.add), drain=True, inc=(eV, 1))
            es(lambda e, pr=pr, dch=dch: e.activation(
                xss[pr], xc[pr], AF.Silu, bias=self.mcb[:, dch:dch + 1], scale=1.0),
               wait=[(eV, None), (eT, xss_use[pr]), (qst, None)],
               inc=(eS, 1))
            silu_done[pr] = eS.n
            xs_spill[pr] = dma(self.xs_d[b, dch], xss[pr], wait=[(eS, None)])
            if dch > 0:
                xp_block(dch - 1)
            z_block(dch)
        xp_block(KI - 1)
        bst = self.p16.alloc(512)
        cst = self.p16.alloc(512)
        for fc in range(NFC):
            es(lambda e, fc=fc: e.copy(bst[0:64, :],
                                       bcacc[0:64, fc * 512:(fc + 1) * 512]),
               wait=[(eV, None), (qst, None)], inc=(eS, 1))
            es(lambda e, fc=fc: e.copy(cst[64:128, :],
                                       bcacc[64:128, fc * 512:(fc + 1) * 512]),
               wait=[(qst, None)], inc=(eS, 1))
            t1 = dma(self.bmat_d[b, :, fc * 512:(fc + 1) * 512], bst[0:64, :],
                     wait=[(eS, None)])
            t2 = dma(self.cmat_d[b, :, fc * 512:(fc + 1) * 512], cst[64:128, :])
        self.bc_tgt = self.qio.n
        self.p32.pop()
        self.p16.pop()

    def scan(self, b):
        c = self.c
        KI, T, NS, Tc = c.KI, c.T, c.NS, c.Tc
        NDH, DPH, GD, NDG, NG, NNG, HB = c.NDH, c.DPH, c.GD, c.NDG, c.NG, c.NNG, c.HB
        qio, qst = self.qio, self.qst
        eV, eS, eT, eG = self.eV, self.eS, self.eT, self.eG
        dma, ev, es, ep, eg = self.dma, self.ev, self.es, self.ep, self.eg
        ps = self.psum

        self.p32.push()
        self.p16.push()
        self.pf16.push()
        dtrep = self.p32.alloc(T)
        dtt = self.p32.alloc(GD * Tc).rearrange("p (d t) -> p d t", d=GD)
        spt = self.p32.alloc(Tc)
        yfull = self.p32.alloc(Tc)
        gz = self.p32.alloc(Tc)
        ST = self.p32.alloc(KI * NS)
        dtx = self.p16.alloc(GD * Tc).rearrange("p (d t) -> p d t", d=GD)
        xssl = self.p16.alloc(GD * Tc).rearrange("p (d t) -> p d t", d=GD)
        brep = self.p16.alloc(2 * NG * Tc).rearrange("p (r n t) -> p r n t", r=2, n=NG)
        crep = self.p16.alloc(2 * NG * Tc).rearrange("p (r n t) -> p r n t", r=2, n=NG)
        ubuf = self.p16.alloc(GD * Tc).rearrange("p (d t) -> p d t", d=GD)
        sbig = self.p16.alloc(GD * Tc).rearrange("p (d t) -> p d t", d=GD)
        scb = self.p16.alloc(2 * GD * Tc).rearrange("p (r d t) -> p r d t", r=2, d=GD)
        zsl = [self.p16.alloc(Tc) for _ in range(2)]
        ysl = [self.p16.alloc(Tc) for _ in range(2)]

        dma(self.row_d[0:1, :], self.dtr_row, wait=[(eV, None)])
        dma(dtrep, self.row_d[0:1, :].broadcast_to([128, T]), wait=[(qst, None)])
        dt_bcast = None

        bc_use = [0, 0]
        bc_useG = [0, 0]
        dab_use = [0, 0]
        scb_use = [0, 0]
        it = 0
        y_ev = 0
        ysl_dma = [0, 0]
        for tc in range(c.NTC):
            tcs = tc * Tc
            for dh in range(NDH):
                dch0 = dh * DPH
                for dg in range(NDG):
                    xs_t = []
                    for i in range(GD):
                        tx = dma(xssl[:, i, :],
                                 self.xs_d[b, dch0 + dg * GD + i, :, tcs:tcs + Tc],
                                 wait=[(eV, None), (qst, None)])
                        xs_t.append(tx)
                    for i in range(GD):
                        dch = dch0 + dg * GD + i
                        es(lambda e, i=i, dch=dch, tcs=tcs: e.activation(
                            spt, dtrep[:, tcs:tcs + Tc], AF.Abs,
                            scale=self.dtw[:, dch:dch + 1],
                            bias=self.dtb[:, dch:dch + 1]),
                           wait=[(qio, None), (eV, None)], drain=True, inc=(eS, 1))
                        es(lambda e: e.activation(spt, spt, AF.Exp, scale=-1.0),
                           drain=True, inc=(eS, 1))
                        es(lambda e: e.activation(spt, spt, AF.Ln, bias=1.0),
                           drain=True, inc=(eS, 1))
                        es(lambda e, i=i, dch=dch, tcs=tcs: e.activation(
                            dtt[:, i, :], dtrep[:, tcs:tcs + Tc], AF.Relu,
                            scale=self.dtw[:, dch:dch + 1],
                            bias=self.dtb[:, dch:dch + 1]),
                           drain=True, inc=(eS, 1))
                        ev(lambda e, i=i: e.tensor_add(dtt[:, i, :], dtt[:, i, :],
                                                       spt),
                           wait=[(eS, None)], drain=True, inc=(eV, 1))
                        eg(lambda e, i=i: e.tensor_mul(dtx[:, i, :], dtt[:, i, :],
                                                       xssl[:, i, :]),
                           wait=[(qio, xs_t[i]), (eV, None)], drain=True, inc=(eG, 1))
                    dt_built = eV.n
                    jds = list(range(GD))
                    for ng in range(NNG):
                        gpr = ng % 2
                        n0 = ng * NG
                        tB = dma(brep[:, gpr],
                                 self.bmat_d[b, n0:n0 + NG, tcs:tcs + Tc]
                                 .unsqueeze(0).broadcast_to([128, NG, Tc]),
                                 wait=[(eV, bc_use[gpr]), (eG, bc_useG[gpr]),
                                       (qst, None)])
                        tC = dma(crep[:, gpr],
                                 self.cmat_d[b, n0:n0 + NG, tcs:tcs + Tc]
                                 .unsqueeze(0).broadcast_to([128, NG, Tc]))
                        for ni in range(NG):
                            n = n0 + ni
                            pr = it % 2
                            for j in jds:
                                dch = dch0 + dg * GD + j
                                es(lambda e, pr=pr, j=j, dch=dch, n=n: e.activation(
                                    self.dab(pr, j), dtt[:, j, :], AF.Exp,
                                    scale=self.aneg[:, dch, n:n + 1]),
                                   wait=[(eV, dab_use[pr]), (eV, dt_built)],
                                   inc=(eS, 1) if j == jds[-1] else None)
                            ev(lambda e, gpr=gpr, ni=ni: e.tensor_mul(
                                ubuf[:, :, :],
                                dtx[:, :, :],
                                brep[:, gpr, ni, :].unsqueeze(1)
                                .broadcast_to([128, GD, Tc])),
                               wait=[(qio, tB), (eS, None), (eG, None)],
                               inc=(eV, 1))
                            for j in jds:
                                dch = dch0 + dg * GD + j
                                jj = j
                                if tc == 0:
                                    init = 0.0
                                else:
                                    init = ST[:, dch * NS + n:dch * NS + n + 1]
                                ev(lambda e, jj=jj, pr=pr, init=init: e.tensor_tensor_scan(
                                    sbig[:, jj, :], self.dab(pr, jj), ubuf[:, jj, :],
                                    init, ALU.mult, ALU.add),
                                   wait=[(eG, None)], drain=(jj == 0),
                                   inc=(eV, 1) if jj == GD - 1 else None)
                            dab_use[pr] = eV.n
                            ev(lambda e, pr=pr, gpr=gpr, ni=ni: e.tensor_mul(
                                scb[:, pr, :, :],
                                sbig[:, :, :],
                                crep[:, gpr, ni, :].unsqueeze(1)
                                .broadcast_to([128, GD, Tc])),
                               wait=[(qio, tC), (eT, scb_use[pr])], drain=True,
                               inc=(eV, 1))
                            if tc < c.NTC - 1:
                                eg(lambda e, dch0=dch0, dg=dg, n=n: e.tensor_copy(
                                    ST.rearrange("p (d n) -> p d n", n=NS)
                                    [:, dch0 + dg * GD:dch0 + (dg + 1) * GD, n:n + 1],
                                    sbig[:, :, Tc - 1:Tc]),
                                   wait=[(eV, None)], inc=(eG, 1))
                            sC_done = eV.n
                            first = (ng == 0 and ni == 0)
                            last = (ng == NNG - 1 and ni == NG - 1)
                            HBS = c.HBS
                            ybank = max(Tc, 512)
                            for j in range(GD):
                                for h in range(HB):
                                    ep(lambda e, j=j, h=h, pr=pr, first=first,
                                       last=last, HBS=HBS, ybank=ybank: e.matmul(
                                           ps[:, j * ybank + h * HBS:
                                              j * ybank + (h + 1) * HBS],
                                           self.idb[:],
                                           scb[:, pr, j, h * HBS:(h + 1) * HBS],
                                           start=first, stop=last),
                                       wait=[(eV, sC_done), (eV, y_ev)],
                                       inc=(eT, 1) if (j == GD - 1 and h == HB - 1)
                                       else None)
                            scb_use[pr] = eT.n
                            it += 1
                        bc_use[gpr] = eV.n
                        bc_useG[gpr] = eG.n
                    for j in range(GD):
                        dch = dch0 + dg * GD + j
                        zpr = j % 2
                        tz = dma(zsl[zpr], self.z_d[b, dch, :, tcs:tcs + Tc],
                                 wait=[(eS, None), (qst, None)])
                        ev(lambda e, j=j, dch=dch: e.scalar_tensor_tensor(
                            yfull, xssl[:, j, :],
                            self.dvec[:, dch:dch + 1],
                            ps[:, j * max(Tc, 512):j * max(Tc, 512) + Tc],
                            ALU.mult, ALU.add),
                           wait=[(eT, None)], drain=True, inc=(eV, 1))
                        es(lambda e, zpr=zpr: e.activation(gz, zsl[zpr], AF.Silu),
                           wait=[(qio, tz), (eV, None)], drain=True, inc=(eS, 1))
                        ev(lambda e, zpr=zpr: e.tensor_mul(ysl[zpr], yfull, gz),
                           wait=[(eS, None), (qst, None)], drain=True,
                           inc=(eV, 1))
                        ysl_dma[zpr] = dma(self.y_d[b, dch, :, tcs:tcs + Tc],
                                           ysl[zpr], wait=[(eV, None)])
                    y_ev = eV.n
        self.y_tgt = self.qio.n
        self.p32.pop()
        self.p16.pop()
        self.pf16.pop()

    def warm(self, n=24, wait=()):
        ps = self.psum
        for i in range(n):
            self.ep(lambda e, i=i, n=n: e.matmul(
                ps[:, 3968:4096], self.idb[:], self.idb[:],
                start=(i == 0), stop=(i == n - 1)),
                wait=([(self.eT, None)] + list(wait)) if i == 0 else (),
                inc=(self.eT, 1))

    def dab(self, pr, j):
        Tc, GD = self.c.Tc, self.c.GD
        off = (pr * GD + j) * Tc
        return self.pf16.t[:, off:off + Tc]

    def phase34(self, b):
        c = self.c
        KD, KI, KH, T, D, DH, Tc = c.KD, c.KI, c.KH, c.T, c.D, c.DH, c.Tc
        NFC, NTT, TT_C = c.NFC, c.NTT, c.TT_C
        qio, qst, qout = self.qio, self.qst, self.qout
        eV, eS, eT, eG = self.eV, self.eS, self.eT, self.eG
        dma, ev, es, ep, eg = self.dma, self.ev, self.es, self.ep, self.eg
        ps = self.psum
        smv = self.smv

        self.p32.push()
        self.p16.push()
        self.pf16.push()
        f16q = [self.pf16.alloc(D) for _ in range(2)]
        f16h = [self.pf16.alloc(T) for _ in range(2)]
        x1qT = self.p16.alloc(KD * T).rearrange("p (k t) -> p k t", k=KD)
        isrep = self.p16.alloc(T)
        r1b = [self.p32.alloc(D) for _ in range(2)]
        sqb = [self.p32.alloc(D) for _ in range(2)]
        xtk = [self.p32.alloc(D) for _ in range(2)]
        TG = min(8, TT_C)
        self.p32.push()
        r1big = self.p32.alloc(TG * D).rearrange("p (a t) -> p a t", a=TG)
        self.p16.push()
        Yb = self.p16.alloc(KI * Tc).rearrange("p (k t) -> p k t", k=KI)
        attn = self.p16.alloc(KD * Tc).rearrange("p (k t) -> p k t", k=KD)
        xqb = [self.p16.alloc(D) for _ in range(2)]
        wop = [self.p16.alloc(KI * 128) for _ in range(2)]

        def layer_norm(tile, pr, out, wait_extra=(), wait_out=()):
            ev(lambda e: e.reduce_sum(smv[:, 0:1], tile, axis=AX.X),
               wait=list(wait_extra), drain=True, inc=(eV, 1))
            es(lambda e, pr=pr: e.activation(sqb[pr], tile, AF.Square),
               wait=[(eV, None)], inc=(eS, 1))
            ev(lambda e, pr=pr: e.reduce_sum(smv[:, 1:2], sqb[pr], axis=AX.X),
               wait=[(eS, None)], drain=True, inc=(eV, 1))
            ev(lambda e: e.tensor_scalar(smv[:, 0:2], smv[:, 0:2], 1.0 / D, None,
                                         ALU.mult), drain=True, inc=(eV, 1))
            ev(lambda e: e.tensor_mul(smv[:, 2:3], smv[:, 0:1], smv[:, 0:1]),
               drain=True, inc=(eV, 1))
            ev(lambda e: e.tensor_sub(smv[:, 1:2], smv[:, 1:2], smv[:, 2:3]),
               drain=True, inc=(eV, 1))
            ev(lambda e: e.tensor_scalar(smv[:, 1:2], smv[:, 1:2], 1e-5, None,
                                         ALU.add), drain=True, inc=(eV, 1))
            ev(lambda e: e.reciprocal(smv[:, 2:3], smv[:, 1:2]), drain=True,
               inc=(eV, 1))
            es(lambda e: e.activation(smv[:, 2:3], smv[:, 2:3], AF.Sqrt),
               wait=[(eV, None)], drain=True, inc=(eS, 1))
            ev(lambda e: e.tensor_scalar(out, tile, smv[:, 0:1], smv[:, 2:3],
                                         ALU.subtract, ALU.mult),
               wait=[(eS, None)] + list(wait_out), drain=True, inc=(eV, 1))

        self.warm()
        wop_use = [0, 0]
        op_ev = [0, 0]
        ptok_use = [0, 0]
        xtk_use = [0, 0]
        pxq_ev = 0
        for tc in range(c.NTC):
            tcs = tc * Tc
            yt = 0
            for kc in range(KI):
                yt = dma(Yb[:, kc, :], self.y_d[b, kc, :, tcs:tcs + Tc],
                         wait=[(eT, None), (qst, None)])
            for m in range(KD):
                pr = m % 2
                tW = dma(wop[pr], self.opw_d[m], wait=[(eT, wop_use[pr])])
                for h in range(c.HB):
                    HBS = c.HBS
                    fpr = (m * c.HB + h) % 2
                    for kc in range(KI):
                        ep(lambda e, pr=pr, fpr=fpr, kc=kc, h=h, HBS=HBS: e.matmul(
                            ps[:, fpr * 512:fpr * 512 + HBS],
                            wop[pr][:, kc * 128:(kc + 1) * 128],
                            Yb[:, kc, h * HBS:(h + 1) * HBS],
                            start=(kc == 0), stop=(kc == KI - 1)),
                            wait=[(qio, max(tW, yt)), (eS, op_ev[fpr])],
                            inc=(eT, 1) if kc == KI - 1 else None)
                    es(lambda e, m=m, h=h, fpr=fpr, HBS=HBS: e.copy(
                        attn[:, m, h * HBS:(h + 1) * HBS],
                        ps[:, fpr * 512:fpr * 512 + HBS]),
                       wait=[(eT, None)], inc=(eS, 1))
                    op_ev[fpr] = eS.n
                wop_use[pr] = eT.n
            attn_done = eS.n
            TT = min(8, TT_C)
            for tg in range(0, TT_C, TT):
              for j in range(TT):
                tt = tg + j
                pr = tt % 2
                pt0 = 1024 + pr * 1024
                for m in range(KD):
                    ep(lambda e, m=m, tt=tt, pt0=pt0: e.transpose(
                        self.psum_b[:, 2 * pt0 + m * 128:2 * pt0 + (m + 1) * 128],
                        attn[:, m, tt * 128:(tt + 1) * 128], self.idb[:]),
                        wait=[(eS, attn_done), (eV, ptok_use[pr])],
                        inc=(eT, 1) if m == KD - 1 else None)
                tX = dma(xtk[pr], self.x_h[b, tcs + tt * 128:tcs + (tt + 1) * 128, :],
                         wait=[(eV, xtk_use[pr])])
                ev(lambda e, pr=pr, pt0=pt0, j=j: e.tensor_add(
                    r1big[:, j, :], self.psum_b[:, 2 * pt0:2 * pt0 + D], xtk[pr]),
                   wait=[(eT, None), (qio, tX), (qst, None)], drain=True,
                   inc=(eV, 1))
                ptok_use[pr] = eV.n
                xtk_use[pr] = eV.n
              smb = self.smb
              gs = tc * TT_C + tg
              ev(lambda e: e.reduce_sum(smb[:, 0:TT], r1big[:, :, :], axis=AX.X),
                 drain=True, inc=(eV, 1))
              for j in range(TT):
                  qr = j % 2
                  es(lambda e, qr=qr, j=j: e.activation(sqb[qr], r1big[:, j, :],
                                                        AF.Square),
                     wait=[(eV, None)], inc=(eS, 1))
                  ev(lambda e, qr=qr, j=j: e.reduce_sum(
                      smb[:, TT + j:TT + j + 1], sqb[qr], axis=AX.X),
                     wait=[(eS, None)], drain=True, inc=(eV, 1))
              ev(lambda e: e.tensor_scalar(smb[:, 0:2 * TT], smb[:, 0:2 * TT],
                                           1.0 / D, None, ALU.mult),
                 drain=True, inc=(eV, 1))
              ev(lambda e: e.tensor_mul(smb[:, 2 * TT:3 * TT], smb[:, 0:TT],
                                        smb[:, 0:TT]), drain=True, inc=(eV, 1))
              ev(lambda e: e.tensor_sub(smb[:, TT:2 * TT], smb[:, TT:2 * TT],
                                        smb[:, 2 * TT:3 * TT]), drain=True,
                 inc=(eV, 1))
              ev(lambda e: e.tensor_scalar(smb[:, TT:2 * TT], smb[:, TT:2 * TT],
                                           1e-5, None, ALU.add), drain=True,
                 inc=(eV, 1))
              ev(lambda e: e.reciprocal(smb[:, 2 * TT:3 * TT], smb[:, TT:2 * TT]),
                 drain=True, inc=(eV, 1))
              es(lambda e: e.activation(smb[:, 2 * TT:3 * TT],
                                        smb[:, 2 * TT:3 * TT], AF.Sqrt),
                 wait=[(eV, None)], drain=True, inc=(eS, 1))
              for j in range(TT):
                  tt = tg + j
                  ev(lambda e, j=j: e.tensor_scalar(
                      r1big[:, j, :], r1big[:, j, :], smb[:, j:j + 1],
                      smb[:, 2 * TT + j:2 * TT + j + 1], ALU.subtract, ALU.mult),
                     wait=[(eS, None)], drain=True, inc=(eV, 1))
                  dma(self.x1_d[b, tcs + tt * 128:tcs + (tt + 1) * 128, :],
                      r1big[:, j, :], wait=[(eV, None)])
              ev(lambda e: e.reduce_max(smb[:, 3 * TT:4 * TT], r1big[:, :, :],
                                        axis=AX.X, apply_absolute_value=True),
                 drain=True, inc=(eV, 1))
              ev(lambda e: e.tensor_scalar(smb[:, 3 * TT:4 * TT],
                                           smb[:, 3 * TT:4 * TT], 1e-5, None,
                                           ALU.max), drain=True, inc=(eV, 1))
              ev(lambda e: e.reciprocal(smb[:, 4 * TT:5 * TT],
                                        smb[:, 3 * TT:4 * TT]),
                 drain=True, inc=(eV, 1))
              ev(lambda e: e.tensor_scalar(smb[:, 4 * TT:5 * TT],
                                           smb[:, 4 * TT:5 * TT], 127.0, None,
                                           ALU.mult), drain=True, inc=(eV, 1))
              ev(lambda e, gs=gs, TT=TT: e.tensor_mul(
                  self.itok[:, gs:gs + TT], smb[:, 3 * TT:4 * TT],
                  self.iw1[:].broadcast_to([128, TT])), drain=True, inc=(eV, 1))
              for j in range(TT):
                  pr = j % 2
                  gtt = gs + j
                  ev(lambda e, pr=pr, j=j: e.tensor_scalar(
                      f16q[pr], r1big[:, j, :],
                      smb[:, 4 * TT + j:4 * TT + j + 1],
                      None, ALU.mult), drain=True, inc=(eV, 1))
                  ev(lambda e, pr=pr: e.tensor_scalar(f16q[pr], f16q[pr],
                                                      MAGIC16, None, ALU.add),
                     drain=True, inc=(eV, 1))
                  ev(lambda e, pr=pr: e.tensor_scalar(xqb[pr], f16q[pr],
                                                      MAGIC16, None, ALU.subtract),
                     drain=True, inc=(eV, 1))
                  for kc in range(KD):
                      ep(lambda e, pr=pr, kc=kc: e.transpose(
                          self.psum_b[:, 6144 + kc * 128:6144 + (kc + 1) * 128],
                          xqb[pr][:, kc * 128:(kc + 1) * 128], self.idb[:]),
                          wait=[(eV, None), (eS, pxq_ev)],
                          inc=(eT, 1) if kc == KD - 1 else None)
                  es(lambda e, gtt=gtt: e.copy(
                      x1qT[:, :, gtt * 128:(gtt + 1) * 128],
                      self.psum_b[:, 6144:6144 + D]
                      .rearrange("p (k t) -> p k t", k=KD)),
                     wait=[(eT, None)], inc=(eS, 1))
                  pxq_ev = eS.n
        dma(self.itok_d, self.itok[:, 0:NTT], wait=[(eV, None)])
        dma(self.is_row[0:1, :],
            self.itok_d.transpose([1, 0]).unsqueeze(0),
            wait=[(qst, None)], slow=True)
        ev(lambda e: e.tensor_copy(self.is_rowb, self.is_row),
           wait=[(qio, None)], drain=True, inc=(eV, 1))
        dma(self.rowb_d[0:1, :], self.is_rowb, wait=[(eV, None)])
        dma(isrep, self.rowb_d[0:1, :].broadcast_to([128, T]), wait=[(qst, None)])
        is_done = None

        self.p32.pop()
        self.p16.pop()
        hst = [self.p16.alloc(T) for _ in range(2)]
        w1t = [self.p16.alloc(KD * 128) for _ in range(2)]
        h2r = [self.p16.alloc(512) for _ in range(2)]
        sc2rep = self.p16.alloc(T)
        iv2rep = self.p16.alloc(T)
        w2t = [self.p16.alloc(KD * 128) for _ in range(2)]
        h2s = [self.p16.alloc(512) for _ in range(2)]
        outb = [self.p32.alloc(D) for _ in range(2)]
        self.p32.push()
        hpad = [self.p32.alloc(T + 2) for _ in range(2)]
        hcv = [self.p32.alloc(T) for _ in range(2)]

        gtok = self.p32.alloc(c.NTT)
        tmax = self.p32.alloc(c.NTT)
        ev(lambda e: e.memset(gtok, 0.0), inc=(eV, 1))
        x1qT_done = pxq_ev
        self.warm(wait=[(self.eS, x1qT_done)])
        gm_ev = [0, 0]
        w1_use = [0, 0]
        h_ev = [0, 0]
        hc_use = [0, 0]
        hs_use = [0, 0]
        for mh in range(KH):
            pr = mh % 2
            tW = dma(w1t[pr], self.w1t_d[mh], wait=[(eT, w1_use[pr])])
            ev(lambda e, pr=pr: e.memset(hpad[pr][:, 0:1], 0.0),
               wait=[(eV, hc_use[pr])], inc=(eV, 1))
            ev(lambda e, pr=pr: e.memset(hpad[pr][:, T + 1:T + 2], 0.0), inc=(eV, 1))
            pad_ev = eV.n
            for fc in range(NFC):
                fpr = fc % 2
                for kc in range(KD):
                    ep(lambda e, pr=pr, fpr=fpr, kc=kc, fc=fc: e.matmul(
                        ps[:, fpr * 512:(fpr + 1) * 512],
                        w1t[pr][:, kc * 128:(kc + 1) * 128],
                        x1qT[:, kc, fc * 512:(fc + 1) * 512],
                        start=(kc == 0), stop=(kc == KD - 1)),
                        wait=[(qio, tW), (eV, h_ev[fpr]), (eS, x1qT_done)],
                        inc=(eT, 1) if kc == KD - 1 else None)
                ev(lambda e, pr=pr, fc=fc, fpr=fpr: e.tensor_mul(
                    hpad[pr][:, 1 + fc * 512:1 + (fc + 1) * 512],
                    ps[:, fpr * 512:(fpr + 1) * 512],
                    isrep[:, fc * 512:(fc + 1) * 512]),
                   wait=[(eT, None), (qio, None), (eV, pad_ev)], drain=True,
                   inc=(eV, 1))
                h_ev[fpr] = eV.n
            w1_use[pr] = eT.n
            ev(lambda e, pr=pr, mh=mh: e.tensor_scalar(
                hcv[pr], hpad[pr][:, 0:T], self.fcw[:, mh, 0:1], None, ALU.mult),
               drain=True, inc=(eV, 1))
            for k in range(1, 3):
                ev(lambda e, pr=pr, mh=mh, k=k: e.scalar_tensor_tensor(
                    hcv[pr], hpad[pr][:, k:k + T], self.fcw[:, mh, k:k + 1],
                    hcv[pr], ALU.mult, ALU.add), drain=True, inc=(eV, 1))
            es(lambda e, pr=pr, mh=mh: e.activation(
                hst[pr], hcv[pr], AF.Silu, bias=self.fcb[:, mh:mh + 1], scale=1.0),
               wait=[(eV, None), (qst, None)], drain=True, inc=(eS, 1))
            hc_use[pr] = eS.n
            dma(self.hs_d[b, mh], hst[pr], wait=[(eS, None)])
            hs_use[pr] = self.qio.n
            ab0 = 4096 + (mh % 2) * c.NTT * 128
            for tt in range(c.NTT):
                ep(lambda e, pr=pr, tt=tt, ab0=ab0: e.transpose(
                    self.psum_b[:, ab0 + tt * 128:ab0 + (tt + 1) * 128],
                    hst[pr][:, tt * 128:(tt + 1) * 128], self.idb[:]),
                    wait=[(eS, None), (eV, gm_ev[mh % 2])],
                    inc=(eT, 1) if tt == c.NTT - 1 else None)
            ev(lambda e, ab0=ab0: e.reduce_max(
                tmax, self.psum_b[:, ab0:ab0 + c.NTT * 128]
                .rearrange("p (a b) -> p a b", a=c.NTT), axis=AX.X,
                apply_absolute_value=True),
               wait=[(eT, None)], drain=True, inc=(eV, 1))
            ev(lambda e: e.tensor_max(gtok, gtok, tmax), drain=True, inc=(eV, 1))
            gm_ev[mh % 2] = eV.n
        self.p32.pop()
        self.p32.push()
        sc2f = self.p32.alloc(T)[0:1, :]
        dma(self.gtok_d, gtok, wait=[(eV, None)])
        dma(self.gm_row, self.gtok_d.transpose([1, 0]).unsqueeze(0),
            wait=[(qst, None)], slow=True)
        ev(lambda e: e.tensor_scalar(self.gm_row, self.gm_row, 1e-5, None,
                                     ALU.max), wait=[(qio, None)], drain=True,
           inc=(eV, 1))
        ev(lambda e: e.reciprocal(sc2f, self.gm_row), drain=True,
           inc=(eV, 1))
        ev(lambda e: e.tensor_scalar(sc2f, sc2f, 127.0, None,
                                     ALU.mult), drain=True, inc=(eV, 1))
        ev(lambda e: e.tensor_scalar(self.iv2_rowb, self.gm_row,
                                     self.iw2[0:1, 0:1], None, ALU.mult),
           drain=True, inc=(eV, 1))
        ev(lambda e: e.tensor_copy(self.sc2_rowb, sc2f),
           drain=True, inc=(eV, 1))
        dma(self.rowb_d[1:2, :], self.sc2_rowb, wait=[(eV, None)])
        dma(self.rowb_d[2:3, :], self.iv2_rowb)
        dma(sc2rep, self.rowb_d[1:2, :].broadcast_to([128, T]), wait=[(qst, None)])
        dma(iv2rep, self.rowb_d[2:3, :].broadcast_to([128, T]))
        self.p32.pop()
        self.p32.push()
        ffn_f = self.p32.alloc(KD * 512).rearrange("p (m t) -> p m t", m=KD)
        qv = [self.p32.alloc(512) for _ in range(2)]
        ffn_ev = 0
        fc_clear = 0
        pt_use = [0, 0]
        w2_use = [0, 0]
        self.warm(wait=[(self.eV, None)])
        for fc in range(NFC):
            for kh in range(KH):
                pr = kh % 2
                th = dma(h2r[pr], self.hs_d[b, kh, :, fc * 512:(fc + 1) * 512],
                         wait=[(eT, w2_use[pr]), (qst, None)])
                tW = dma(w2t[pr], self.w2t_d[kh])
                ev(lambda e, pr=pr, fc=fc: e.tensor_mul(
                    qv[pr], h2r[pr], sc2rep[:, fc * 512:(fc + 1) * 512]),
                   wait=[(qio, th), (eT, w2_use[pr])], drain=True, inc=(eV, 1))
                ev(lambda e, pr=pr: e.tensor_scalar(
                    f16h[pr][:, 0:512], qv[pr], MAGIC16, None, ALU.add),
                   drain=True, inc=(eV, 1))
                ev(lambda e, pr=pr: e.tensor_scalar(
                    h2s[pr], f16h[pr][:, 0:512], MAGIC16, None, ALU.subtract),
                   drain=True, inc=(eV, 1))
                hq_done = eV.n
                for m in range(KD):
                    wl = [(qio, tW), (eV, hq_done), (eV, fc_clear)]
                    if fc == 0 and kh == 0 and m == KD - 1:
                        wl.append((eT, None))
                    ep(lambda e, pr=pr, m=m, kh=kh: e.matmul(
                        ps[:, m * 512:(m + 1) * 512],
                        w2t[pr][:, m * 128:(m + 1) * 128], h2s[pr],
                        start=(kh == 0), stop=(kh == KH - 1)),
                        wait=wl,
                        inc=(eT, 1) if m == KD - 1 else None)
                w2_use[pr] = eT.n
            for m in range(KD):
                ev(lambda e, m=m, fc=fc: e.tensor_mul(
                    ffn_f[:, m, :], ps[:, m * 512:(m + 1) * 512],
                    iv2rep[:, fc * 512:(fc + 1) * 512]),
                   wait=[(eT, None)], drain=True, inc=(eV, 1))
            ffn_ev = eV.n
            for tt in range(4):
                pr = tt % 2
                rows = fc * 512 + tt * 128
                for m in range(KD):
                    ep(lambda e, m=m, tt=tt, pr=pr: e.transpose(
                        ps[:, pr * 1024 + m * 128:pr * 1024 + (m + 1) * 128],
                        ffn_f[:, m, tt * 128:(tt + 1) * 128], self.idf[:]),
                        wait=[(eV, ffn_ev), (eV, pt_use[pr])],
                        inc=(eT, 1) if m == KD - 1 else None)
                tX = dma(xtk[pr], self.x1_d[b, rows:rows + 128, :],
                         wait=[(eV, xtk_use[pr]), (qst, None)])
                ev(lambda e, pr=pr: e.tensor_add(r1b[pr], ps[:, pr * 1024:pr * 1024 + D],
                                                 xtk[pr]),
                   wait=[(eT, None), (qio, tX)], drain=True, inc=(eV, 1))
                pt_use[pr] = eV.n
                xtk_use[pr] = eV.n
                layer_norm(r1b[pr], pr, outb[pr], wait_out=[(qout, None)])
                dma(self.out_h[b, rows:rows + 128, :], outb[pr],
                    wait=[(eV, None)], sem=qout)
            fc_clear = eV.n
        self.p32.pop()
        self.p32.pop()
        self.p16.pop()
        self.pf16.pop()


def build_program(cfg, dbg=False):
    return Builder(cfg).build(dbg=dbg)


def kernel(**inputs):
    import ml_dtypes
    cfg = Cfg(small=False)
    x = np.ascontiguousarray(np.asarray(inputs["L0_attn_x"], dtype=np.float32))
    n_cores = 8
    shards = np.split(x, n_cores, axis=0)
    base = {k: np.ascontiguousarray(np.asarray(v, dtype=np.float32))
            for k, v in inputs.items() if k != "L0_attn_x"}
    ident_f = np.eye(128, dtype=np.float32)
    ident_b = np.eye(128, dtype=ml_dtypes.bfloat16)
    nc = build_program(cfg)
    in_maps = []
    for i in range(n_cores):
        m = dict(base)
        m["L0_attn_x"] = shards[i]
        m["ident_f32"] = ident_f
        m["ident_bf16"] = ident_b
        in_maps.append(m)
    res = run_bass_kernel_spmd(nc, in_maps, core_ids=list(range(n_cores)))
    return np.concatenate([r["out"] for r in res.results], axis=0)


# version 25
# speedup vs baseline: 1.0346x; 1.0030x over previous
import numpy as np

import concourse.bass as bass
import concourse.mybir as mybir
import concourse.library_config as libcfg
from concourse.bass_isa_rust import ReduceOp
from concourse.bass_utils import run_bass_kernel_spmd

F32 = mybir.dt.float32
BF16 = mybir.dt.bfloat16
F16 = mybir.dt.float16
ALU = mybir.AluOpType
AF = mybir.ActivationFunctionType
AX = mybir.AxisListType

MAGIC = float(np.float32(3 * 2 ** 22))
MAGIC16 = 1536.0


class Cfg:
    def __init__(self, small=False):
        if small:
            self.B, self.T, self.D, self.DI, self.DH, self.Tc = 1, 512, 512, 512, 1024, 256
        else:
            self.B, self.T, self.D, self.DI, self.DH, self.Tc = 2, 2048, 1024, 2048, 4096, 1024
        self.NS = 64
        self.KD = self.D // 128
        self.KI = self.DI // 128
        self.KH = self.DH // 128
        self.NTC = self.T // self.Tc
        self.NFC = self.T // 512
        self.NTT = self.T // 128
        self.TT_C = self.Tc // 128
        self.NDH = 2
        self.DPH = self.KI // self.NDH
        self.GD = min(4, self.DPH)
        self.NDG = self.DPH // self.GD
        self.NG = 4
        self.NNG = self.NS // self.NG
        self.HBS = min(512, self.Tc)
        self.HB = self.Tc // self.HBS
        assert self.GD * self.Tc * 4 <= 8 * 2048


ENGINES = ("pe", "ve", "se", "ge", "sy")


class Sem:
    __slots__ = ("h", "n", "name", "dma", "last_waited", "boundary")

    def __init__(self, h, name, dma=False):
        self.h, self.n, self.name, self.dma = h, 0, name, dma
        self.last_waited = 0
        self.boundary = 0

    def bump(self, k):
        self.n += k
        return self.n


class Pool:
    def __init__(self, t, elems):
        self.t, self.elems, self.off, self.marks = t, elems, 0, []

    def alloc(self, n):
        assert self.off + n <= self.elems, f"pool overflow: {self.off}+{n}>{self.elems}"
        o = self.off
        self.off += n
        return self.t[:, o:o + n]

    def push(self):
        self.marks.append(self.off)

    def pop(self):
        self.off = self.marks.pop()


class Builder:
    def __init__(self, cfg):
        self.c = cfg
        self.nc = bass.Bass("TRN2", target_bir_lowering=False)
        self.ops = {e: [] for e in ENGINES}
        self.lastwait = {e: {} for e in ENGINES}
        self._guards = []
        self.dma_sems = []

    def emit(self, e, fn, wait=(), drain=False, inc=None):
        if drain:
            sem = getattr(self, "engine_sem", {}).get(e)
            if sem is not None:
                wait = list(wait) + [(sem, sem.n)]
                drain = False
        wl = []
        for sem, cnt in wait:
            if cnt is None or sem.dma:
                cnt = sem.n
            if sem.dma:
                sem.last_waited = max(sem.last_waited, cnt)
            if cnt > self.lastwait[e].get(id(sem), -1):
                wl.append((sem, cnt))
                self.lastwait[e][id(sem)] = cnt
        if e == "sy" and wl:
            for s in self.dma_sems:
                s.boundary = s.n
        if inc is not None:
            inc[0].bump(inc[1])

        def op(eng, wl=wl, drain=drain, fn=fn, inc=inc):
            for sem, cnt in wl:
                eng.wait_ge(sem.h, cnt)
            if drain:
                eng.drain()
            r = fn(eng)
            if inc is not None:
                r.then_inc(inc[0].h, inc[1])

        self.ops[e].append(op)

    def barrier(self):
        b = self.bar
        tgt = b.n + len(ENGINES)
        b.n = tgt
        dwaits = [(s.h, s.n) for s in self.dma_sems if s.n > 0]
        for s in self.dma_sems:
            s.last_waited = max(s.last_waited, s.n)
            s.boundary = s.n
        for e in ENGINES:
            def op(eng, tgt=tgt, b=b, dwaits=dwaits):
                for h, n in dwaits:
                    eng.wait_ge(h, n)
                eng.drain()
                eng.sem_inc(b.h, 1)
                eng.wait_ge(b.h, tgt)
            self.ops[e].append(op)
            self.lastwait[e] = {}

    def dma(self, dst, src, wait=(), sem=None, slow=False):
        if sem is None:
            try:
                is_store = "DRAM" in str(dst.space)
            except Exception:
                is_store = False
            sem = self.qst if is_store else self.qio
        wait = list(wait)
        if sem.last_waited > sem.boundary:
            wait.append((sem, sem.last_waited))
            self.lastwait["sy"].pop(id(sem), None)
        self.emit("sy", lambda e, d=dst, s=src, sl=slow: e.dma_start(
            out=d, in_=s, allow_slow_non_contiguous=sl),
                  wait=wait, inc=(sem, 16))
        return sem.n

    def ev(self, fn, **kw):
        self.emit("ve", fn, **kw)

    def es(self, fn, **kw):
        self.emit("se", fn, **kw)

    def ep(self, fn, **kw):
        self.emit("pe", fn, **kw)

    def eg(self, fn, **kw):
        self.emit("ge", fn, **kw)

    def sb(self, name, shape, dtype):
        g = self.nc.sbuf_tensor(name, list(shape), dtype)
        t = g.__enter__()
        self._guards.append(g)
        return t

    def build(self, dbg=False):
        nc, c = self.nc, self.c
        B, T, D, DI, DH, NS = c.B, c.T, c.D, c.DI, c.DH, c.NS
        KD, KI, KH, Tc = c.KD, c.KI, c.KH, c.Tc

        din = lambda n, s: nc.dram_tensor(n, list(s), F32, kind="ExternalInput").ap()
        self.x_h = din("L0_attn_x", (B, T, D))
        self.ipw_h = din("L0_attn_in_proj_w", (D, 2 * DI))
        self.mcw_h = din("L0_attn_conv_w", (DI, 1, 4))
        self.mcb_h = din("L0_attn_conv_b", (DI,))
        self.xpw_h = din("L0_attn_x_proj_w", (DI, 2 * NS + 1))
        self.dtw_h = din("L0_attn_dt_w", (1, DI))
        self.dtb_h = din("L0_attn_dt_b", (DI,))
        self.alog_h = din("L0_attn_A_log", (DI, NS))
        self.dvec_h = din("L0_attn_D", (DI,))
        self.opw_h = din("L0_attn_out_proj_w", (DI, D))
        self.w1_h = din("L0_ffn_w1", (DH, D))
        self.fcw_h = din("L0_ffn_conv_w", (DH, 1, 3))
        self.fcb_h = din("L0_ffn_conv_b", (DH,))
        self.w2_h = din("L0_ffn_w2", (D, DH))
        self.idf_h = din("ident_f32", (128, 128))
        self.idb_h = nc.dram_tensor("ident_bf16", [128, 128], BF16,
                                    kind="ExternalInput").ap()
        self.out_h = nc.dram_tensor("out", [B, T, D], F32, kind="ExternalOutput").ap()

        scr = lambda n, s, d: nc.dram_tensor(n, list(s), d, kind="Internal").ap()
        self.ipw_d = scr("ipw_bf", (2 * KI, 128, KD * 128), BF16)
        self.opw_d = scr("opw_bf", (KD, 128, KI * 128), BF16)
        self.w1t_d = scr("w1qT", (KH, 128, KD * 128), BF16)
        self.w2t_d = scr("w2qT", (KH, 128, KD * 128), BF16)
        self.z_d = scr("z_dram", (B, KI, 128, T), BF16)
        self.xs_d = scr("xs_dram", (B, KI, 128, T), BF16)
        self.bmat_d = scr("bmat_dram", (B, NS, T), BF16)
        self.cmat_d = scr("cmat_dram", (B, NS, T), BF16)
        self.y_d = scr("y_dram", (B, KI, 128, T), BF16)
        self.x1_d = scr("x1_dram", (B, T, D), F32)
        self.hs_d = scr("hs_dram", (B, KH, 128, T), BF16)
        self.h2q_d = scr("h2q_dram", (B, KH, 128, T), BF16)
        self.itok_d = scr("itok_dram", (128, c.NTT), F32)
        self.gtok_d = scr("gtok_dram", (128, c.NTT), F32)
        self.row_d = scr("row_dram", (2, T), F32)
        self.rowb_d = scr("rowb_dram", (4, T), BF16)
        self.scl_d = scr("scl_dram", (2, 2), F32)

        self.idf = self.sb("idf", (128, 128), F32)
        self.idb = self.sb("idb", (128, 128), BF16)
        self.mcw = self.sb("mcw", (128, KI, 4), F32)
        self.mcb = self.sb("mcb", (128, KI), F32)
        self.dtw = self.sb("dtw", (128, KI), F32)
        self.dtb = self.sb("dtb", (128, KI), F32)
        self.dvec = self.sb("dvec", (128, KI), F32)
        self.aneg = self.sb("aneg", (128, KI, NS), F32)
        self.fcw = self.sb("fcw", (128, KH, 3), F32)
        self.fcb = self.sb("fcb", (128, KH), F32)
        self.xpw = self.sb("xpw", (128, KI, 144), BF16)
        self.sw1 = self.sb("sw1", (128, 1), F32)
        self.iw1 = self.sb("iw1", (128, 1), F32)
        self.sw2 = self.sb("sw2", (128, 1), F32)
        self.iw2 = self.sb("iw2", (128, 1), F32)
        self.rowbuf = self.sb("rowbuf", (128, T), F32)
        self.gm_row = self.rowbuf[0:1, 0:T]
        self.dtr_row = self.rowbuf[32:33, 0:T]
        self.is_row = self.rowbuf[64:65, 0:T]
        self.rowb16 = self.sb("rowb16", (128, 2 * T), BF16)
        self.sc2_rowb = self.rowb16[0:1, 0:T]
        self.iv2_rowb = self.rowb16[0:1, T:2 * T]
        self.is_rowb = self.rowb16[64:65, 0:T]
        self.smv = self.sb("smv", (128, 16), F32)
        self.smb = self.sb("smb", (128, 5 * c.TT_C), F32)
        self.itok = self.sb("itok", (128, c.NTT), F32)

        self.p32 = Pool(self.sb("p32", (128, 16640), F32), 16640)
        self.p16 = Pool(self.sb("p16", (128, 49408), BF16), 49408)
        self.pf16 = Pool(self.sb("pf16", (128, 8192), F16), 8192)
        pg = self.nc.psum_tensor("psum", [128, 4096], F32)
        self.psum = pg.__enter__()
        self._guards.append(pg)
        self.psum_b = self.psum[:, :].bitcast(BF16)

        self.sems = {}
        for name in ("bar", "qio", "qst", "qout", "eV", "eS", "eT", "eG"):
            g = nc.semaphore(name)
            h = g.__enter__()
            self._guards.append(g)
            self.sems[name] = Sem(h, name, dma=name.startswith("q"))
        self.bar = self.sems["bar"]
        self.qio = self.sems["qio"]
        self.qst = self.sems["qst"]
        self.qout = self.sems["qout"]
        self.dma_sems = [self.sems[n] for n in ("qio", "qst", "qout")]
        self.eV = self.sems["eV"]
        self.eS = self.sems["eS"]
        self.eT = self.sems["eT"]
        self.eG = self.sems["eG"]
        self.engine_sem = {"ve": self.eV, "se": self.eS, "pe": self.eT,
                           "ge": self.eG}

        self.phase0()
        self.barrier()
        for b in range(B):
            self.phase1(b)
            self.barrier()
            self.scan(b)
            self.barrier()
            self.phase34(b)
            self.barrier()
        if dbg:
            for name, t in (("d_xs", self.xs_d), ("d_z", self.z_d),
                            ("d_bm", self.bmat_d), ("d_cm", self.cmat_d),
                            ("d_y", self.y_d), ("d_x1", self.x1_d),
                            ("d_hs", self.hs_d), ("d_h2q", self.h2q_d)):
                o = nc.dram_tensor(name, list(t.shape), t.dtype,
                                   kind="ExternalOutput").ap()
                self.dma(o, t, sem=self.qout)
        self.emit("sy", lambda e: e.nop(), wait=[(self.qout, None)])

        with nc.Block() as block:
            block.sync(lambda e: [op(e) for op in self.ops["sy"]])
            block.vector(lambda e: [op(e) for op in self.ops["ve"]])
            block.scalar(lambda e: [op(e) for op in self.ops["se"]])
            block.tensor(lambda e: [op(e) for op in self.ops["pe"]])
            block.gpsimd(lambda e: [op(e) for op in self.ops["ge"]])

        for g in reversed(self._guards):
            g.__exit__(None, None, None)
        return nc

    def phase0(self):
        c, KD, KI, KH = self.c, self.c.KD, self.c.KI, self.c.KH
        D, DH, NS = self.c.D, self.c.DH, self.c.NS
        qio, qst = self.qio, self.qst
        eV, eS, eT, eG = self.eV, self.eS, self.eT, self.eG
        dma, ev, es, ep, eg = self.dma, self.ev, self.es, self.ep, self.eg

        dma(self.idf[:], self.idf_h)
        dma(self.idb[:], self.idb_h)
        for kc in range(KI):
            dma(self.mcw[:, kc, :], self.mcw_h[kc * 128:(kc + 1) * 128, 0, :])
            dma(self.aneg[:, kc, :], self.alog_h[kc * 128:(kc + 1) * 128, :])
        dma(self.mcb[:], self.mcb_h.rearrange("(c p) -> p c", p=128), slow=True)
        dma(self.dtw[:], self.dtw_h[0, :].rearrange("(c p) -> p c", p=128), slow=True)
        dma(self.dtb[:], self.dtb_h.rearrange("(c p) -> p c", p=128), slow=True)
        dma(self.dvec[:], self.dvec_h.rearrange("(c p) -> p c", p=128), slow=True)
        for kc in range(KH):
            dma(self.fcw[:, kc, :], self.fcw_h[kc * 128:(kc + 1) * 128, 0, :])
        dma(self.fcb[:], self.fcb_h.rearrange("(c p) -> p c", p=128), slow=True)
        ld0 = qio.n

        es(lambda e: e.activation(self.aneg[:], self.aneg[:], AF.Exp),
           wait=[(qio, ld0)], inc=(eS, 1))
        ev(lambda e: e.tensor_scalar(self.aneg[:], self.aneg[:], -1.0, None, ALU.mult),
           wait=[(eS, None)], inc=(eV, 1))

        self.p32.push()
        stg = self.p32.alloc(2 * NS + 1)
        for kc in range(KI):
            t = dma(stg, self.xpw_h[kc * 128:(kc + 1) * 128, :], wait=[(eV, None)])
            ev(lambda e, kc=kc: e.tensor_copy(self.xpw[:, kc, 0:2 * NS + 1], stg),
               wait=[(qio, t)], drain=True, inc=(eV, 1))
        self.p32.pop()

        DI2 = 2 * self.c.DI
        self.p32.push()
        self.p16.push()
        wstg = [self.p32.alloc(max(DI2, DH)) for _ in range(2)]
        wbig = self.p16.alloc(KD * DI2).rearrange("p (k m) -> p k m", k=KD)
        mstg = [self.p16.alloc(KI * 128) for _ in range(2)]
        cast_ev = [eV.n, eV.n]
        for kc in range(KD):
            pr = kc % 2
            t = dma(wstg[pr][:, :DI2], self.ipw_h[kc * 128:(kc + 1) * 128, :],
                    wait=[(eV, cast_ev[pr])])
            ev(lambda e, pr=pr, kc=kc: e.tensor_copy(wbig[:, kc, :],
                                                     wstg[pr][:, :DI2]),
               wait=[(qio, t)], inc=(eV, 1))
            cast_ev[pr] = eV.n
        for m in range(2 * KI):
            pr = m % 2
            ev(lambda e, pr=pr, m=m: e.tensor_copy(
                mstg[pr][:, :KD * 128],
                wbig[:, :, m * 128:(m + 1) * 128]),
               wait=[(eV, None), (qst, None)], inc=(eV, 1))
            dma(self.ipw_d[m], mstg[pr][:, :KD * 128], wait=[(eV, None)])
        cast_ev = [eV.n, eV.n]
        for kc in range(KI):
            pr = kc % 2
            t = dma(wstg[pr][:, :D], self.opw_h[kc * 128:(kc + 1) * 128, :],
                    wait=[(eV, cast_ev[pr])])
            ev(lambda e, pr=pr, kc=kc: e.tensor_copy(
                wbig.rearrange("p k m -> p (k m)")[:, kc * D:(kc + 1) * D],
                wstg[pr][:, :D]),
               wait=[(qio, t), (qst, None)], inc=(eV, 1))
            cast_ev[pr] = eV.n
        obig = wbig.rearrange("p k m -> p (k m)")[:, :KI * D].rearrange(
            "p (k m) -> p k m", k=KI)
        for m in range(KD):
            pr = m % 2
            ev(lambda e, pr=pr, m=m: e.tensor_copy(
                mstg[pr], obig[:, :, m * 128:(m + 1) * 128]),
               wait=[(eV, None), (qst, None)], inc=(eV, 1))
            dma(self.opw_d[m], mstg[pr], wait=[(eV, None)])

        acc = self.p32.alloc(4)
        ev(lambda e: e.memset(acc[:, 0:2], 0.0), inc=(eV, 1))
        for src_, col, rows, cols in ((self.w1_h, 0, KH, D), (self.w2_h, 1, KD, DH)):
            for r in range(rows):
                pr = r % 2
                t = dma(wstg[pr][:, :cols], src_[r * 128:(r + 1) * 128, :],
                        wait=[(eV, None)])
                ev(lambda e, pr=pr, col=col, cols=cols: e.reduce_sum(
                    acc[:, 2 + col:3 + col], wstg[pr][:, :cols], axis=AX.X,
                    apply_absolute_value=True),
                   wait=[(qio, t)], drain=True, inc=(eV, 1))
                ev(lambda e, col=col: e.tensor_add(
                    acc[:, col:col + 1], acc[:, col:col + 1], acc[:, 2 + col:3 + col]),
                   drain=True, inc=(eV, 1))
        ep(lambda e: e.transpose(self.psum[0:2, 0:128], acc[:, 0:2], self.idf[:]),
           wait=[(eV, None), (eS, None)], inc=(eT, 1))
        ev(lambda e: e.reduce_sum(acc[0:2, 2:3], self.psum[0:2, 0:128], axis=AX.X),
           wait=[(eT, None)], drain=True, inc=(eV, 1))
        scale = 1.0 / (DH * D)
        ev(lambda e: e.tensor_scalar(acc[0:2, 2:3], acc[0:2, 2:3], scale, 1e-5,
                                     ALU.mult, ALU.max), drain=True, inc=(eV, 1))
        ev(lambda e: e.tensor_scalar(acc[0:2, 3:4], acc[0:2, 2:3], 1.0 / 127.0, None,
                                     ALU.mult), drain=True, inc=(eV, 1))
        ev(lambda e: e.reciprocal(acc[0:2, 0:1], acc[0:2, 2:3]), drain=True,
           inc=(eV, 1))
        dma(self.scl_d[0:1, :], acc[0:2, 0:1], wait=[(eV, None)], slow=True)
        dma(self.scl_d[1:2, :], acc[0:2, 3:4], wait=[(eV, None)], slow=True)
        dma(self.sw1[:], self.scl_d[0:1, 0:1].broadcast_to([128, 1]), slow=True,
            wait=[(qst, None)])
        dma(self.sw2[:], self.scl_d[0:1, 1:2].broadcast_to([128, 1]), slow=True)
        dma(self.iw1[:], self.scl_d[1:2, 0:1].broadcast_to([128, 1]), slow=True)
        dma(self.iw2[:], self.scl_d[1:2, 1:2].broadcast_to([128, 1]), slow=True)

        self.pf16.push()
        f16w = self.pf16.alloc(max(D, DH))
        qb = [self.p16.alloc(max(D, DH)) for _ in range(2)]
        rslab = [self.p16.alloc(KD * 128) for _ in range(2)]
        w2big = wbig.rearrange("p k m -> p (k m)")[:, :KH * KD * 128].rearrange(
            "p (h m q) -> p h m q", h=KH, m=KD)
        for src_, rows, cols, scol, w2 in ((self.w1_h, KH, D, self.sw1, False),
                                           (self.w2_h, KD, DH, self.sw2, True)):
            for r in range(rows):
                pr = r % 2
                t = dma(wstg[pr][:, :cols], src_[r * 128:(r + 1) * 128, :],
                        wait=[(eV, None)])
                ev(lambda e, pr=pr, cols=cols, scol=scol: e.tensor_scalar(
                    wstg[pr][:, :cols], wstg[pr][:, :cols], scol[:, 0:1], None,
                    ALU.mult), wait=[(qio, t), (eV, None), (eT, None)], drain=True,
                   inc=(eV, 1))
                ev(lambda e, pr=pr, cols=cols: e.tensor_scalar(
                    wstg[pr][:, :cols], wstg[pr][:, :cols], 1.49, -1.49,
                    ALU.min, ALU.max), drain=True, inc=(eV, 1))
                ev(lambda e, pr=pr, cols=cols: e.tensor_scalar(
                    f16w[:, :cols], wstg[pr][:, :cols], MAGIC16, None,
                    ALU.add), drain=True, inc=(eV, 1))
                ev(lambda e, pr=pr, cols=cols: e.tensor_scalar(
                    qb[pr][:, :cols], f16w[:, :cols], MAGIC16, None,
                    ALU.subtract), drain=True, inc=(eV, 1))
                nblk = cols // 128
                for g0 in range(0, nblk, 4):
                    gn = min(4, nblk - g0)
                    for j in range(gn):
                        kc = g0 + j
                        self.ep(lambda e, pr=pr, kc=kc, j=j: e.transpose(
                            self.psum_b[:, j * 128:(j + 1) * 128],
                            qb[pr][:, kc * 128:(kc + 1) * 128], self.idb[:]),
                            wait=[(eV, None), (eS, None)],
                            inc=(eT, 1) if j == gn - 1 else None)
                    if w2:
                        es(lambda e, r=r, g0=g0, gn=gn: e.copy(
                            w2big[:, g0:g0 + gn, r, :],
                            self.psum_b[:, :gn * 128].rearrange(
                                "p (a q) -> p a q", a=gn)),
                           wait=[(eT, None)], inc=(eS, 1))
                    else:
                        es(lambda e, pr=pr, g0=g0, gn=gn: e.copy(
                            rslab[pr][:, g0 * 128:(g0 + gn) * 128],
                            self.psum_b[:, :gn * 128]),
                           wait=[(eT, None), (qst, None)], inc=(eS, 1))
                if not w2:
                    dma(self.w1t_d[r], rslab[pr], wait=[(eS, None)])
        for kh in range(KH):
            dma(self.w2t_d[kh],
                w2big.rearrange("p h m q -> p (h m q)")
                [:, kh * KD * 128:(kh + 1) * KD * 128],
                wait=[(eS, None)])
        self.p32.pop()
        self.p16.pop()
        self.pf16.pop()
        self.w_prep_tgt = self.qio.n

    def phase1(self, b):
        c = self.c
        KD, KI, T, D, NS, NFC, NTT = c.KD, c.KI, c.T, c.D, c.NS, c.NFC, c.NTT
        qio, qst = self.qio, self.qst
        eV, eS, eT, eG = self.eV, self.eS, self.eT, self.eG
        dma, ev, es, ep, eg = self.dma, self.ev, self.es, self.ep, self.eg
        ps = self.psum

        self.p32.push()
        self.p16.push()
        xT = self.p16.alloc(KD * T).rearrange("p (k t) -> p k t", k=KD)
        xtok = [self.p32.alloc(D) for _ in range(2)]
        self.warm()
        tr_ev = [0, 0]
        ev_ev = [0, 0]
        for tt in range(NTT):
            pr = tt % 2
            t = dma(xtok[pr], self.x_h[b, tt * 128:(tt + 1) * 128, :],
                    wait=[(eT, tr_ev[pr])])
            for g in range(KD // 4):
                for j in range(4):
                    kc = g * 4 + j
                    ep(lambda e, pr=pr, kc=kc, j=j: e.transpose(
                        ps[:, pr * 1024 + (kc % 4) * 128 + (kc // 4) * 512:
                           pr * 1024 + (kc % 4) * 128 + (kc // 4) * 512 + 128],
                        xtok[pr][:, kc * 128:(kc + 1) * 128], self.idf[:]),
                        wait=[(qio, t), (eS, ev_ev[pr])],
                        inc=(eT, 1) if (g == KD // 4 - 1 and j == 3) else None)
            tr_ev[pr] = eT.n
            for g in range(KD // 4):
                es(lambda e, pr=pr, g=g, tt=tt: e.copy(
                    xT[:, g * 4:(g + 1) * 4, tt * 128:(tt + 1) * 128],
                    ps[:, pr * 1024 + g * 512:pr * 1024 + (g + 1) * 512]
                    .rearrange("p (k t) -> p k t", k=4)),
                   wait=[(eT, None)], inc=(eS, 1))
            ev_ev[pr] = eS.n
        xT_done = eS.n

        xs_pad = [self.p16.alloc(T + 3) for _ in range(2)]
        xss = [self.p16.alloc(T) for _ in range(2)]
        xc = [self.p32.alloc(T) for _ in range(2)]
        bcacc = self.p32.alloc(T)
        wt = [self.p16.alloc(KD * 128) for _ in range(2)]
        zst = [self.p16.alloc(512) for _ in range(2)]
        wt_use = [0, 0]
        ip_ev = [0, 0]
        xss_use = [0, 0]
        silu_done = [0, 0]
        conv_done = [0, 0]
        xs_spill = [0, 0]
        zspill = [0, 0]
        zp_ev = [0, 0]
        dt_ev = [0, 0]
        bc_ev = [0, 0]
        ev(lambda e: e.memset(self.dtr_row[:], 0.0), inc=(eV, 1))

        def xp_block(dch):
            pr = dch % 2
            for fc in range(NFC):
                bpr = (dch * NFC + fc) % 2
                ep(lambda e, pr=pr, dch=dch, fc=fc, bpr=bpr: e.matmul(
                    ps[:, 2048 + bpr * 512:2048 + (bpr + 1) * 512],
                    self.xpw[:, dch, 0:128], xss[pr][:, fc * 512:(fc + 1) * 512],
                    start=True, stop=True),
                    wait=[(eS, silu_done[pr]), (eV, bc_ev[bpr])], inc=(eT, 1))
                if dch == 0:
                    ev(lambda e, fc=fc, bpr=bpr: e.tensor_copy(
                        bcacc[:, fc * 512:(fc + 1) * 512],
                        ps[:, 2048 + bpr * 512:2048 + (bpr + 1) * 512]),
                       wait=[(eT, None)], drain=True, inc=(eV, 1))
                else:
                    ev(lambda e, fc=fc, bpr=bpr: e.scalar_tensor_tensor(
                        bcacc[:, fc * 512:(fc + 1) * 512],
                        ps[:, 2048 + bpr * 512:2048 + (bpr + 1) * 512], 1.0,
                        bcacc[:, fc * 512:(fc + 1) * 512], ALU.mult, ALU.add),
                       wait=[(eT, None)], drain=True, inc=(eV, 1))
                bc_ev[bpr] = eV.n
                dpr = (dch * NFC + fc) % 2
                ep(lambda e, pr=pr, dch=dch, fc=fc, dpr=dpr: e.matmul(
                    ps[0:1, 1024 + dpr * 512:1024 + dpr * 512 + 512],
                    self.xpw[:, dch, 128:129], xss[pr][:, fc * 512:(fc + 1) * 512],
                    start=True, stop=True),
                    wait=[(eV, dt_ev[dpr])], inc=(eT, 1))
                ev(lambda e, fc=fc, dpr=dpr: e.scalar_tensor_tensor(
                    self.dtr_row[0:1, fc * 512:(fc + 1) * 512],
                    ps[0:1, 1024 + dpr * 512:1024 + dpr * 512 + 512], 1.0,
                    self.dtr_row[0:1, fc * 512:(fc + 1) * 512], ALU.mult, ALU.add),
                   wait=[(eT, None)], drain=True, inc=(eV, 1))
                dt_ev[dpr] = eV.n
            xss_use[pr] = eT.n

        zwt = [self.p16.alloc(KD * 128) for _ in range(2)]
        zwt_use = [0, 0]

        def z_block(mz):
            pr = mz % 2
            tW = dma(zwt[pr], self.ipw_d[KI + mz], wait=[(eT, zwt_use[pr])])
            for fc in range(NFC):
                fpr = fc % 2
                for kc in range(KD):
                    wl = [(qio, tW), (eS, zp_ev[fpr])]
                    if mz == 0 and fc == 1 and kc == 0:
                        wl.append((eT, None))
                    ep(lambda e, pr=pr, fpr=fpr, kc=kc, fc=fc: e.matmul(
                        ps[:, 3072 + fpr * 512:3072 + (fpr + 1) * 512],
                        zwt[pr][:, kc * 128:(kc + 1) * 128],
                        xT[:, kc, fc * 512:(fc + 1) * 512],
                        start=(kc == 0), stop=(kc == KD - 1)),
                        wait=wl,
                        inc=(eT, 1) if kc == KD - 1 else None)
                es(lambda e, fpr=fpr: e.copy(zst[fpr],
                                             ps[:, 3072 + fpr * 512:
                                                3072 + (fpr + 1) * 512]),
                   wait=[(eT, None), (qst, None)], inc=(eS, 1))
                zspill[fpr] = dma(self.z_d[b, mz, :, fc * 512:(fc + 1) * 512],
                                  zst[fpr], wait=[(eS, None)])
                zp_ev[fpr] = eS.n
            zwt_use[pr] = eT.n


        for dch in range(KI):
            pr = dch % 2
            tW = dma(wt[pr], self.ipw_d[dch],
                     wait=[(eT, wt_use[pr]), (qst, None)])
            ev(lambda e, pr=pr: e.memset(xs_pad[pr][:, 0:3], 0.0), inc=(eV, 1))
            pad_ev = eV.n
            for fc in range(NFC):
                fpr = fc % 2
                for kc in range(KD):
                    ep(lambda e, pr=pr, fpr=fpr, kc=kc, fc=fc: e.matmul(
                        ps[:, fpr * 512:(fpr + 1) * 512],
                        wt[pr][:, kc * 128:(kc + 1) * 128],
                        xT[:, kc, fc * 512:(fc + 1) * 512],
                        start=(kc == 0), stop=(kc == KD - 1)),
                        wait=[(qio, tW), (eS, ip_ev[fpr]), (eS, xT_done)],
                        inc=(eT, 1) if kc == KD - 1 else None)
                es(lambda e, pr=pr, fpr=fpr, fc=fc: e.copy(
                    xs_pad[pr][:, 3 + fc * 512:3 + (fc + 1) * 512],
                    ps[:, fpr * 512:(fpr + 1) * 512]),
                   wait=[(eT, None), (eV, conv_done[pr])], inc=(eS, 1))
                ip_ev[fpr] = eS.n
            wt_use[pr] = eT.n
            ev(lambda e, pr=pr, dch=dch: e.tensor_scalar(
                xc[pr], xs_pad[pr][:, 0:T], self.mcw[:, dch, 0:1], None, ALU.mult),
               wait=[(eS, None), (eV, pad_ev)], inc=(eV, 1))
            for k in range(1, 4):
                ev(lambda e, pr=pr, k=k, dch=dch: e.scalar_tensor_tensor(
                    xc[pr], xs_pad[pr][:, k:k + T], self.mcw[:, dch, k:k + 1],
                    xc[pr], ALU.mult, ALU.add), drain=True, inc=(eV, 1))
            es(lambda e, pr=pr, dch=dch: e.activation(
                xss[pr], xc[pr], AF.Silu, bias=self.mcb[:, dch:dch + 1], scale=1.0),
               wait=[(eV, None), (eT, xss_use[pr]), (qst, None)],
               inc=(eS, 1))
            silu_done[pr] = eS.n
            xs_spill[pr] = dma(self.xs_d[b, dch], xss[pr], wait=[(eS, None)])
            if dch > 0:
                xp_block(dch - 1)
        xp_block(KI - 1)
        for mz in range(KI):
            z_block(mz)
        bst = self.p16.alloc(512)
        cst = self.p16.alloc(512)
        for fc in range(NFC):
            es(lambda e, fc=fc: e.copy(bst[0:64, :],
                                       bcacc[0:64, fc * 512:(fc + 1) * 512]),
               wait=[(eV, None), (qst, None)], inc=(eS, 1))
            es(lambda e, fc=fc: e.copy(cst[64:128, :],
                                       bcacc[64:128, fc * 512:(fc + 1) * 512]),
               wait=[(qst, None)], inc=(eS, 1))
            t1 = dma(self.bmat_d[b, :, fc * 512:(fc + 1) * 512], bst[0:64, :],
                     wait=[(eS, None)])
            t2 = dma(self.cmat_d[b, :, fc * 512:(fc + 1) * 512], cst[64:128, :])
        self.bc_tgt = self.qio.n
        self.p32.pop()
        self.p16.pop()

    def scan(self, b):
        c = self.c
        KI, T, NS, Tc = c.KI, c.T, c.NS, c.Tc
        NDH, DPH, GD, NDG, NG, NNG, HB = c.NDH, c.DPH, c.GD, c.NDG, c.NG, c.NNG, c.HB
        qio, qst = self.qio, self.qst
        eV, eS, eT, eG = self.eV, self.eS, self.eT, self.eG
        dma, ev, es, ep, eg = self.dma, self.ev, self.es, self.ep, self.eg
        ps = self.psum

        self.p32.push()
        self.p16.push()
        self.pf16.push()
        dtrep = self.p32.alloc(T)
        dtt = self.p32.alloc(GD * Tc).rearrange("p (d t) -> p d t", d=GD)
        spt = self.p32.alloc(Tc)
        yfull = self.p32.alloc(Tc)
        gz = self.p32.alloc(Tc)
        ST = self.p32.alloc(KI * NS)
        dtx = self.p16.alloc(GD * Tc).rearrange("p (d t) -> p d t", d=GD)
        xssl = self.p16.alloc(GD * Tc).rearrange("p (d t) -> p d t", d=GD)
        brep = self.p16.alloc(2 * NG * Tc).rearrange("p (r n t) -> p r n t", r=2, n=NG)
        crep = self.p16.alloc(2 * NG * Tc).rearrange("p (r n t) -> p r n t", r=2, n=NG)
        ubuf = self.p16.alloc(GD * Tc).rearrange("p (d t) -> p d t", d=GD)
        sbig = self.p16.alloc(GD * Tc).rearrange("p (d t) -> p d t", d=GD)
        scb = self.p16.alloc(2 * GD * Tc).rearrange("p (r d t) -> p r d t", r=2, d=GD)
        zsl = [self.p16.alloc(Tc) for _ in range(2)]
        ysl = [self.p16.alloc(Tc) for _ in range(2)]

        dma(self.row_d[0:1, :], self.dtr_row, wait=[(eV, None)])
        dma(dtrep, self.row_d[0:1, :].broadcast_to([128, T]), wait=[(qst, None)])
        dt_bcast = None

        bc_use = [0, 0]
        bc_useG = [0, 0]
        dab_use = [0, 0]
        scb_use = [0, 0]
        it = 0
        y_ev = 0
        ysl_dma = [0, 0]
        for tc in range(c.NTC):
            tcs = tc * Tc
            for dh in range(NDH):
                dch0 = dh * DPH
                for dg in range(NDG):
                    xs_t = []
                    for i in range(GD):
                        tx = dma(xssl[:, i, :],
                                 self.xs_d[b, dch0 + dg * GD + i, :, tcs:tcs + Tc],
                                 wait=[(eV, None), (qst, None)])
                        xs_t.append(tx)
                    for i in range(GD):
                        dch = dch0 + dg * GD + i
                        es(lambda e, i=i, dch=dch, tcs=tcs: e.activation(
                            spt, dtrep[:, tcs:tcs + Tc], AF.Abs,
                            scale=self.dtw[:, dch:dch + 1],
                            bias=self.dtb[:, dch:dch + 1]),
                           wait=[(qio, None), (eV, None)], drain=True, inc=(eS, 1))
                        es(lambda e: e.activation(spt, spt, AF.Exp, scale=-1.0),
                           drain=True, inc=(eS, 1))
                        es(lambda e: e.activation(spt, spt, AF.Ln, bias=1.0),
                           drain=True, inc=(eS, 1))
                        es(lambda e, i=i, dch=dch, tcs=tcs: e.activation(
                            dtt[:, i, :], dtrep[:, tcs:tcs + Tc], AF.Relu,
                            scale=self.dtw[:, dch:dch + 1],
                            bias=self.dtb[:, dch:dch + 1]),
                           drain=True, inc=(eS, 1))
                        ev(lambda e, i=i: e.tensor_add(dtt[:, i, :], dtt[:, i, :],
                                                       spt),
                           wait=[(eS, None)], drain=True, inc=(eV, 1))
                        eg(lambda e, i=i: e.tensor_mul(dtx[:, i, :], dtt[:, i, :],
                                                       xssl[:, i, :]),
                           wait=[(qio, xs_t[i]), (eV, None)], drain=True, inc=(eG, 1))
                    dt_built = eV.n
                    jds = list(range(GD))
                    for ng in range(NNG):
                        gpr = ng % 2
                        n0 = ng * NG
                        tB = dma(brep[:, gpr],
                                 self.bmat_d[b, n0:n0 + NG, tcs:tcs + Tc]
                                 .unsqueeze(0).broadcast_to([128, NG, Tc]),
                                 wait=[(eV, bc_use[gpr]), (eG, bc_useG[gpr]),
                                       (qst, None)])
                        tC = dma(crep[:, gpr],
                                 self.cmat_d[b, n0:n0 + NG, tcs:tcs + Tc]
                                 .unsqueeze(0).broadcast_to([128, NG, Tc]))
                        for ni in range(NG):
                            n = n0 + ni
                            pr = it % 2
                            for j in jds:
                                dch = dch0 + dg * GD + j
                                es(lambda e, pr=pr, j=j, dch=dch, n=n: e.activation(
                                    self.dab(pr, j), dtt[:, j, :], AF.Exp,
                                    scale=self.aneg[:, dch, n:n + 1]),
                                   wait=[(eV, dab_use[pr]), (eV, dt_built)],
                                   inc=(eS, 1) if j == jds[-1] else None)
                            ev(lambda e, gpr=gpr, ni=ni: e.tensor_mul(
                                ubuf[:, :, :],
                                dtx[:, :, :],
                                brep[:, gpr, ni, :].unsqueeze(1)
                                .broadcast_to([128, GD, Tc])),
                               wait=[(qio, tB), (eS, None), (eG, None)],
                               inc=(eV, 1))
                            for j in jds:
                                dch = dch0 + dg * GD + j
                                jj = j
                                if tc == 0:
                                    init = 0.0
                                else:
                                    init = ST[:, dch * NS + n:dch * NS + n + 1]
                                ev(lambda e, jj=jj, pr=pr, init=init: e.tensor_tensor_scan(
                                    sbig[:, jj, :], self.dab(pr, jj), ubuf[:, jj, :],
                                    init, ALU.mult, ALU.add),
                                   wait=[(eG, None)], drain=(jj == 0),
                                   inc=(eV, 1) if jj == GD - 1 else None)
                            dab_use[pr] = eV.n
                            ev(lambda e, pr=pr, gpr=gpr, ni=ni: e.tensor_mul(
                                scb[:, pr, :, :],
                                sbig[:, :, :],
                                crep[:, gpr, ni, :].unsqueeze(1)
                                .broadcast_to([128, GD, Tc])),
                               wait=[(qio, tC), (eT, scb_use[pr])], drain=True,
                               inc=(eV, 1))
                            if tc < c.NTC - 1:
                                eg(lambda e, dch0=dch0, dg=dg, n=n: e.tensor_copy(
                                    ST.rearrange("p (d n) -> p d n", n=NS)
                                    [:, dch0 + dg * GD:dch0 + (dg + 1) * GD, n:n + 1],
                                    sbig[:, :, Tc - 1:Tc]),
                                   wait=[(eV, None)], inc=(eG, 1))
                            sC_done = eV.n
                            first = (ng == 0 and ni == 0)
                            last = (ng == NNG - 1 and ni == NG - 1)
                            HBS = c.HBS
                            ybank = max(Tc, 512)
                            for j in range(GD):
                                for h in range(HB):
                                    ep(lambda e, j=j, h=h, pr=pr, first=first,
                                       last=last, HBS=HBS, ybank=ybank: e.matmul(
                                           ps[:, j * ybank + h * HBS:
                                              j * ybank + (h + 1) * HBS],
                                           self.idb[:],
                                           scb[:, pr, j, h * HBS:(h + 1) * HBS],
                                           start=first, stop=last),
                                       wait=[(eV, sC_done), (eV, y_ev)],
                                       inc=(eT, 1) if (j == GD - 1 and h == HB - 1)
                                       else None)
                            scb_use[pr] = eT.n
                            it += 1
                        bc_use[gpr] = eV.n
                        bc_useG[gpr] = eG.n
                    for j in range(GD):
                        dch = dch0 + dg * GD + j
                        zpr = j % 2
                        tz = dma(zsl[zpr], self.z_d[b, dch, :, tcs:tcs + Tc],
                                 wait=[(eS, None), (qst, None)])
                        ev(lambda e, j=j, dch=dch: e.scalar_tensor_tensor(
                            yfull, xssl[:, j, :],
                            self.dvec[:, dch:dch + 1],
                            ps[:, j * max(Tc, 512):j * max(Tc, 512) + Tc],
                            ALU.mult, ALU.add),
                           wait=[(eT, None)], drain=True, inc=(eV, 1))
                        es(lambda e, zpr=zpr: e.activation(gz, zsl[zpr], AF.Silu),
                           wait=[(qio, tz), (eV, None)], drain=True, inc=(eS, 1))
                        ev(lambda e, zpr=zpr: e.tensor_mul(ysl[zpr], yfull, gz),
                           wait=[(eS, None), (qst, None)], drain=True,
                           inc=(eV, 1))
                        ysl_dma[zpr] = dma(self.y_d[b, dch, :, tcs:tcs + Tc],
                                           ysl[zpr], wait=[(eV, None)])
                    y_ev = eV.n
        self.y_tgt = self.qio.n
        self.p32.pop()
        self.p16.pop()
        self.pf16.pop()

    def warm(self, n=24, wait=()):
        ps = self.psum
        for i in range(n):
            self.ep(lambda e, i=i, n=n: e.matmul(
                ps[:, 3968:4096], self.idb[:], self.idb[:],
                start=(i == 0), stop=(i == n - 1)),
                wait=([(self.eT, None)] + list(wait)) if i == 0 else (),
                inc=(self.eT, 1))

    def dab(self, pr, j):
        Tc, GD = self.c.Tc, self.c.GD
        off = (pr * GD + j) * Tc
        return self.pf16.t[:, off:off + Tc]

    def phase34(self, b):
        c = self.c
        KD, KI, KH, T, D, DH, Tc = c.KD, c.KI, c.KH, c.T, c.D, c.DH, c.Tc
        NFC, NTT, TT_C = c.NFC, c.NTT, c.TT_C
        qio, qst, qout = self.qio, self.qst, self.qout
        eV, eS, eT, eG = self.eV, self.eS, self.eT, self.eG
        dma, ev, es, ep, eg = self.dma, self.ev, self.es, self.ep, self.eg
        ps = self.psum
        smv = self.smv

        self.p32.push()
        self.p16.push()
        self.pf16.push()
        f16q = [self.pf16.alloc(D) for _ in range(2)]
        f16h = [self.pf16.alloc(T) for _ in range(2)]
        x1qT = self.p16.alloc(KD * T).rearrange("p (k t) -> p k t", k=KD)
        isrep = self.p16.alloc(T)
        r1b = [self.p32.alloc(D) for _ in range(2)]
        sqb = [self.p32.alloc(D) for _ in range(2)]
        xtk = [self.p32.alloc(D) for _ in range(2)]
        TG = min(8, TT_C)
        self.p32.push()
        r1big = self.p32.alloc(TG * D).rearrange("p (a t) -> p a t", a=TG)
        self.p16.push()
        Yb = self.p16.alloc(KI * Tc).rearrange("p (k t) -> p k t", k=KI)
        attn = self.p16.alloc(KD * Tc).rearrange("p (k t) -> p k t", k=KD)
        xqb = [self.p16.alloc(D) for _ in range(2)]
        wop = [self.p16.alloc(KI * 128) for _ in range(2)]

        def layer_norm(tile, pr, out, wait_extra=(), wait_out=()):
            ev(lambda e: e.reduce_sum(smv[:, 0:1], tile, axis=AX.X),
               wait=list(wait_extra), drain=True, inc=(eV, 1))
            es(lambda e, pr=pr: e.activation(sqb[pr], tile, AF.Square),
               wait=[(eV, None)], inc=(eS, 1))
            ev(lambda e, pr=pr: e.reduce_sum(smv[:, 1:2], sqb[pr], axis=AX.X),
               wait=[(eS, None)], drain=True, inc=(eV, 1))
            ev(lambda e: e.tensor_scalar(smv[:, 0:2], smv[:, 0:2], 1.0 / D, None,
                                         ALU.mult), drain=True, inc=(eV, 1))
            ev(lambda e: e.tensor_mul(smv[:, 2:3], smv[:, 0:1], smv[:, 0:1]),
               drain=True, inc=(eV, 1))
            ev(lambda e: e.tensor_sub(smv[:, 1:2], smv[:, 1:2], smv[:, 2:3]),
               drain=True, inc=(eV, 1))
            ev(lambda e: e.tensor_scalar(smv[:, 1:2], smv[:, 1:2], 1e-5, None,
                                         ALU.add), drain=True, inc=(eV, 1))
            ev(lambda e: e.reciprocal(smv[:, 2:3], smv[:, 1:2]), drain=True,
               inc=(eV, 1))
            es(lambda e: e.activation(smv[:, 2:3], smv[:, 2:3], AF.Sqrt),
               wait=[(eV, None)], drain=True, inc=(eS, 1))
            ev(lambda e: e.tensor_scalar(out, tile, smv[:, 0:1], smv[:, 2:3],
                                         ALU.subtract, ALU.mult),
               wait=[(eS, None)] + list(wait_out), drain=True, inc=(eV, 1))

        self.warm()
        wop_use = [0, 0]
        op_ev = [0, 0]
        ptok_use = [0, 0]
        xtk_use = [0, 0]
        pxq_ev = 0
        for tc in range(c.NTC):
            tcs = tc * Tc
            yt = 0
            for kc in range(KI):
                yt = dma(Yb[:, kc, :], self.y_d[b, kc, :, tcs:tcs + Tc],
                         wait=[(eT, None), (qst, None)])
            for m in range(KD):
                pr = m % 2
                tW = dma(wop[pr], self.opw_d[m], wait=[(eT, wop_use[pr])])
                for h in range(c.HB):
                    HBS = c.HBS
                    fpr = (m * c.HB + h) % 2
                    for kc in range(KI):
                        ep(lambda e, pr=pr, fpr=fpr, kc=kc, h=h, HBS=HBS: e.matmul(
                            ps[:, fpr * 512:fpr * 512 + HBS],
                            wop[pr][:, kc * 128:(kc + 1) * 128],
                            Yb[:, kc, h * HBS:(h + 1) * HBS],
                            start=(kc == 0), stop=(kc == KI - 1)),
                            wait=[(qio, max(tW, yt)), (eS, op_ev[fpr])],
                            inc=(eT, 1) if kc == KI - 1 else None)
                    es(lambda e, m=m, h=h, fpr=fpr, HBS=HBS: e.copy(
                        attn[:, m, h * HBS:(h + 1) * HBS],
                        ps[:, fpr * 512:fpr * 512 + HBS]),
                       wait=[(eT, None)], inc=(eS, 1))
                    op_ev[fpr] = eS.n
                wop_use[pr] = eT.n
            attn_done = eS.n
            TT = min(8, TT_C)
            for tg in range(0, TT_C, TT):
              for j in range(TT):
                tt = tg + j
                pr = tt % 2
                pt0 = 1024 + pr * 1024
                for m in range(KD):
                    ep(lambda e, m=m, tt=tt, pt0=pt0: e.transpose(
                        self.psum_b[:, 2 * pt0 + m * 128:2 * pt0 + (m + 1) * 128],
                        attn[:, m, tt * 128:(tt + 1) * 128], self.idb[:]),
                        wait=[(eS, attn_done), (eV, ptok_use[pr])],
                        inc=(eT, 1) if m == KD - 1 else None)
                tX = dma(xtk[pr], self.x_h[b, tcs + tt * 128:tcs + (tt + 1) * 128, :],
                         wait=[(eV, xtk_use[pr])])
                ev(lambda e, pr=pr, pt0=pt0, j=j: e.tensor_add(
                    r1big[:, j, :], self.psum_b[:, 2 * pt0:2 * pt0 + D], xtk[pr]),
                   wait=[(eT, None), (qio, tX), (qst, None)], drain=True,
                   inc=(eV, 1))
                ptok_use[pr] = eV.n
                xtk_use[pr] = eV.n
              smb = self.smb
              gs = tc * TT_C + tg
              ev(lambda e: e.reduce_sum(smb[:, 0:TT], r1big[:, :, :], axis=AX.X),
                 drain=True, inc=(eV, 1))
              for j in range(TT):
                  qr = j % 2
                  es(lambda e, qr=qr, j=j: e.activation(sqb[qr], r1big[:, j, :],
                                                        AF.Square),
                     wait=[(eV, None)], inc=(eS, 1))
                  ev(lambda e, qr=qr, j=j: e.reduce_sum(
                      smb[:, TT + j:TT + j + 1], sqb[qr], axis=AX.X),
                     wait=[(eS, None)], drain=True, inc=(eV, 1))
              ev(lambda e: e.tensor_scalar(smb[:, 0:2 * TT], smb[:, 0:2 * TT],
                                           1.0 / D, None, ALU.mult),
                 drain=True, inc=(eV, 1))
              ev(lambda e: e.tensor_mul(smb[:, 2 * TT:3 * TT], smb[:, 0:TT],
                                        smb[:, 0:TT]), drain=True, inc=(eV, 1))
              ev(lambda e: e.tensor_sub(smb[:, TT:2 * TT], smb[:, TT:2 * TT],
                                        smb[:, 2 * TT:3 * TT]), drain=True,
                 inc=(eV, 1))
              ev(lambda e: e.tensor_scalar(smb[:, TT:2 * TT], smb[:, TT:2 * TT],
                                           1e-5, None, ALU.add), drain=True,
                 inc=(eV, 1))
              ev(lambda e: e.reciprocal(smb[:, 2 * TT:3 * TT], smb[:, TT:2 * TT]),
                 drain=True, inc=(eV, 1))
              es(lambda e: e.activation(smb[:, 2 * TT:3 * TT],
                                        smb[:, 2 * TT:3 * TT], AF.Sqrt),
                 wait=[(eV, None)], drain=True, inc=(eS, 1))
              for j in range(TT):
                  tt = tg + j
                  ev(lambda e, j=j: e.tensor_scalar(
                      r1big[:, j, :], r1big[:, j, :], smb[:, j:j + 1],
                      smb[:, 2 * TT + j:2 * TT + j + 1], ALU.subtract, ALU.mult),
                     wait=[(eS, None)], drain=True, inc=(eV, 1))
                  dma(self.x1_d[b, tcs + tt * 128:tcs + (tt + 1) * 128, :],
                      r1big[:, j, :], wait=[(eV, None)])
              ev(lambda e: e.reduce_max(smb[:, 3 * TT:4 * TT], r1big[:, :, :],
                                        axis=AX.X, apply_absolute_value=True),
                 drain=True, inc=(eV, 1))
              ev(lambda e: e.tensor_scalar(smb[:, 3 * TT:4 * TT],
                                           smb[:, 3 * TT:4 * TT], 1e-5, None,
                                           ALU.max), drain=True, inc=(eV, 1))
              ev(lambda e: e.reciprocal(smb[:, 4 * TT:5 * TT],
                                        smb[:, 3 * TT:4 * TT]),
                 drain=True, inc=(eV, 1))
              ev(lambda e: e.tensor_scalar(smb[:, 4 * TT:5 * TT],
                                           smb[:, 4 * TT:5 * TT], 127.0, None,
                                           ALU.mult), drain=True, inc=(eV, 1))
              ev(lambda e, gs=gs, TT=TT: e.tensor_mul(
                  self.itok[:, gs:gs + TT], smb[:, 3 * TT:4 * TT],
                  self.iw1[:].broadcast_to([128, TT])), drain=True, inc=(eV, 1))
              for j in range(TT):
                  pr = j % 2
                  gtt = gs + j
                  ev(lambda e, pr=pr, j=j: e.tensor_scalar(
                      f16q[pr], r1big[:, j, :],
                      smb[:, 4 * TT + j:4 * TT + j + 1],
                      None, ALU.mult), drain=True, inc=(eV, 1))
                  ev(lambda e, pr=pr: e.tensor_scalar(f16q[pr], f16q[pr],
                                                      MAGIC16, None, ALU.add),
                     drain=True, inc=(eV, 1))
                  ev(lambda e, pr=pr: e.tensor_scalar(xqb[pr], f16q[pr],
                                                      MAGIC16, None, ALU.subtract),
                     drain=True, inc=(eV, 1))
                  for kc in range(KD):
                      ep(lambda e, pr=pr, kc=kc: e.transpose(
                          self.psum_b[:, 6144 + kc * 128:6144 + (kc + 1) * 128],
                          xqb[pr][:, kc * 128:(kc + 1) * 128], self.idb[:]),
                          wait=[(eV, None), (eS, pxq_ev)],
                          inc=(eT, 1) if kc == KD - 1 else None)
                  es(lambda e, gtt=gtt: e.copy(
                      x1qT[:, :, gtt * 128:(gtt + 1) * 128],
                      self.psum_b[:, 6144:6144 + D]
                      .rearrange("p (k t) -> p k t", k=KD)),
                     wait=[(eT, None)], inc=(eS, 1))
                  pxq_ev = eS.n
        dma(self.itok_d, self.itok[:, 0:NTT], wait=[(eV, None)])
        dma(self.is_row[0:1, :],
            self.itok_d.transpose([1, 0]).unsqueeze(0),
            wait=[(qst, None)], slow=True)
        ev(lambda e: e.tensor_copy(self.is_rowb, self.is_row),
           wait=[(qio, None)], drain=True, inc=(eV, 1))
        dma(self.rowb_d[0:1, :], self.is_rowb, wait=[(eV, None)])
        dma(isrep, self.rowb_d[0:1, :].broadcast_to([128, T]), wait=[(qst, None)])
        is_done = None

        self.p32.pop()
        self.p16.pop()
        hst = [self.p16.alloc(T) for _ in range(2)]
        w1t = [self.p16.alloc(KD * 128) for _ in range(2)]
        h2r = [self.p16.alloc(512) for _ in range(2)]
        sc2rep = self.p16.alloc(T)
        iv2rep = self.p16.alloc(T)
        w2t = [self.p16.alloc(KD * 128) for _ in range(2)]
        h2s = [self.p16.alloc(512) for _ in range(2)]
        outb = [self.p32.alloc(D) for _ in range(2)]
        self.p32.push()
        hpad = [self.p32.alloc(T + 2) for _ in range(2)]
        hcv = [self.p32.alloc(T) for _ in range(2)]

        gtok = self.p32.alloc(c.NTT)
        tmax = self.p32.alloc(c.NTT)
        ev(lambda e: e.memset(gtok, 0.0), inc=(eV, 1))
        x1qT_done = pxq_ev
        self.warm(wait=[(self.eS, x1qT_done)])
        gm_ev = [0, 0]
        w1_use = [0, 0]
        h_ev = [0, 0]
        hc_use = [0, 0]
        hs_use = [0, 0]
        for mh in range(KH):
            pr = mh % 2
            tW = dma(w1t[pr], self.w1t_d[mh], wait=[(eT, w1_use[pr])])
            ev(lambda e, pr=pr: e.memset(hpad[pr][:, 0:1], 0.0),
               wait=[(eV, hc_use[pr])], inc=(eV, 1))
            ev(lambda e, pr=pr: e.memset(hpad[pr][:, T + 1:T + 2], 0.0), inc=(eV, 1))
            pad_ev = eV.n
            for fc in range(NFC):
                fpr = fc % 2
                for kc in range(KD):
                    ep(lambda e, pr=pr, fpr=fpr, kc=kc, fc=fc: e.matmul(
                        ps[:, fpr * 512:(fpr + 1) * 512],
                        w1t[pr][:, kc * 128:(kc + 1) * 128],
                        x1qT[:, kc, fc * 512:(fc + 1) * 512],
                        start=(kc == 0), stop=(kc == KD - 1)),
                        wait=[(qio, tW), (eV, h_ev[fpr]), (eS, x1qT_done)],
                        inc=(eT, 1) if kc == KD - 1 else None)
                ev(lambda e, pr=pr, fc=fc, fpr=fpr: e.tensor_mul(
                    hpad[pr][:, 1 + fc * 512:1 + (fc + 1) * 512],
                    ps[:, fpr * 512:(fpr + 1) * 512],
                    isrep[:, fc * 512:(fc + 1) * 512]),
                   wait=[(eT, None), (qio, None), (eV, pad_ev)], drain=True,
                   inc=(eV, 1))
                h_ev[fpr] = eV.n
            w1_use[pr] = eT.n
            ev(lambda e, pr=pr, mh=mh: e.tensor_scalar(
                hcv[pr], hpad[pr][:, 0:T], self.fcw[:, mh, 0:1], None, ALU.mult),
               drain=True, inc=(eV, 1))
            for k in range(1, 3):
                ev(lambda e, pr=pr, mh=mh, k=k: e.scalar_tensor_tensor(
                    hcv[pr], hpad[pr][:, k:k + T], self.fcw[:, mh, k:k + 1],
                    hcv[pr], ALU.mult, ALU.add), drain=True, inc=(eV, 1))
            es(lambda e, pr=pr, mh=mh: e.activation(
                hst[pr], hcv[pr], AF.Silu, bias=self.fcb[:, mh:mh + 1], scale=1.0),
               wait=[(eV, None), (qst, None)], drain=True, inc=(eS, 1))
            hc_use[pr] = eS.n
            dma(self.hs_d[b, mh], hst[pr], wait=[(eS, None)])
            hs_use[pr] = self.qio.n
            ab0 = 4096 + (mh % 2) * c.NTT * 128
            for tt in range(c.NTT):
                ep(lambda e, pr=pr, tt=tt, ab0=ab0: e.transpose(
                    self.psum_b[:, ab0 + tt * 128:ab0 + (tt + 1) * 128],
                    hst[pr][:, tt * 128:(tt + 1) * 128], self.idb[:]),
                    wait=[(eS, None), (eV, gm_ev[mh % 2])],
                    inc=(eT, 1) if tt == c.NTT - 1 else None)
            ev(lambda e, ab0=ab0: e.reduce_max(
                tmax, self.psum_b[:, ab0:ab0 + c.NTT * 128]
                .rearrange("p (a b) -> p a b", a=c.NTT), axis=AX.X,
                apply_absolute_value=True),
               wait=[(eT, None)], drain=True, inc=(eV, 1))
            ev(lambda e: e.tensor_max(gtok, gtok, tmax), drain=True, inc=(eV, 1))
            gm_ev[mh % 2] = eV.n
        self.p32.pop()
        self.p32.push()
        sc2f = self.p32.alloc(T)[0:1, :]
        dma(self.gtok_d, gtok, wait=[(eV, None)])
        dma(self.gm_row, self.gtok_d.transpose([1, 0]).unsqueeze(0),
            wait=[(qst, None)], slow=True)
        ev(lambda e: e.tensor_scalar(self.gm_row, self.gm_row, 1e-5, None,
                                     ALU.max), wait=[(qio, None)], drain=True,
           inc=(eV, 1))
        ev(lambda e: e.reciprocal(sc2f, self.gm_row), drain=True,
           inc=(eV, 1))
        ev(lambda e: e.tensor_scalar(sc2f, sc2f, 127.0, None,
                                     ALU.mult), drain=True, inc=(eV, 1))
        ev(lambda e: e.tensor_scalar(self.iv2_rowb, self.gm_row,
                                     self.iw2[0:1, 0:1], None, ALU.mult),
           drain=True, inc=(eV, 1))
        ev(lambda e: e.tensor_copy(self.sc2_rowb, sc2f),
           drain=True, inc=(eV, 1))
        dma(self.rowb_d[1:2, :], self.sc2_rowb, wait=[(eV, None)])
        dma(self.rowb_d[2:3, :], self.iv2_rowb)
        dma(sc2rep, self.rowb_d[1:2, :].broadcast_to([128, T]), wait=[(qst, None)])
        dma(iv2rep, self.rowb_d[2:3, :].broadcast_to([128, T]))
        self.p32.pop()
        self.p32.push()
        ffn_f = self.p32.alloc(KD * 512).rearrange("p (m t) -> p m t", m=KD)
        qv = [self.p32.alloc(512) for _ in range(2)]
        ffn_ev = 0
        fc_clear = 0
        pt_use = [0, 0]
        w2_use = [0, 0]
        self.warm(wait=[(self.eV, None)])
        for fc in range(NFC):
            for kh in range(KH):
                pr = kh % 2
                th = dma(h2r[pr], self.hs_d[b, kh, :, fc * 512:(fc + 1) * 512],
                         wait=[(eT, w2_use[pr]), (qst, None)])
                tW = dma(w2t[pr], self.w2t_d[kh])
                ev(lambda e, pr=pr, fc=fc: e.tensor_mul(
                    qv[pr], h2r[pr], sc2rep[:, fc * 512:(fc + 1) * 512]),
                   wait=[(qio, th), (eT, w2_use[pr])], drain=True, inc=(eV, 1))
                ev(lambda e, pr=pr: e.tensor_scalar(
                    f16h[pr][:, 0:512], qv[pr], MAGIC16, None, ALU.add),
                   drain=True, inc=(eV, 1))
                ev(lambda e, pr=pr: e.tensor_scalar(
                    h2s[pr], f16h[pr][:, 0:512], MAGIC16, None, ALU.subtract),
                   drain=True, inc=(eV, 1))
                hq_done = eV.n
                for m in range(KD):
                    wl = [(qio, tW), (eV, hq_done), (eV, fc_clear)]
                    if fc == 0 and kh == 0 and m == KD - 1:
                        wl.append((eT, None))
                    ep(lambda e, pr=pr, m=m, kh=kh: e.matmul(
                        ps[:, m * 512:(m + 1) * 512],
                        w2t[pr][:, m * 128:(m + 1) * 128], h2s[pr],
                        start=(kh == 0), stop=(kh == KH - 1)),
                        wait=wl,
                        inc=(eT, 1) if m == KD - 1 else None)
                w2_use[pr] = eT.n
            for m in range(KD):
                ev(lambda e, m=m, fc=fc: e.tensor_mul(
                    ffn_f[:, m, :], ps[:, m * 512:(m + 1) * 512],
                    iv2rep[:, fc * 512:(fc + 1) * 512]),
                   wait=[(eT, None)], drain=True, inc=(eV, 1))
            ffn_ev = eV.n
            for tt in range(4):
                pr = tt % 2
                rows = fc * 512 + tt * 128
                for m in range(KD):
                    ep(lambda e, m=m, tt=tt, pr=pr: e.transpose(
                        ps[:, pr * 1024 + m * 128:pr * 1024 + (m + 1) * 128],
                        ffn_f[:, m, tt * 128:(tt + 1) * 128], self.idf[:]),
                        wait=[(eV, ffn_ev), (eV, pt_use[pr])],
                        inc=(eT, 1) if m == KD - 1 else None)
                tX = dma(xtk[pr], self.x1_d[b, rows:rows + 128, :],
                         wait=[(eV, xtk_use[pr]), (qst, None)])
                ev(lambda e, pr=pr: e.tensor_add(r1b[pr], ps[:, pr * 1024:pr * 1024 + D],
                                                 xtk[pr]),
                   wait=[(eT, None), (qio, tX)], drain=True, inc=(eV, 1))
                pt_use[pr] = eV.n
                xtk_use[pr] = eV.n
                layer_norm(r1b[pr], pr, outb[pr], wait_out=[(qout, None)])
                dma(self.out_h[b, rows:rows + 128, :], outb[pr],
                    wait=[(eV, None)], sem=qout)
            fc_clear = eV.n
        self.p32.pop()
        self.p32.pop()
        self.p16.pop()
        self.pf16.pop()


def build_program(cfg, dbg=False):
    return Builder(cfg).build(dbg=dbg)


def kernel(**inputs):
    import ml_dtypes
    cfg = Cfg(small=False)
    x = np.ascontiguousarray(np.asarray(inputs["L0_attn_x"], dtype=np.float32))
    n_cores = 8
    shards = np.split(x, n_cores, axis=0)
    base = {k: np.ascontiguousarray(np.asarray(v, dtype=np.float32))
            for k, v in inputs.items() if k != "L0_attn_x"}
    ident_f = np.eye(128, dtype=np.float32)
    ident_b = np.eye(128, dtype=ml_dtypes.bfloat16)
    nc = build_program(cfg)
    in_maps = []
    for i in range(n_cores):
        m = dict(base)
        m["L0_attn_x"] = shards[i]
        m["ident_f32"] = ident_f
        m["ident_bf16"] = ident_b
        in_maps.append(m)
    res = run_bass_kernel_spmd(nc, in_maps, core_ids=list(range(n_cores)))
    return np.concatenate([r["out"] for r in res.results], axis=0)


# version 26
# speedup vs baseline: 1.0376x; 1.0029x over previous
import numpy as np

import concourse.bass as bass
import concourse.mybir as mybir
import concourse.library_config as libcfg
from concourse.bass_isa_rust import ReduceOp
from concourse.bass_utils import run_bass_kernel_spmd

F32 = mybir.dt.float32
BF16 = mybir.dt.bfloat16
F16 = mybir.dt.float16
ALU = mybir.AluOpType
AF = mybir.ActivationFunctionType
AX = mybir.AxisListType

MAGIC = float(np.float32(3 * 2 ** 22))
MAGIC16 = 1536.0


class Cfg:
    def __init__(self, small=False):
        if small:
            self.B, self.T, self.D, self.DI, self.DH, self.Tc = 1, 512, 512, 512, 1024, 256
        else:
            self.B, self.T, self.D, self.DI, self.DH, self.Tc = 2, 2048, 1024, 2048, 4096, 1024
        self.NS = 64
        self.KD = self.D // 128
        self.KI = self.DI // 128
        self.KH = self.DH // 128
        self.NTC = self.T // self.Tc
        self.NFC = self.T // 512
        self.NTT = self.T // 128
        self.TT_C = self.Tc // 128
        self.NDH = 2
        self.DPH = self.KI // self.NDH
        self.GD = min(4, self.DPH)
        self.NDG = self.DPH // self.GD
        self.NG = 4
        self.NNG = self.NS // self.NG
        self.HBS = min(512, self.Tc)
        self.HB = self.Tc // self.HBS
        assert self.GD * self.Tc * 4 <= 8 * 2048


ENGINES = ("pe", "ve", "se", "ge", "sy")


class Sem:
    __slots__ = ("h", "n", "name", "dma", "last_waited", "boundary")

    def __init__(self, h, name, dma=False):
        self.h, self.n, self.name, self.dma = h, 0, name, dma
        self.last_waited = 0
        self.boundary = 0

    def bump(self, k):
        self.n += k
        return self.n


class Pool:
    def __init__(self, t, elems):
        self.t, self.elems, self.off, self.marks = t, elems, 0, []

    def alloc(self, n):
        assert self.off + n <= self.elems, f"pool overflow: {self.off}+{n}>{self.elems}"
        o = self.off
        self.off += n
        return self.t[:, o:o + n]

    def push(self):
        self.marks.append(self.off)

    def pop(self):
        self.off = self.marks.pop()


class Builder:
    def __init__(self, cfg):
        self.c = cfg
        self.nc = bass.Bass("TRN2", target_bir_lowering=False)
        self.ops = {e: [] for e in ENGINES}
        self.lastwait = {e: {} for e in ENGINES}
        self._guards = []
        self.dma_sems = []

    def emit(self, e, fn, wait=(), drain=False, inc=None):
        if drain:
            sem = getattr(self, "engine_sem", {}).get(e)
            if sem is not None:
                wait = list(wait) + [(sem, sem.n)]
                drain = False
        wl = []
        for sem, cnt in wait:
            if cnt is None or sem.dma:
                cnt = sem.n
            if sem.dma:
                sem.last_waited = max(sem.last_waited, cnt)
            if cnt > self.lastwait[e].get(id(sem), -1):
                wl.append((sem, cnt))
                self.lastwait[e][id(sem)] = cnt
        if e == "sy" and wl:
            for s in self.dma_sems:
                s.boundary = s.n
        if inc is not None:
            inc[0].bump(inc[1])

        def op(eng, wl=wl, drain=drain, fn=fn, inc=inc):
            for sem, cnt in wl:
                eng.wait_ge(sem.h, cnt)
            if drain:
                eng.drain()
            r = fn(eng)
            if inc is not None:
                r.then_inc(inc[0].h, inc[1])

        self.ops[e].append(op)

    def barrier(self):
        b = self.bar
        tgt = b.n + len(ENGINES)
        b.n = tgt
        dwaits = [(s.h, s.n) for s in self.dma_sems if s.n > 0]
        for s in self.dma_sems:
            s.last_waited = max(s.last_waited, s.n)
            s.boundary = s.n
        for e in ENGINES:
            def op(eng, tgt=tgt, b=b, dwaits=dwaits):
                for h, n in dwaits:
                    eng.wait_ge(h, n)
                eng.drain()
                eng.sem_inc(b.h, 1)
                eng.wait_ge(b.h, tgt)
            self.ops[e].append(op)
            self.lastwait[e] = {}

    def dma(self, dst, src, wait=(), sem=None, slow=False):
        if sem is None:
            try:
                is_store = "DRAM" in str(dst.space)
            except Exception:
                is_store = False
            sem = self.qst if is_store else self.qio
        wait = list(wait)
        if sem.last_waited > sem.boundary:
            wait.append((sem, sem.last_waited))
            self.lastwait["sy"].pop(id(sem), None)
        self.emit("sy", lambda e, d=dst, s=src, sl=slow: e.dma_start(
            out=d, in_=s, allow_slow_non_contiguous=sl),
                  wait=wait, inc=(sem, 16))
        return sem.n

    def ev(self, fn, **kw):
        self.emit("ve", fn, **kw)

    def es(self, fn, **kw):
        self.emit("se", fn, **kw)

    def ep(self, fn, **kw):
        self.emit("pe", fn, **kw)

    def eg(self, fn, **kw):
        self.emit("ge", fn, **kw)

    def sb(self, name, shape, dtype):
        g = self.nc.sbuf_tensor(name, list(shape), dtype)
        t = g.__enter__()
        self._guards.append(g)
        return t

    def build(self, dbg=False):
        nc, c = self.nc, self.c
        B, T, D, DI, DH, NS = c.B, c.T, c.D, c.DI, c.DH, c.NS
        KD, KI, KH, Tc = c.KD, c.KI, c.KH, c.Tc

        din = lambda n, s: nc.dram_tensor(n, list(s), F32, kind="ExternalInput").ap()
        self.x_h = din("L0_attn_x", (B, T, D))
        self.ipw_h = din("L0_attn_in_proj_w", (D, 2 * DI))
        self.mcw_h = din("L0_attn_conv_w", (DI, 1, 4))
        self.mcb_h = din("L0_attn_conv_b", (DI,))
        self.xpw_h = din("L0_attn_x_proj_w", (DI, 2 * NS + 1))
        self.dtw_h = din("L0_attn_dt_w", (1, DI))
        self.dtb_h = din("L0_attn_dt_b", (DI,))
        self.alog_h = din("L0_attn_A_log", (DI, NS))
        self.dvec_h = din("L0_attn_D", (DI,))
        self.opw_h = din("L0_attn_out_proj_w", (DI, D))
        self.w1_h = din("L0_ffn_w1", (DH, D))
        self.fcw_h = din("L0_ffn_conv_w", (DH, 1, 3))
        self.fcb_h = din("L0_ffn_conv_b", (DH,))
        self.w2_h = din("L0_ffn_w2", (D, DH))
        self.idf_h = din("ident_f32", (128, 128))
        self.idb_h = nc.dram_tensor("ident_bf16", [128, 128], BF16,
                                    kind="ExternalInput").ap()
        self.out_h = nc.dram_tensor("out", [B, T, D], F32, kind="ExternalOutput").ap()

        scr = lambda n, s, d: nc.dram_tensor(n, list(s), d, kind="Internal").ap()
        self.ipw_d = scr("ipw_bf", (2 * KI, 128, KD * 128), BF16)
        self.opw_d = scr("opw_bf", (KD, 128, KI * 128), BF16)
        self.w1t_d = scr("w1qT", (KH, 128, KD * 128), BF16)
        self.w2t_d = scr("w2qT", (KH, 128, KD * 128), BF16)
        self.z_d = scr("z_dram", (B, KI, 128, T), BF16)
        self.xs_d = scr("xs_dram", (B, KI, 128, T), BF16)
        self.bmat_d = scr("bmat_dram", (B, NS, T), BF16)
        self.cmat_d = scr("cmat_dram", (B, NS, T), BF16)
        self.y_d = scr("y_dram", (B, KI, 128, T), BF16)
        self.x1_d = scr("x1_dram", (B, T, D), F32)
        self.hs_d = scr("hs_dram", (B, KH, 128, T), BF16)
        self.h2q_d = scr("h2q_dram", (B, KH, 128, T), BF16)
        self.itok_d = scr("itok_dram", (128, c.NTT), F32)
        self.gtok_d = scr("gtok_dram", (128, c.NTT), F32)
        self.row_d = scr("row_dram", (2, T), F32)
        self.rowb_d = scr("rowb_dram", (4, T), BF16)
        self.scl_d = scr("scl_dram", (2, 2), F32)

        self.idf = self.sb("idf", (128, 128), F32)
        self.idb = self.sb("idb", (128, 128), BF16)
        self.mcw = self.sb("mcw", (128, KI, 4), F32)
        self.mcb = self.sb("mcb", (128, KI), F32)
        self.dtw = self.sb("dtw", (128, KI), F32)
        self.dtb = self.sb("dtb", (128, KI), F32)
        self.dvec = self.sb("dvec", (128, KI), F32)
        self.aneg = self.sb("aneg", (128, KI, NS), F32)
        self.fcw = self.sb("fcw", (128, KH, 3), F32)
        self.fcb = self.sb("fcb", (128, KH), F32)
        self.xpw = self.sb("xpw", (128, KI, 144), BF16)
        self.sw1 = self.sb("sw1", (128, 1), F32)
        self.iw1 = self.sb("iw1", (128, 1), F32)
        self.sw2 = self.sb("sw2", (128, 1), F32)
        self.iw2 = self.sb("iw2", (128, 1), F32)
        self.rowbuf = self.sb("rowbuf", (128, T), F32)
        self.gm_row = self.rowbuf[0:1, 0:T]
        self.dtr_row = self.rowbuf[32:33, 0:T]
        self.is_row = self.rowbuf[64:65, 0:T]
        self.rowb16 = self.sb("rowb16", (128, 2 * T), BF16)
        self.sc2_rowb = self.rowb16[0:1, 0:T]
        self.iv2_rowb = self.rowb16[0:1, T:2 * T]
        self.is_rowb = self.rowb16[64:65, 0:T]
        self.smv = self.sb("smv", (128, 16), F32)
        self.smb = self.sb("smb", (128, 5 * c.TT_C), F32)
        self.itok = self.sb("itok", (128, c.NTT), F32)

        self.p32 = Pool(self.sb("p32", (128, 16640), F32), 16640)
        self.p16 = Pool(self.sb("p16", (128, 49408), BF16), 49408)
        self.pf16 = Pool(self.sb("pf16", (128, 8192), F16), 8192)
        pg = self.nc.psum_tensor("psum", [128, 4096], F32)
        self.psum = pg.__enter__()
        self._guards.append(pg)
        self.psum_b = self.psum[:, :].bitcast(BF16)

        self.sems = {}
        for name in ("bar", "qio", "qst", "qout", "eV", "eS", "eT", "eG"):
            g = nc.semaphore(name)
            h = g.__enter__()
            self._guards.append(g)
            self.sems[name] = Sem(h, name, dma=name.startswith("q"))
        self.bar = self.sems["bar"]
        self.qio = self.sems["qio"]
        self.qst = self.sems["qst"]
        self.qout = self.sems["qout"]
        self.dma_sems = [self.sems[n] for n in ("qio", "qst", "qout")]
        self.eV = self.sems["eV"]
        self.eS = self.sems["eS"]
        self.eT = self.sems["eT"]
        self.eG = self.sems["eG"]
        self.engine_sem = {"ve": self.eV, "se": self.eS, "pe": self.eT,
                           "ge": self.eG}

        self.phase0()
        self.barrier()
        for b in range(B):
            self.phase1(b)
            self.barrier()
            self.scan(b)
            self.barrier()
            self.phase34(b)
            self.barrier()
        if dbg:
            for name, t in (("d_xs", self.xs_d), ("d_z", self.z_d),
                            ("d_bm", self.bmat_d), ("d_cm", self.cmat_d),
                            ("d_y", self.y_d), ("d_x1", self.x1_d),
                            ("d_hs", self.hs_d), ("d_h2q", self.h2q_d)):
                o = nc.dram_tensor(name, list(t.shape), t.dtype,
                                   kind="ExternalOutput").ap()
                self.dma(o, t, sem=self.qout)
        self.emit("sy", lambda e: e.nop(), wait=[(self.qout, None)])

        with nc.Block() as block:
            block.sync(lambda e: [op(e) for op in self.ops["sy"]])
            block.vector(lambda e: [op(e) for op in self.ops["ve"]])
            block.scalar(lambda e: [op(e) for op in self.ops["se"]])
            block.tensor(lambda e: [op(e) for op in self.ops["pe"]])
            block.gpsimd(lambda e: [op(e) for op in self.ops["ge"]])

        for g in reversed(self._guards):
            g.__exit__(None, None, None)
        return nc

    def phase0(self):
        c, KD, KI, KH = self.c, self.c.KD, self.c.KI, self.c.KH
        D, DH, NS = self.c.D, self.c.DH, self.c.NS
        qio, qst = self.qio, self.qst
        eV, eS, eT, eG = self.eV, self.eS, self.eT, self.eG
        dma, ev, es, ep, eg = self.dma, self.ev, self.es, self.ep, self.eg

        dma(self.idf[:], self.idf_h)
        dma(self.idb[:], self.idb_h)
        for kc in range(KI):
            dma(self.mcw[:, kc, :], self.mcw_h[kc * 128:(kc + 1) * 128, 0, :])
            dma(self.aneg[:, kc, :], self.alog_h[kc * 128:(kc + 1) * 128, :])
        dma(self.mcb[:], self.mcb_h.rearrange("(c p) -> p c", p=128), slow=True)
        dma(self.dtw[:], self.dtw_h[0, :].rearrange("(c p) -> p c", p=128), slow=True)
        dma(self.dtb[:], self.dtb_h.rearrange("(c p) -> p c", p=128), slow=True)
        dma(self.dvec[:], self.dvec_h.rearrange("(c p) -> p c", p=128), slow=True)
        for kc in range(KH):
            dma(self.fcw[:, kc, :], self.fcw_h[kc * 128:(kc + 1) * 128, 0, :])
        dma(self.fcb[:], self.fcb_h.rearrange("(c p) -> p c", p=128), slow=True)
        ld0 = qio.n

        es(lambda e: e.activation(self.aneg[:], self.aneg[:], AF.Exp),
           wait=[(qio, ld0)], inc=(eS, 1))
        ev(lambda e: e.tensor_scalar(self.aneg[:], self.aneg[:], -1.0, None, ALU.mult),
           wait=[(eS, None)], inc=(eV, 1))

        self.p32.push()
        stg = self.p32.alloc(2 * NS + 1)
        for kc in range(KI):
            t = dma(stg, self.xpw_h[kc * 128:(kc + 1) * 128, :], wait=[(eV, None)])
            ev(lambda e, kc=kc: e.tensor_copy(self.xpw[:, kc, 0:2 * NS + 1], stg),
               wait=[(qio, t)], drain=True, inc=(eV, 1))
        self.p32.pop()

        DI2 = 2 * self.c.DI
        self.p32.push()
        self.p16.push()
        wstg = [self.p32.alloc(max(DI2, DH)) for _ in range(2)]
        wbig = self.p16.alloc(KD * DI2).rearrange("p (k m) -> p k m", k=KD)
        mstg = [self.p16.alloc(KI * 128) for _ in range(2)]
        cast_ev = [eV.n, eV.n]
        for kc in range(KD):
            pr = kc % 2
            t = dma(wstg[pr][:, :DI2], self.ipw_h[kc * 128:(kc + 1) * 128, :],
                    wait=[(eV, cast_ev[pr])])
            ev(lambda e, pr=pr, kc=kc: e.tensor_copy(wbig[:, kc, :],
                                                     wstg[pr][:, :DI2]),
               wait=[(qio, t)], inc=(eV, 1))
            cast_ev[pr] = eV.n
        for m in range(2 * KI):
            pr = m % 2
            ev(lambda e, pr=pr, m=m: e.tensor_copy(
                mstg[pr][:, :KD * 128],
                wbig[:, :, m * 128:(m + 1) * 128]),
               wait=[(eV, None), (qst, None)], inc=(eV, 1))
            dma(self.ipw_d[m], mstg[pr][:, :KD * 128], wait=[(eV, None)])
        cast_ev = [eV.n, eV.n]
        for kc in range(KI):
            pr = kc % 2
            t = dma(wstg[pr][:, :D], self.opw_h[kc * 128:(kc + 1) * 128, :],
                    wait=[(eV, cast_ev[pr])])
            ev(lambda e, pr=pr, kc=kc: e.tensor_copy(
                wbig.rearrange("p k m -> p (k m)")[:, kc * D:(kc + 1) * D],
                wstg[pr][:, :D]),
               wait=[(qio, t), (qst, None)], inc=(eV, 1))
            cast_ev[pr] = eV.n
        obig = wbig.rearrange("p k m -> p (k m)")[:, :KI * D].rearrange(
            "p (k m) -> p k m", k=KI)
        for m in range(KD):
            pr = m % 2
            ev(lambda e, pr=pr, m=m: e.tensor_copy(
                mstg[pr], obig[:, :, m * 128:(m + 1) * 128]),
               wait=[(eV, None), (qst, None)], inc=(eV, 1))
            dma(self.opw_d[m], mstg[pr], wait=[(eV, None)])

        acc = self.p32.alloc(4)
        ev(lambda e: e.memset(acc[:, 0:2], 0.0), inc=(eV, 1))
        for src_, col, rows, cols in ((self.w1_h, 0, KH, D), (self.w2_h, 1, KD, DH)):
            for r in range(rows):
                pr = r % 2
                t = dma(wstg[pr][:, :cols], src_[r * 128:(r + 1) * 128, :],
                        wait=[(eV, None)])
                ev(lambda e, pr=pr, col=col, cols=cols: e.reduce_sum(
                    acc[:, 2 + col:3 + col], wstg[pr][:, :cols], axis=AX.X,
                    apply_absolute_value=True),
                   wait=[(qio, t)], drain=True, inc=(eV, 1))
                ev(lambda e, col=col: e.tensor_add(
                    acc[:, col:col + 1], acc[:, col:col + 1], acc[:, 2 + col:3 + col]),
                   drain=True, inc=(eV, 1))
        ep(lambda e: e.transpose(self.psum[0:2, 0:128], acc[:, 0:2], self.idf[:]),
           wait=[(eV, None), (eS, None)], inc=(eT, 1))
        ev(lambda e: e.reduce_sum(acc[0:2, 2:3], self.psum[0:2, 0:128], axis=AX.X),
           wait=[(eT, None)], drain=True, inc=(eV, 1))
        scale = 1.0 / (DH * D)
        ev(lambda e: e.tensor_scalar(acc[0:2, 2:3], acc[0:2, 2:3], scale, 1e-5,
                                     ALU.mult, ALU.max), drain=True, inc=(eV, 1))
        ev(lambda e: e.tensor_scalar(acc[0:2, 3:4], acc[0:2, 2:3], 1.0 / 127.0, None,
                                     ALU.mult), drain=True, inc=(eV, 1))
        ev(lambda e: e.reciprocal(acc[0:2, 0:1], acc[0:2, 2:3]), drain=True,
           inc=(eV, 1))
        dma(self.scl_d[0:1, :], acc[0:2, 0:1], wait=[(eV, None)], slow=True)
        dma(self.scl_d[1:2, :], acc[0:2, 3:4], wait=[(eV, None)], slow=True)
        dma(self.sw1[:], self.scl_d[0:1, 0:1].broadcast_to([128, 1]), slow=True,
            wait=[(qst, None)])
        dma(self.sw2[:], self.scl_d[0:1, 1:2].broadcast_to([128, 1]), slow=True)
        dma(self.iw1[:], self.scl_d[1:2, 0:1].broadcast_to([128, 1]), slow=True)
        dma(self.iw2[:], self.scl_d[1:2, 1:2].broadcast_to([128, 1]), slow=True)

        self.pf16.push()
        f16w = self.pf16.alloc(max(D, DH))
        qb = [self.p16.alloc(max(D, DH)) for _ in range(2)]
        rslab = [self.p16.alloc(KD * 128) for _ in range(2)]
        w2big = wbig.rearrange("p k m -> p (k m)")[:, :KH * KD * 128].rearrange(
            "p (h m q) -> p h m q", h=KH, m=KD)
        for src_, rows, cols, scol, w2 in ((self.w1_h, KH, D, self.sw1, False),
                                           (self.w2_h, KD, DH, self.sw2, True)):
            for r in range(rows):
                pr = r % 2
                t = dma(wstg[pr][:, :cols], src_[r * 128:(r + 1) * 128, :],
                        wait=[(eV, None)])
                ev(lambda e, pr=pr, cols=cols, scol=scol: e.tensor_scalar(
                    wstg[pr][:, :cols], wstg[pr][:, :cols], scol[:, 0:1], None,
                    ALU.mult), wait=[(qio, t), (eV, None), (eT, None)], drain=True,
                   inc=(eV, 1))
                ev(lambda e, pr=pr, cols=cols: e.tensor_scalar(
                    wstg[pr][:, :cols], wstg[pr][:, :cols], 1.49, -1.49,
                    ALU.min, ALU.max), drain=True, inc=(eV, 1))
                ev(lambda e, pr=pr, cols=cols: e.tensor_scalar(
                    f16w[:, :cols], wstg[pr][:, :cols], MAGIC16, None,
                    ALU.add), drain=True, inc=(eV, 1))
                ev(lambda e, pr=pr, cols=cols: e.tensor_scalar(
                    qb[pr][:, :cols], f16w[:, :cols], MAGIC16, None,
                    ALU.subtract), drain=True, inc=(eV, 1))
                nblk = cols // 128
                for g0 in range(0, nblk, 4):
                    gn = min(4, nblk - g0)
                    for j in range(gn):
                        kc = g0 + j
                        self.ep(lambda e, pr=pr, kc=kc, j=j: e.transpose(
                            self.psum_b[:, j * 128:(j + 1) * 128],
                            qb[pr][:, kc * 128:(kc + 1) * 128], self.idb[:]),
                            wait=[(eV, None), (eS, None)],
                            inc=(eT, 1) if j == gn - 1 else None)
                    if w2:
                        es(lambda e, r=r, g0=g0, gn=gn: e.copy(
                            w2big[:, g0:g0 + gn, r, :],
                            self.psum_b[:, :gn * 128].rearrange(
                                "p (a q) -> p a q", a=gn)),
                           wait=[(eT, None)], inc=(eS, 1))
                    else:
                        es(lambda e, pr=pr, g0=g0, gn=gn: e.copy(
                            rslab[pr][:, g0 * 128:(g0 + gn) * 128],
                            self.psum_b[:, :gn * 128]),
                           wait=[(eT, None), (qst, None)], inc=(eS, 1))
                if not w2:
                    dma(self.w1t_d[r], rslab[pr], wait=[(eS, None)])
        for kh in range(KH):
            dma(self.w2t_d[kh],
                w2big.rearrange("p h m q -> p (h m q)")
                [:, kh * KD * 128:(kh + 1) * KD * 128],
                wait=[(eS, None)])
        self.p32.pop()
        self.p16.pop()
        self.pf16.pop()
        self.w_prep_tgt = self.qio.n

    def phase1(self, b):
        c = self.c
        KD, KI, T, D, NS, NFC, NTT = c.KD, c.KI, c.T, c.D, c.NS, c.NFC, c.NTT
        qio, qst = self.qio, self.qst
        eV, eS, eT, eG = self.eV, self.eS, self.eT, self.eG
        dma, ev, es, ep, eg = self.dma, self.ev, self.es, self.ep, self.eg
        ps = self.psum

        self.p32.push()
        self.p16.push()
        xT = self.p16.alloc(KD * T).rearrange("p (k t) -> p k t", k=KD)
        xtok = [self.p32.alloc(D) for _ in range(2)]
        self.warm()
        tr_ev = [0, 0]
        ev_ev = [0, 0]
        for tt in range(NTT):
            pr = tt % 2
            t = dma(xtok[pr], self.x_h[b, tt * 128:(tt + 1) * 128, :],
                    wait=[(eT, tr_ev[pr])])
            for g in range(KD // 4):
                for j in range(4):
                    kc = g * 4 + j
                    ep(lambda e, pr=pr, kc=kc, j=j: e.transpose(
                        ps[:, pr * 1024 + (kc % 4) * 128 + (kc // 4) * 512:
                           pr * 1024 + (kc % 4) * 128 + (kc // 4) * 512 + 128],
                        xtok[pr][:, kc * 128:(kc + 1) * 128], self.idf[:]),
                        wait=[(qio, t), (eS, ev_ev[pr])],
                        inc=(eT, 1) if (g == KD // 4 - 1 and j == 3) else None)
            tr_ev[pr] = eT.n
            for g in range(KD // 4):
                es(lambda e, pr=pr, g=g, tt=tt: e.copy(
                    xT[:, g * 4:(g + 1) * 4, tt * 128:(tt + 1) * 128],
                    ps[:, pr * 1024 + g * 512:pr * 1024 + (g + 1) * 512]
                    .rearrange("p (k t) -> p k t", k=4)),
                   wait=[(eT, None)], inc=(eS, 1))
            ev_ev[pr] = eS.n
        xT_done = eS.n

        xs_pad = [self.p16.alloc(T + 3) for _ in range(2)]
        xss = [self.p16.alloc(T) for _ in range(2)]
        xc = [self.p32.alloc(T) for _ in range(2)]
        bcacc = self.p32.alloc(T)
        wt = [self.p16.alloc(KD * 128) for _ in range(2)]
        zst = [self.p16.alloc(512) for _ in range(2)]
        wt_use = [0, 0]
        ip_ev = [0, 0]
        xss_use = [0, 0]
        silu_done = [0, 0]
        conv_done = [0, 0]
        xs_spill = [0, 0]
        zspill = [0, 0]
        zp_ev = [0, 0]
        dt_ev = [0, 0]
        bc_ev = [0, 0]
        ev(lambda e: e.memset(self.dtr_row[:], 0.0), inc=(eV, 1))

        def xp_block(dch):
            pr = dch % 2
            for fc in range(NFC):
                bpr = (dch * NFC + fc) % 2
                ep(lambda e, pr=pr, dch=dch, fc=fc, bpr=bpr: e.matmul(
                    ps[:, 2048 + bpr * 512:2048 + (bpr + 1) * 512],
                    self.xpw[:, dch, 0:128], xss[pr][:, fc * 512:(fc + 1) * 512],
                    start=True, stop=True),
                    wait=[(eS, silu_done[pr]), (eV, bc_ev[bpr])], inc=(eT, 1))
                if dch == 0:
                    ev(lambda e, fc=fc, bpr=bpr: e.tensor_copy(
                        bcacc[:, fc * 512:(fc + 1) * 512],
                        ps[:, 2048 + bpr * 512:2048 + (bpr + 1) * 512]),
                       wait=[(eT, None)], drain=True, inc=(eV, 1))
                else:
                    ev(lambda e, fc=fc, bpr=bpr: e.scalar_tensor_tensor(
                        bcacc[:, fc * 512:(fc + 1) * 512],
                        ps[:, 2048 + bpr * 512:2048 + (bpr + 1) * 512], 1.0,
                        bcacc[:, fc * 512:(fc + 1) * 512], ALU.mult, ALU.add),
                       wait=[(eT, None)], drain=True, inc=(eV, 1))
                bc_ev[bpr] = eV.n
                dpr = (dch * NFC + fc) % 2
                ep(lambda e, pr=pr, dch=dch, fc=fc, dpr=dpr: e.matmul(
                    ps[0:1, 1024 + dpr * 512:1024 + dpr * 512 + 512],
                    self.xpw[:, dch, 128:129], xss[pr][:, fc * 512:(fc + 1) * 512],
                    start=True, stop=True),
                    wait=[(eV, dt_ev[dpr])], inc=(eT, 1))
                ev(lambda e, fc=fc, dpr=dpr: e.scalar_tensor_tensor(
                    self.dtr_row[0:1, fc * 512:(fc + 1) * 512],
                    ps[0:1, 1024 + dpr * 512:1024 + dpr * 512 + 512], 1.0,
                    self.dtr_row[0:1, fc * 512:(fc + 1) * 512], ALU.mult, ALU.add),
                   wait=[(eT, None)], drain=True, inc=(eV, 1))
                dt_ev[dpr] = eV.n
            xss_use[pr] = eT.n

        zwt = [self.p16.alloc(KD * 128) for _ in range(2)]
        zwt_use = [0, 0]

        def z_block(mz):
            pr = mz % 2
            tW = dma(zwt[pr], self.ipw_d[KI + mz], wait=[(eT, zwt_use[pr])])
            for fc in range(NFC):
                fpr = fc % 2
                for kc in range(KD):
                    wl = [(qio, tW), (eS, zp_ev[fpr])]
                    if mz == 0 and fc == 1 and kc == 0:
                        wl.append((eT, None))
                    ep(lambda e, pr=pr, fpr=fpr, kc=kc, fc=fc: e.matmul(
                        ps[:, 3072 + fpr * 512:3072 + (fpr + 1) * 512],
                        zwt[pr][:, kc * 128:(kc + 1) * 128],
                        xT[:, kc, fc * 512:(fc + 1) * 512],
                        start=(kc == 0), stop=(kc == KD - 1)),
                        wait=wl,
                        inc=(eT, 1) if kc == KD - 1 else None)
                es(lambda e, fpr=fpr: e.copy(zst[fpr],
                                             ps[:, 3072 + fpr * 512:
                                                3072 + (fpr + 1) * 512]),
                   wait=[(eT, None), (qst, None)], inc=(eS, 1))
                zspill[fpr] = dma(self.z_d[b, mz, :, fc * 512:(fc + 1) * 512],
                                  zst[fpr], wait=[(eS, None)])
                zp_ev[fpr] = eS.n
            zwt_use[pr] = eT.n


        for dch in range(KI):
            pr = dch % 2
            tW = dma(wt[pr], self.ipw_d[dch],
                     wait=[(eT, wt_use[pr]), (qst, None)])
            ev(lambda e, pr=pr: e.memset(xs_pad[pr][:, 0:3], 0.0), inc=(eV, 1))
            pad_ev = eV.n
            for fc in range(NFC):
                fpr = fc % 2
                for kc in range(KD):
                    ep(lambda e, pr=pr, fpr=fpr, kc=kc, fc=fc: e.matmul(
                        ps[:, fpr * 512:(fpr + 1) * 512],
                        wt[pr][:, kc * 128:(kc + 1) * 128],
                        xT[:, kc, fc * 512:(fc + 1) * 512],
                        start=(kc == 0), stop=(kc == KD - 1)),
                        wait=[(qio, tW), (eS, ip_ev[fpr]), (eS, xT_done)],
                        inc=(eT, 1) if kc == KD - 1 else None)
                es(lambda e, pr=pr, fpr=fpr, fc=fc: e.copy(
                    xs_pad[pr][:, 3 + fc * 512:3 + (fc + 1) * 512],
                    ps[:, fpr * 512:(fpr + 1) * 512]),
                   wait=[(eT, None), (eV, conv_done[pr])], inc=(eS, 1))
                ip_ev[fpr] = eS.n
            wt_use[pr] = eT.n
            ev(lambda e, pr=pr, dch=dch: e.tensor_scalar(
                xc[pr], xs_pad[pr][:, 0:T], self.mcw[:, dch, 0:1], None, ALU.mult),
               wait=[(eS, None), (eV, pad_ev)], inc=(eV, 1))
            for k in range(1, 4):
                ev(lambda e, pr=pr, k=k, dch=dch: e.scalar_tensor_tensor(
                    xc[pr], xs_pad[pr][:, k:k + T], self.mcw[:, dch, k:k + 1],
                    xc[pr], ALU.mult, ALU.add), drain=True, inc=(eV, 1))
            es(lambda e, pr=pr, dch=dch: e.activation(
                xss[pr], xc[pr], AF.Silu, bias=self.mcb[:, dch:dch + 1], scale=1.0),
               wait=[(eV, None), (eT, xss_use[pr]), (qst, None)],
               inc=(eS, 1))
            silu_done[pr] = eS.n
            xs_spill[pr] = dma(self.xs_d[b, dch], xss[pr], wait=[(eS, None)])
            if dch > 0:
                xp_block(dch - 1)
        xp_block(KI - 1)
        for mz in range(KI):
            z_block(mz)
        bst = self.p16.alloc(512)
        cst = self.p16.alloc(512)
        for fc in range(NFC):
            es(lambda e, fc=fc: e.copy(bst[0:64, :],
                                       bcacc[0:64, fc * 512:(fc + 1) * 512]),
               wait=[(eV, None), (qst, None)], inc=(eS, 1))
            es(lambda e, fc=fc: e.copy(cst[64:128, :],
                                       bcacc[64:128, fc * 512:(fc + 1) * 512]),
               wait=[(qst, None)], inc=(eS, 1))
            t1 = dma(self.bmat_d[b, :, fc * 512:(fc + 1) * 512], bst[0:64, :],
                     wait=[(eS, None)])
            t2 = dma(self.cmat_d[b, :, fc * 512:(fc + 1) * 512], cst[64:128, :])
        self.bc_tgt = self.qio.n
        self.p32.pop()
        self.p16.pop()

    def scan(self, b):
        c = self.c
        KI, T, NS, Tc = c.KI, c.T, c.NS, c.Tc
        NDH, DPH, GD, NDG, NG, NNG, HB = c.NDH, c.DPH, c.GD, c.NDG, c.NG, c.NNG, c.HB
        qio, qst = self.qio, self.qst
        eV, eS, eT, eG = self.eV, self.eS, self.eT, self.eG
        dma, ev, es, ep, eg = self.dma, self.ev, self.es, self.ep, self.eg
        ps = self.psum

        self.p32.push()
        self.p16.push()
        self.pf16.push()
        dtrep = self.p32.alloc(T)
        dtt = self.p32.alloc(GD * Tc).rearrange("p (d t) -> p d t", d=GD)
        spt = self.p32.alloc(Tc)
        yfull = self.p32.alloc(Tc)
        gz = self.p32.alloc(Tc)
        ST = self.p32.alloc(KI * NS)
        dtx = self.p16.alloc(GD * Tc).rearrange("p (d t) -> p d t", d=GD)
        xssl = self.p16.alloc(GD * Tc).rearrange("p (d t) -> p d t", d=GD)
        brep = self.p16.alloc(2 * NG * Tc).rearrange("p (r n t) -> p r n t", r=2, n=NG)
        crep = self.p16.alloc(2 * NG * Tc).rearrange("p (r n t) -> p r n t", r=2, n=NG)
        ubuf = self.p16.alloc(GD * Tc).rearrange("p (d t) -> p d t", d=GD)
        sbig = self.p16.alloc(GD * Tc).rearrange("p (d t) -> p d t", d=GD)
        scb = self.p16.alloc(2 * GD * Tc).rearrange("p (r d t) -> p r d t", r=2, d=GD)
        zsl = [self.p16.alloc(Tc) for _ in range(2)]
        ysl = [self.p16.alloc(Tc) for _ in range(2)]

        dma(self.row_d[0:1, :], self.dtr_row, wait=[(eV, None)])
        dma(dtrep, self.row_d[0:1, :].broadcast_to([128, T]), wait=[(qst, None)])
        dt_bcast = None

        bc_use = [0, 0]
        bc_useG = [0, 0]
        dab_use = [0, 0]
        scb_use = [0, 0]
        it = 0
        y_ev = 0
        ysl_dma = [0, 0]
        for tc in range(c.NTC):
            tcs = tc * Tc
            for dh in range(NDH):
                dch0 = dh * DPH
                for dg in range(NDG):
                    xs_t = []
                    for i in range(GD):
                        tx = dma(xssl[:, i, :],
                                 self.xs_d[b, dch0 + dg * GD + i, :, tcs:tcs + Tc],
                                 wait=[(eV, None), (qst, None)])
                        xs_t.append(tx)
                    for i in range(GD):
                        dch = dch0 + dg * GD + i
                        es(lambda e, i=i, dch=dch, tcs=tcs: e.activation(
                            spt, dtrep[:, tcs:tcs + Tc], AF.Abs,
                            scale=self.dtw[:, dch:dch + 1],
                            bias=self.dtb[:, dch:dch + 1]),
                           wait=[(qio, None), (eV, None)], drain=True, inc=(eS, 1))
                        es(lambda e: e.activation(spt, spt, AF.Exp, scale=-1.0),
                           drain=True, inc=(eS, 1))
                        es(lambda e: e.activation(spt, spt, AF.Ln, bias=1.0),
                           drain=True, inc=(eS, 1))
                        es(lambda e, i=i, dch=dch, tcs=tcs: e.activation(
                            dtt[:, i, :], dtrep[:, tcs:tcs + Tc], AF.Relu,
                            scale=self.dtw[:, dch:dch + 1],
                            bias=self.dtb[:, dch:dch + 1]),
                           drain=True, inc=(eS, 1))
                        ev(lambda e, i=i: e.tensor_add(dtt[:, i, :], dtt[:, i, :],
                                                       spt),
                           wait=[(eS, None)], drain=True, inc=(eV, 1))
                        eg(lambda e, i=i: e.tensor_mul(dtx[:, i, :], dtt[:, i, :],
                                                       xssl[:, i, :]),
                           wait=[(qio, xs_t[i]), (eV, None)], drain=True, inc=(eG, 1))
                    dt_built = eV.n
                    jds = list(range(GD))
                    for ng in range(NNG):
                        gpr = ng % 2
                        n0 = ng * NG
                        tB = dma(brep[:, gpr],
                                 self.bmat_d[b, n0:n0 + NG, tcs:tcs + Tc]
                                 .unsqueeze(0).broadcast_to([128, NG, Tc]),
                                 wait=[(eV, bc_use[gpr]), (eG, bc_useG[gpr]),
                                       (qst, None)])
                        tC = dma(crep[:, gpr],
                                 self.cmat_d[b, n0:n0 + NG, tcs:tcs + Tc]
                                 .unsqueeze(0).broadcast_to([128, NG, Tc]))
                        for ni in range(NG):
                            n = n0 + ni
                            pr = it % 2
                            for j in jds:
                                dch = dch0 + dg * GD + j
                                es(lambda e, pr=pr, j=j, dch=dch, n=n: e.activation(
                                    self.dab(pr, j), dtt[:, j, :], AF.Exp,
                                    scale=self.aneg[:, dch, n:n + 1]),
                                   wait=[(eV, dab_use[pr]), (eV, dt_built)],
                                   inc=(eS, 1) if j == jds[-1] else None)
                            ev(lambda e, gpr=gpr, ni=ni: e.tensor_mul(
                                ubuf[:, :, :],
                                dtx[:, :, :],
                                brep[:, gpr, ni, :].unsqueeze(1)
                                .broadcast_to([128, GD, Tc])),
                               wait=[(qio, tB), (eS, None), (eG, None)],
                               inc=(eV, 1))
                            for j in jds:
                                dch = dch0 + dg * GD + j
                                jj = j
                                if tc == 0:
                                    init = 0.0
                                else:
                                    init = ST[:, dch * NS + n:dch * NS + n + 1]
                                ev(lambda e, jj=jj, pr=pr, init=init: e.tensor_tensor_scan(
                                    sbig[:, jj, :], self.dab(pr, jj), ubuf[:, jj, :],
                                    init, ALU.mult, ALU.add),
                                   wait=[(eG, None)], drain=(jj == 0),
                                   inc=(eV, 1) if jj == GD - 1 else None)
                            dab_use[pr] = eV.n
                            ev(lambda e, pr=pr, gpr=gpr, ni=ni: e.tensor_mul(
                                scb[:, pr, :, :],
                                sbig[:, :, :],
                                crep[:, gpr, ni, :].unsqueeze(1)
                                .broadcast_to([128, GD, Tc])),
                               wait=[(qio, tC), (eT, scb_use[pr])], drain=True,
                               inc=(eV, 1))
                            if tc < c.NTC - 1:
                                eg(lambda e, dch0=dch0, dg=dg, n=n: e.tensor_copy(
                                    ST.rearrange("p (d n) -> p d n", n=NS)
                                    [:, dch0 + dg * GD:dch0 + (dg + 1) * GD, n:n + 1],
                                    sbig[:, :, Tc - 1:Tc]),
                                   wait=[(eV, None)], inc=(eG, 1))
                            sC_done = eV.n
                            first = (ng == 0 and ni == 0)
                            last = (ng == NNG - 1 and ni == NG - 1)
                            HBS = c.HBS
                            ybank = max(Tc, 512)
                            for j in range(GD):
                                for h in range(HB):
                                    ep(lambda e, j=j, h=h, pr=pr, first=first,
                                       last=last, HBS=HBS, ybank=ybank: e.matmul(
                                           ps[:, j * ybank + h * HBS:
                                              j * ybank + (h + 1) * HBS],
                                           self.idb[:],
                                           scb[:, pr, j, h * HBS:(h + 1) * HBS],
                                           start=first, stop=last),
                                       wait=[(eV, sC_done), (eV, y_ev)],
                                       inc=(eT, 1) if (j == GD - 1 and h == HB - 1)
                                       else None)
                            scb_use[pr] = eT.n
                            it += 1
                        bc_use[gpr] = eV.n
                        bc_useG[gpr] = eG.n
                    for j in range(GD):
                        dch = dch0 + dg * GD + j
                        zpr = j % 2
                        tz = dma(zsl[zpr], self.z_d[b, dch, :, tcs:tcs + Tc],
                                 wait=[(eS, None), (qst, None)])
                        ev(lambda e, j=j, dch=dch: e.scalar_tensor_tensor(
                            yfull, xssl[:, j, :],
                            self.dvec[:, dch:dch + 1],
                            ps[:, j * max(Tc, 512):j * max(Tc, 512) + Tc],
                            ALU.mult, ALU.add),
                           wait=[(eT, None)], drain=True, inc=(eV, 1))
                        es(lambda e, zpr=zpr: e.activation(gz, zsl[zpr], AF.Silu),
                           wait=[(qio, tz), (eV, None)], drain=True, inc=(eS, 1))
                        ev(lambda e, zpr=zpr: e.tensor_mul(ysl[zpr], yfull, gz),
                           wait=[(eS, None), (qst, None)], drain=True,
                           inc=(eV, 1))
                        ysl_dma[zpr] = dma(self.y_d[b, dch, :, tcs:tcs + Tc],
                                           ysl[zpr], wait=[(eV, None)])
                    y_ev = eV.n
        self.y_tgt = self.qio.n
        self.p32.pop()
        self.p16.pop()
        self.pf16.pop()

    def warm(self, n=24, wait=()):
        ps = self.psum
        for i in range(n):
            self.ep(lambda e, i=i, n=n: e.matmul(
                ps[:, 3968:4096], self.idb[:], self.idb[:],
                start=(i == 0), stop=(i == n - 1)),
                wait=([(self.eT, None)] + list(wait)) if i == 0 else (),
                inc=(self.eT, 1))

    def dab(self, pr, j):
        Tc, GD = self.c.Tc, self.c.GD
        off = (pr * GD + j) * Tc
        return self.pf16.t[:, off:off + Tc]

    def phase34(self, b):
        c = self.c
        KD, KI, KH, T, D, DH, Tc = c.KD, c.KI, c.KH, c.T, c.D, c.DH, c.Tc
        NFC, NTT, TT_C = c.NFC, c.NTT, c.TT_C
        qio, qst, qout = self.qio, self.qst, self.qout
        eV, eS, eT, eG = self.eV, self.eS, self.eT, self.eG
        dma, ev, es, ep, eg = self.dma, self.ev, self.es, self.ep, self.eg
        ps = self.psum
        smv = self.smv

        self.p32.push()
        self.p16.push()
        self.pf16.push()
        f16q = [self.pf16.alloc(D) for _ in range(2)]
        f16h = [self.pf16.alloc(T) for _ in range(2)]
        x1qT = self.p16.alloc(KD * T).rearrange("p (k t) -> p k t", k=KD)
        isrep = self.p16.alloc(T)
        r1b = [self.p32.alloc(D) for _ in range(2)]
        sqb = [self.p32.alloc(D) for _ in range(2)]
        xtk = [self.p32.alloc(D) for _ in range(2)]
        TG = min(8, TT_C)
        self.p32.push()
        r1big = self.p32.alloc(TG * D).rearrange("p (a t) -> p a t", a=TG)
        self.p16.push()
        Yb = self.p16.alloc(KI * Tc).rearrange("p (k t) -> p k t", k=KI)
        attn = self.p16.alloc(KD * Tc).rearrange("p (k t) -> p k t", k=KD)
        xqb = [self.p16.alloc(D) for _ in range(2)]
        wop = [self.p16.alloc(KI * 128) for _ in range(2)]

        def layer_norm(tile, pr, out, wait_extra=(), wait_out=()):
            ev(lambda e: e.reduce_sum(smv[:, 0:1], tile, axis=AX.X),
               wait=list(wait_extra), drain=True, inc=(eV, 1))
            es(lambda e, pr=pr: e.activation(sqb[pr], tile, AF.Square),
               wait=[(eV, None)], inc=(eS, 1))
            ev(lambda e, pr=pr: e.reduce_sum(smv[:, 1:2], sqb[pr], axis=AX.X),
               wait=[(eS, None)], drain=True, inc=(eV, 1))
            ev(lambda e: e.tensor_scalar(smv[:, 0:2], smv[:, 0:2], 1.0 / D, None,
                                         ALU.mult), drain=True, inc=(eV, 1))
            ev(lambda e: e.tensor_mul(smv[:, 2:3], smv[:, 0:1], smv[:, 0:1]),
               drain=True, inc=(eV, 1))
            ev(lambda e: e.tensor_sub(smv[:, 1:2], smv[:, 1:2], smv[:, 2:3]),
               drain=True, inc=(eV, 1))
            ev(lambda e: e.tensor_scalar(smv[:, 1:2], smv[:, 1:2], 1e-5, None,
                                         ALU.add), drain=True, inc=(eV, 1))
            ev(lambda e: e.reciprocal(smv[:, 2:3], smv[:, 1:2]), drain=True,
               inc=(eV, 1))
            es(lambda e: e.activation(smv[:, 2:3], smv[:, 2:3], AF.Sqrt),
               wait=[(eV, None)], drain=True, inc=(eS, 1))
            ev(lambda e: e.tensor_scalar(out, tile, smv[:, 0:1], smv[:, 2:3],
                                         ALU.subtract, ALU.mult),
               wait=[(eS, None)] + list(wait_out), drain=True, inc=(eV, 1))

        self.warm()
        wop_use = [0, 0]
        op_ev = [0, 0]
        ptok_use = [0, 0]
        xtk_use = [0, 0]
        pxq_ev = 0
        for tc in range(c.NTC):
            tcs = tc * Tc
            yt = 0
            for kc in range(KI):
                yt = dma(Yb[:, kc, :], self.y_d[b, kc, :, tcs:tcs + Tc],
                         wait=[(eT, None), (qst, None)])
            for m in range(KD):
                pr = m % 2
                tW = dma(wop[pr], self.opw_d[m], wait=[(eT, wop_use[pr])])
                for h in range(c.HB):
                    HBS = c.HBS
                    fpr = (m * c.HB + h) % 2
                    for kc in range(KI):
                        ep(lambda e, pr=pr, fpr=fpr, kc=kc, h=h, HBS=HBS: e.matmul(
                            ps[:, fpr * 512:fpr * 512 + HBS],
                            wop[pr][:, kc * 128:(kc + 1) * 128],
                            Yb[:, kc, h * HBS:(h + 1) * HBS],
                            start=(kc == 0), stop=(kc == KI - 1)),
                            wait=[(qio, max(tW, yt)), (eS, op_ev[fpr])],
                            inc=(eT, 1) if kc == KI - 1 else None)
                    es(lambda e, m=m, h=h, fpr=fpr, HBS=HBS: e.copy(
                        attn[:, m, h * HBS:(h + 1) * HBS],
                        ps[:, fpr * 512:fpr * 512 + HBS]),
                       wait=[(eT, None)], inc=(eS, 1))
                    op_ev[fpr] = eS.n
                wop_use[pr] = eT.n
            attn_done = eS.n
            TT = min(8, TT_C)
            for tg in range(0, TT_C, TT):
              for j in range(TT):
                tt = tg + j
                pr = tt % 2
                pt0 = 1024 + pr * 1024
                for m in range(KD):
                    ep(lambda e, m=m, tt=tt, pt0=pt0: e.transpose(
                        self.psum_b[:, 2 * pt0 + m * 128:2 * pt0 + (m + 1) * 128],
                        attn[:, m, tt * 128:(tt + 1) * 128], self.idb[:]),
                        wait=[(eS, attn_done), (eV, ptok_use[pr])],
                        inc=(eT, 1) if m == KD - 1 else None)
                tX = dma(xtk[pr], self.x_h[b, tcs + tt * 128:tcs + (tt + 1) * 128, :],
                         wait=[(eV, xtk_use[pr])])
                ev(lambda e, pr=pr, pt0=pt0, j=j: e.tensor_add(
                    r1big[:, j, :], self.psum_b[:, 2 * pt0:2 * pt0 + D], xtk[pr]),
                   wait=[(eT, None), (qio, tX), (qst, None)], drain=True,
                   inc=(eV, 1))
                ptok_use[pr] = eV.n
                xtk_use[pr] = eV.n
              smb = self.smb
              gs = tc * TT_C + tg
              ev(lambda e: e.reduce_sum(smb[:, 0:TT], r1big[:, :, :], axis=AX.X),
                 drain=True, inc=(eV, 1))
              for j in range(TT):
                  qr = j % 2
                  es(lambda e, qr=qr, j=j: e.activation(sqb[qr], r1big[:, j, :],
                                                        AF.Square),
                     wait=[(eV, None)], inc=(eS, 1))
                  ev(lambda e, qr=qr, j=j: e.reduce_sum(
                      smb[:, TT + j:TT + j + 1], sqb[qr], axis=AX.X),
                     wait=[(eS, None)], drain=True, inc=(eV, 1))
              ev(lambda e: e.tensor_scalar(smb[:, 0:2 * TT], smb[:, 0:2 * TT],
                                           1.0 / D, None, ALU.mult),
                 drain=True, inc=(eV, 1))
              ev(lambda e: e.tensor_mul(smb[:, 2 * TT:3 * TT], smb[:, 0:TT],
                                        smb[:, 0:TT]), drain=True, inc=(eV, 1))
              ev(lambda e: e.tensor_sub(smb[:, TT:2 * TT], smb[:, TT:2 * TT],
                                        smb[:, 2 * TT:3 * TT]), drain=True,
                 inc=(eV, 1))
              ev(lambda e: e.tensor_scalar(smb[:, TT:2 * TT], smb[:, TT:2 * TT],
                                           1e-5, None, ALU.add), drain=True,
                 inc=(eV, 1))
              ev(lambda e: e.reciprocal(smb[:, 2 * TT:3 * TT], smb[:, TT:2 * TT]),
                 drain=True, inc=(eV, 1))
              es(lambda e: e.activation(smb[:, 2 * TT:3 * TT],
                                        smb[:, 2 * TT:3 * TT], AF.Sqrt),
                 wait=[(eV, None)], drain=True, inc=(eS, 1))
              for j in range(TT):
                  tt = tg + j
                  ev(lambda e, j=j: e.tensor_scalar(
                      r1big[:, j, :], r1big[:, j, :], smb[:, j:j + 1],
                      smb[:, 2 * TT + j:2 * TT + j + 1], ALU.subtract, ALU.mult),
                     wait=[(eS, None)], drain=True, inc=(eV, 1))
                  dma(self.x1_d[b, tcs + tt * 128:tcs + (tt + 1) * 128, :],
                      r1big[:, j, :], wait=[(eV, None)])
              ev(lambda e: e.reduce_max(smb[:, 3 * TT:4 * TT], r1big[:, :, :],
                                        axis=AX.X, apply_absolute_value=True),
                 drain=True, inc=(eV, 1))
              ev(lambda e: e.tensor_scalar(smb[:, 3 * TT:4 * TT],
                                           smb[:, 3 * TT:4 * TT], 1e-5, None,
                                           ALU.max), drain=True, inc=(eV, 1))
              ev(lambda e: e.reciprocal(smb[:, 4 * TT:5 * TT],
                                        smb[:, 3 * TT:4 * TT]),
                 drain=True, inc=(eV, 1))
              ev(lambda e: e.tensor_scalar(smb[:, 4 * TT:5 * TT],
                                           smb[:, 4 * TT:5 * TT], 127.0, None,
                                           ALU.mult), drain=True, inc=(eV, 1))
              ev(lambda e, gs=gs, TT=TT: e.tensor_mul(
                  self.itok[:, gs:gs + TT], smb[:, 3 * TT:4 * TT],
                  self.iw1[:].broadcast_to([128, TT])), drain=True, inc=(eV, 1))
              for j in range(TT):
                  pr = j % 2
                  gtt = gs + j
                  ev(lambda e, pr=pr, j=j: e.tensor_scalar(
                      f16q[pr], r1big[:, j, :],
                      smb[:, 4 * TT + j:4 * TT + j + 1],
                      None, ALU.mult), drain=True, inc=(eV, 1))
                  ev(lambda e, pr=pr: e.tensor_scalar(f16q[pr], f16q[pr],
                                                      MAGIC16, None, ALU.add),
                     drain=True, inc=(eV, 1))
                  ev(lambda e, pr=pr: e.tensor_scalar(xqb[pr], f16q[pr],
                                                      MAGIC16, None, ALU.subtract),
                     drain=True, inc=(eV, 1))
                  for kc in range(KD):
                      ep(lambda e, pr=pr, kc=kc: e.transpose(
                          self.psum_b[:, 6144 + kc * 128:6144 + (kc + 1) * 128],
                          xqb[pr][:, kc * 128:(kc + 1) * 128], self.idb[:]),
                          wait=[(eV, None), (eS, pxq_ev)],
                          inc=(eT, 1) if kc == KD - 1 else None)
                  es(lambda e, gtt=gtt: e.copy(
                      x1qT[:, :, gtt * 128:(gtt + 1) * 128],
                      self.psum_b[:, 6144:6144 + D]
                      .rearrange("p (k t) -> p k t", k=KD)),
                     wait=[(eT, None)], inc=(eS, 1))
                  pxq_ev = eS.n
        dma(self.itok_d, self.itok[:, 0:NTT], wait=[(eV, None)])
        dma(self.is_row[0:1, :],
            self.itok_d.transpose([1, 0]).unsqueeze(0),
            wait=[(qst, None)], slow=True)
        ev(lambda e: e.tensor_copy(self.is_rowb, self.is_row),
           wait=[(qio, None)], drain=True, inc=(eV, 1))
        dma(self.rowb_d[0:1, :], self.is_rowb, wait=[(eV, None)])
        dma(isrep, self.rowb_d[0:1, :].broadcast_to([128, T]), wait=[(qst, None)])
        is_done = None

        self.p32.pop()
        self.p16.pop()
        hst = [self.p16.alloc(T) for _ in range(2)]
        w1t = [self.p16.alloc(KD * 128) for _ in range(2)]
        h2r = [self.p16.alloc(512) for _ in range(2)]
        sc2rep = self.p16.alloc(T)
        iv2rep = self.p16.alloc(T)
        w2t = [self.p16.alloc(KD * 128) for _ in range(2)]
        h2s = [self.p16.alloc(512) for _ in range(2)]
        outb = [self.p32.alloc(D) for _ in range(2)]
        self.p32.push()
        hpad = [self.p16.alloc(T + 2) for _ in range(2)]
        dgb = [self.p16.alloc(3 * 128) for _ in range(2)]

        gtok = self.p32.alloc(c.NTT)
        tmax = self.p32.alloc(c.NTT)
        ev(lambda e: e.memset(gtok, 0.0), inc=(eV, 1))
        x1qT_done = pxq_ev
        self.warm(wait=[(self.eS, x1qT_done)])
        gm_ev = [0, 0]
        w1_use = [0, 0]
        h_ev = [0, 0]
        hc_use = [0, 0]
        hs_use = [0, 0]
        conv_free = 0
        dg_use = [0, 0]
        for mh in range(KH):
            pr = mh % 2
            tW = dma(w1t[pr], self.w1t_d[mh], wait=[(eT, w1_use[pr])])
            ev(lambda e, pr=pr: e.memset(hpad[pr][:, 0:1], 0.0),
               wait=[(eV, hc_use[pr])], inc=(eV, 1))
            ev(lambda e, pr=pr: e.memset(hpad[pr][:, T + 1:T + 2], 0.0), inc=(eV, 1))
            for k in range(3):
                ev(lambda e, pr=pr, mh=mh, k=k: e.tensor_mul(
                    dgb[pr][:, k * 128:(k + 1) * 128], self.idb[:],
                    self.fcw[:, mh, k:k + 1].broadcast_to([128, 128])),
                   wait=[(eT, dg_use[pr])] if k == 0 else (), inc=(eV, 1))
            pad_ev = eV.n
            for fc in range(NFC):
                fpr = fc % 2
                for kc in range(KD):
                    ep(lambda e, pr=pr, fpr=fpr, kc=kc, fc=fc: e.matmul(
                        ps[:, fpr * 512:(fpr + 1) * 512],
                        w1t[pr][:, kc * 128:(kc + 1) * 128],
                        x1qT[:, kc, fc * 512:(fc + 1) * 512],
                        start=(kc == 0), stop=(kc == KD - 1)),
                        wait=[(qio, tW), (eV, h_ev[fpr]), (eS, x1qT_done)],
                        inc=(eT, 1) if kc == KD - 1 else None)
                ev(lambda e, pr=pr, fc=fc, fpr=fpr: e.tensor_mul(
                    hpad[pr][:, 1 + fc * 512:1 + (fc + 1) * 512],
                    ps[:, fpr * 512:(fpr + 1) * 512],
                    isrep[:, fc * 512:(fc + 1) * 512]),
                   wait=[(eT, None), (qio, None), (eV, pad_ev)], drain=True,
                   inc=(eV, 1))
                h_ev[fpr] = eV.n
            w1_use[pr] = eT.n
            h_done = eV.n
            for c4 in range(T // 512):
                for k in range(3):
                    ep(lambda e, pr=pr, k=k, c4=c4: e.matmul(
                        ps[:, 1024 + c4 * 512:1536 + c4 * 512],
                        dgb[pr][:, k * 128:(k + 1) * 128],
                        hpad[pr][:, k + c4 * 512:k + c4 * 512 + 512],
                        start=(k == 0), stop=(k == 2)),
                        wait=[(eV, h_done), (eS, conv_free)],
                        inc=(eT, 1) if (k == 2 and c4 == T // 512 - 1) else None)
            dg_use[pr] = eT.n
            es(lambda e, pr=pr, mh=mh: e.activation(
                hst[pr], ps[:, 1024:1024 + T], AF.Silu,
                bias=self.fcb[:, mh:mh + 1], scale=1.0),
               wait=[(eT, None), (qst, None)], drain=True, inc=(eS, 1))
            conv_free = eS.n
            hc_use[pr] = eS.n
            dma(self.hs_d[b, mh], hst[pr], wait=[(eS, None)])
            hs_use[pr] = self.qio.n
            ab0 = 6144
            for tt in range(c.NTT):
                ep(lambda e, pr=pr, tt=tt, ab0=ab0: e.transpose(
                    self.psum_b[:, ab0 + tt * 128:ab0 + (tt + 1) * 128],
                    hst[pr][:, tt * 128:(tt + 1) * 128], self.idb[:]),
                    wait=[(eS, None), (eV, gm_ev[0])],
                    inc=(eT, 1) if tt == c.NTT - 1 else None)
            ev(lambda e, ab0=ab0: e.reduce_max(
                tmax, self.psum_b[:, ab0:ab0 + c.NTT * 128]
                .rearrange("p (a b) -> p a b", a=c.NTT), axis=AX.X,
                apply_absolute_value=True),
               wait=[(eT, None)], drain=True, inc=(eV, 1))
            ev(lambda e: e.tensor_max(gtok, gtok, tmax), drain=True, inc=(eV, 1))
            gm_ev[0] = eV.n
        self.p32.pop()
        self.p32.push()
        sc2f = self.p32.alloc(T)[0:1, :]
        dma(self.gtok_d, gtok, wait=[(eV, None)])
        dma(self.gm_row, self.gtok_d.transpose([1, 0]).unsqueeze(0),
            wait=[(qst, None)], slow=True)
        ev(lambda e: e.tensor_scalar(self.gm_row, self.gm_row, 1e-5, None,
                                     ALU.max), wait=[(qio, None)], drain=True,
           inc=(eV, 1))
        ev(lambda e: e.reciprocal(sc2f, self.gm_row), drain=True,
           inc=(eV, 1))
        ev(lambda e: e.tensor_scalar(sc2f, sc2f, 127.0, None,
                                     ALU.mult), drain=True, inc=(eV, 1))
        ev(lambda e: e.tensor_scalar(self.iv2_rowb, self.gm_row,
                                     self.iw2[0:1, 0:1], None, ALU.mult),
           drain=True, inc=(eV, 1))
        ev(lambda e: e.tensor_copy(self.sc2_rowb, sc2f),
           drain=True, inc=(eV, 1))
        dma(self.rowb_d[1:2, :], self.sc2_rowb, wait=[(eV, None)])
        dma(self.rowb_d[2:3, :], self.iv2_rowb)
        dma(sc2rep, self.rowb_d[1:2, :].broadcast_to([128, T]), wait=[(qst, None)])
        dma(iv2rep, self.rowb_d[2:3, :].broadcast_to([128, T]))
        self.p32.pop()
        self.p32.push()
        ffn_f = self.p32.alloc(KD * 512).rearrange("p (m t) -> p m t", m=KD)
        qv = [self.p32.alloc(512) for _ in range(2)]
        ffn_ev = 0
        fc_clear = 0
        pt_use = [0, 0]
        w2_use = [0, 0]
        self.warm(wait=[(self.eV, None)])
        for fc in range(NFC):
            for kh in range(KH):
                pr = kh % 2
                th = dma(h2r[pr], self.hs_d[b, kh, :, fc * 512:(fc + 1) * 512],
                         wait=[(eT, w2_use[pr]), (qst, None)])
                tW = dma(w2t[pr], self.w2t_d[kh])
                ev(lambda e, pr=pr, fc=fc: e.tensor_mul(
                    qv[pr], h2r[pr], sc2rep[:, fc * 512:(fc + 1) * 512]),
                   wait=[(qio, th), (eT, w2_use[pr])], drain=True, inc=(eV, 1))
                ev(lambda e, pr=pr: e.tensor_scalar(
                    f16h[pr][:, 0:512], qv[pr], MAGIC16, None, ALU.add),
                   drain=True, inc=(eV, 1))
                ev(lambda e, pr=pr: e.tensor_scalar(
                    h2s[pr], f16h[pr][:, 0:512], MAGIC16, None, ALU.subtract),
                   drain=True, inc=(eV, 1))
                hq_done = eV.n
                for m in range(KD):
                    wl = [(qio, tW), (eV, hq_done), (eV, fc_clear)]
                    if fc == 0 and kh == 0 and m == KD - 1:
                        wl.append((eT, None))
                    ep(lambda e, pr=pr, m=m, kh=kh: e.matmul(
                        ps[:, m * 512:(m + 1) * 512],
                        w2t[pr][:, m * 128:(m + 1) * 128], h2s[pr],
                        start=(kh == 0), stop=(kh == KH - 1)),
                        wait=wl,
                        inc=(eT, 1) if m == KD - 1 else None)
                w2_use[pr] = eT.n
            for m in range(KD):
                ev(lambda e, m=m, fc=fc: e.tensor_mul(
                    ffn_f[:, m, :], ps[:, m * 512:(m + 1) * 512],
                    iv2rep[:, fc * 512:(fc + 1) * 512]),
                   wait=[(eT, None)], drain=True, inc=(eV, 1))
            ffn_ev = eV.n
            for tt in range(4):
                pr = tt % 2
                rows = fc * 512 + tt * 128
                for m in range(KD):
                    ep(lambda e, m=m, tt=tt, pr=pr: e.transpose(
                        ps[:, pr * 1024 + m * 128:pr * 1024 + (m + 1) * 128],
                        ffn_f[:, m, tt * 128:(tt + 1) * 128], self.idf[:]),
                        wait=[(eV, ffn_ev), (eV, pt_use[pr])],
                        inc=(eT, 1) if m == KD - 1 else None)
                tX = dma(xtk[pr], self.x1_d[b, rows:rows + 128, :],
                         wait=[(eV, xtk_use[pr]), (qst, None)])
                ev(lambda e, pr=pr: e.tensor_add(r1b[pr], ps[:, pr * 1024:pr * 1024 + D],
                                                 xtk[pr]),
                   wait=[(eT, None), (qio, tX)], drain=True, inc=(eV, 1))
                pt_use[pr] = eV.n
                xtk_use[pr] = eV.n
                layer_norm(r1b[pr], pr, outb[pr], wait_out=[(qout, None)])
                dma(self.out_h[b, rows:rows + 128, :], outb[pr],
                    wait=[(eV, None)], sem=qout)
            fc_clear = eV.n
        self.p32.pop()
        self.p32.pop()
        self.p16.pop()
        self.pf16.pop()


def build_program(cfg, dbg=False):
    return Builder(cfg).build(dbg=dbg)


def kernel(**inputs):
    import ml_dtypes
    cfg = Cfg(small=False)
    x = np.ascontiguousarray(np.asarray(inputs["L0_attn_x"], dtype=np.float32))
    n_cores = 8
    shards = np.split(x, n_cores, axis=0)
    base = {k: np.ascontiguousarray(np.asarray(v, dtype=np.float32))
            for k, v in inputs.items() if k != "L0_attn_x"}
    ident_f = np.eye(128, dtype=np.float32)
    ident_b = np.eye(128, dtype=ml_dtypes.bfloat16)
    nc = build_program(cfg)
    in_maps = []
    for i in range(n_cores):
        m = dict(base)
        m["L0_attn_x"] = shards[i]
        m["ident_f32"] = ident_f
        m["ident_bf16"] = ident_b
        in_maps.append(m)
    res = run_bass_kernel_spmd(nc, in_maps, core_ids=list(range(n_cores)))
    return np.concatenate([r["out"] for r in res.results], axis=0)
